# version 14
# baseline (speedup 1.0000x reference)
import math
import numpy as np
import concourse.bass as bass
import concourse.mybir as mybir
from concourse.bass_utils import run_bass_kernel_spmd

F32 = mybir.dt.float32
BF16 = mybir.dt.bfloat16
I32 = mybir.dt.int32
AF = mybir.ActivationFunctionType
ALU = mybir.AluOpType
AX = mybir.AxisListType

T = 4096
D = 1024
H = 16
DH = 64
MF = 256
M2 = 512
I = 2816
KC = D // 128
ICN = I // 128
TT = 512
NT = T // TT
TB = 128
NTB = TT // TB

PI = math.pi
INV127 = 1.0 / 127.0
LN_THR = math.log(1e-6 * math.sqrt(MF))
MLN16 = -math.log(math.sqrt(MF))

(C_ONE, C_HALFPI, C_MHALF, C_EPS, C_LNTHR, C_PHALF, C_ZERO, C_MONE,
 C_MLN16, C_SPARE, C_PI, C_MPI, C_MHPI) = range(13)

ENGS = ("sync", "tensor", "vector", "scalar", "gpsimd")


class Plan:
    def __init__(self, nc):
        self.nc = nc
        self.ops = {e: [] for e in ENGS}
        self.sems = {}
        self.cnt = {}
        self.mark = {e: {} for e in ENGS}
        self.g = {}

    def sem(self, name):
        s = self.nc.alloc_semaphore(name)
        self.sems[name] = s
        self.cnt[name] = 0

    def op(self, eng, f, sem=None, by=1, waits=()):
        if eng == "vector":
            waits = list(waits) + [("VE", self.cnt["VE"])]
        elif eng == "scalar":
            waits = list(waits) + [("SE", self.cnt["SE"])]
        real = []
        wm = self.mark[eng]
        for sn, val in waits:
            if val > 0 and wm.get(sn, 0) < val:
                wm[sn] = val
                real.append((self.sems[sn], val))
        semh = None
        if sem is not None:
            self.cnt[sem] += by
            semh = self.sems[sem]
        def fn(e, _w=real, _f=f, _s=semh, _b=by):
            for sh, v in _w:
                e.wait_ge(sh, v)
            ins = _f(e)
            if _s is not None:
                ins.then_inc(_s, _b)
        self.ops[eng].append(fn)

    def wait(self, eng, waits):
        wm = self.mark[eng]
        for sn, val in waits:
            if val > 0 and wm.get(sn, 0) < val:
                wm[sn] = val
                s = self.sems[sn]
                self.ops[eng].append(lambda e, _s=s, _v=val: e.wait_ge(_s, _v))

    def c(self, name):
        return self.cnt[name]

    def barrier_eng(self, eng):
        self.wait(eng, [(s, self.cnt[s]) for s in self.cnt
                        if s != "GE"])


def build_core_program(stop=None, debug=False, amask="nvpkt"):
    nc = bass.Bass("TRN2", target_bir_lowering=False)

    x_t = nc.dram_tensor("x_t", [D, T], F32, kind="ExternalInput").ap()
    wq_t = nc.dram_tensor("wq_t", [D, D], F32, kind="ExternalInput").ap()
    wk_t = nc.dram_tensor("wk_t", [D, D], F32, kind="ExternalInput").ap()
    wv_t = nc.dram_tensor("wv_t", [D, D], F32, kind="ExternalInput").ap()
    wo_t = nc.dram_tensor("wo_t", [D, D], F32, kind="ExternalInput").ap()
    rf_rep_i = nc.dram_tensor("rf_rep", [128, MF], F32, kind="ExternalInput").ap()
    rf_bd_i = nc.dram_tensor("rf_bd", [128, M2], F32, kind="ExternalInput").ap()
    wg_n = nc.dram_tensor("wg_n", [I, D], F32, kind="ExternalInput").ap()
    wu_n = nc.dram_tensor("wu_n", [I, D], F32, kind="ExternalInput").ap()
    wd_n = nc.dram_tensor("wd_n", [D, I], F32, kind="ExternalInput").ap()
    wg_tt = nc.dram_tensor("wg_tt", [D, I], F32, kind="ExternalInput").ap()
    wu_tt = nc.dram_tensor("wu_tt", [D, I], F32, kind="ExternalInput").ap()
    wd_tt = nc.dram_tensor("wd_tt", [I, D], F32, kind="ExternalInput").ap()
    clip2 = nc.dram_tensor("clip2", [1, 2], F32, kind="ExternalInput").ap()
    consts_i = nc.dram_tensor("consts", [128, 13], F32, kind="ExternalInput").ap()
    onesrow_i = nc.dram_tensor("onesrow", [1, 128], F32, kind="ExternalInput").ap()
    ident_i = nc.dram_tensor("ident", [128, 128], F32, kind="ExternalInput").ap()
    maskh_i = nc.dram_tensor("maskh", [128, KC * H], F32, kind="ExternalInput").ap()
    sel_i = nc.dram_tensor("sel_all", [128, KC * 128], F32, kind="ExternalInput").ap()

    y_t = nc.dram_tensor("y_t", [D, T], F32, kind="ExternalOutput").ap()
    dbg = (nc.dram_tensor("dbg", [128, 16384], F32, kind="ExternalOutput").ap()
           if debug else None)

    xln_s = nc.dram_tensor("xln_s", [D, T], F32).ap()
    h_s = nc.dram_tensor("h_s", [I, T], F32).ap()
    wdq_s = nc.dram_tensor("wdq_s", [I, D], BF16).ap()
    cc_in1 = nc.dram_tensor("cc_in1", [1, 8], F32)
    cc_out1 = nc.dram_tensor("cc_out1", [1, 8], F32)
    cc_in2 = nc.dram_tensor("cc_in2", [1, 8], F32)
    cc_out2 = nc.dram_tensor("cc_out2", [1, 8], F32)

    A = lambda *a: nc.alloc_sbuf_tensor(*a).ap()
    consts = A("consts_sb", [128, 13], F32)
    onesrow = A("onesrow_sb", [1, 128], F32)
    ident = A("ident_sb", [128, 128], F32)
    rf_rep = A("rf_rep_sb", [128, MF], F32)
    rf_bd = A("rf_bd_sb", [128, M2], F32)
    maskh = A("maskh_sb", [128, KC * H], F32)
    sel_all = A("sel_sb", [128, KC * 128], F32)
    clip_sb = A("clip_sb", [1, 2], F32)
    sig2 = A("sig2_sb", [1, 2], F32)
    scale_sb = A("scaleK_sb", [128, H], F32)
    zst = A("zst_sb", [128, TT], F32)
    zall = A("zall_sb", [128, TT], F32)
    thr_fm = A("thr_sb", [H, TT], F32)
    rsrow = A("rsrow", [1, TT], F32)
    nurow = A("nurow", [1, TT], F32)
    sc512 = A("sc512", [128, TT], F32)
    xmaxst = A("xmaxst", [128, NT * KC], F32)
    hmaxst = A("hmaxst", [128, NT * ICN], F32)
    sall_g = A("sall_g", [128, ICN], F32)
    sall_u = A("sall_u", [128, ICN], F32)
    sall_d = A("sall_d", [128, KC], F32)
    ax_all = A("ax_all", [128, ICN], F32)
    bu_all = A("bu_all", [128, ICN], F32)
    reff_all = A("reff_all", [128, ICN], F32)
    cd_all = A("cd_all", [128, KC], F32)
    rx_col = A("rx_col", [128, 1], F32)
    sxs_col = A("sxs_col", [128, 1], F32)
    rh_col = A("rh_col", [128, 1], F32)
    shs_col = A("shs_col", [128, 1], F32)
    r1x1 = A("r1x1", [1, 1], F32)
    r1x1b = A("r1x1b", [1, 1], F32)
    gmax1 = A("gmax1", [1, 1], F32)
    gmax2 = A("gmax2", [1, 1], F32)
    ccst = A("ccst", [1, 8], F32)
    tp22 = A("tp22", [ICN, 128], F32)
    mred = A("mred", [128, ICN], F32)
    m1col = A("m1col", [128, 1], F32)

    PS = [nc.alloc_psum_tensor(f"ps{i}", [128, 512], F32).ap() for i in range(8)]

    BASE = 40960
    TOP = 229344

    def place(name, shape, dtype, off_kb):
        off = int(off_kb * 1024)
        nbytes = int(np.prod(shape[1:])) * mybir.dt.size(dtype)
        assert BASE + off + nbytes <= TOP, (name, off_kb, nbytes)
        return nc.alloc_sbuf_tensor_at(name, shape, dtype, offset=BASE + off).ap()

    wk_sb = place("wk_sb", [128, KC * D], F32, 0)
    wv_sb = place("wv_sb", [128, KC * D], F32, 32)
    kvacc = place("kvacc", [128, H * 512], F32, 64)
    xa0 = place("xa0", [128, KC * TT], F32, 96)
    ktile = place("ktile", [128, KC * TT], F32, 128)
    ksq = place("ksq", [128, KC * TT], F32, 144)
    kfps = [place("kfp0", [128, 2 * M2], F32, 144),
            place("kfp1", [128, 2 * M2], F32, 112),
            place("kfp2", [128, 2 * M2], F32, 116),
            place("kfp3", [128, 2 * M2], F32, 120)]
    pti = place("pti", [128, M2], I32, 148)
    actc = place("actc", [128, M2], F32, 148)
    ap0 = [place("ap0_0", [128, M2], F32, 150),
           place("ap0_1", [128, M2], F32, 152),
           place("ap0_2", [128, M2], F32, 124),
           place("ap0_3", [128, M2], F32, 126)]
    au = place("au", [128, M2], F32, 154)
    as1 = [place("as1_0", [128, M2], F32, 156),
           place("as1_1", [128, M2], F32, 177)]
    asgn = place("asgn", [128, M2], F32, 158)
    v65a = place("v65a", [128, NTB * H * 65], F32, 160)
    scale_a = place("scale_a", [128, NTB * H], F32, 176.5)
    kvfm = place("kvfm", [128, H * 4 * 65], F32, 96)

    wq_sb = place("wq_sb", [128, KC * D], F32, 0)
    wo_sb = place("wo_sb", [128, KC * D], F32, 32)
    xb0 = place("xb0", [128, KC * TT], F32, 64)
    qtiles = [place("qtileA", [128, KC * TT], F32, 113),
              place("qtileB", [128, KC * TT], F32, 80)]
    qsq = place("qsq", [128, KC * TT], F32, 129)
    attn = place("attn", [128, KC * TT], F32, 145)
    qf0 = place("qf0", [128, 4 * TT], F32, 161)
    qf1 = place("qf1", [128, 4 * TT], F32, 169)
    bp0 = [place(f"bp0_{m}", [128, M2], F32, 129 + 2 * m) for m in range(2)]
    bu = [place(f"bu_{m}", [128, M2], F32, 133 + 2 * m) for m in range(2)]
    bsgn = [place(f"bsgn_{m}", [128, M2], F32, 137 + 2 * m) for m in range(2)]
    bs1 = [place(f"bs1_{m}", [128, M2], F32, 141 + 2 * m) for m in range(2)]
    ptbi = place("ptbi", [128, 512], I32, 177)
    actcB = place("actcB", [128, 512], F32, 177)
    ba2 = [place(f"ba2_{m}", [128, M2], F32, 179 + 2 * m) for m in range(2)]

    wgq = place("wgq", [128, KC * I], BF16, 0)
    wuq = place("wuq", [128, KC * I], BF16, 44)
    wdq = place("wdq", [128, ICN * D], BF16, 88)
    rb = place("rb", [128, I], F32, 132)
    wtc0 = place("wtc0", [128, I], F32, 143)
    avq = place("avq", [128, I], F32, 154)
    sgnq = place("sgnq", [128, I], BF16, 165)
    t2q = place("t2q", [128, I], BF16, 171)
    natd2 = place("natd2", [128, I], F32, 165)
    natd3 = place("natd3", [128, I], F32, 132)

    xl0 = place("xl0", [128, KC * TT], F32, 88)
    xl1 = place("xl1", [128, KC * TT], F32, 104)
    xqb2 = [place("xqb0", [128, KC * TT], BF16, 120),
            place("xqb1", [128, KC * TT], BF16, 172)]
    hb = [place(f"hb{i}", [128, TT], F32, 128 + 2 * i) for i in range(4)]
    silu0 = place("silu0", [128, TT], F32, 136)
    silu1 = place("silu1", [128, TT], F32, 138)
    avf = place("avf", [128, KC * TT], F32, 140)
    sgnf = place("sgnf", [128, KC * TT], BF16, 156)
    t2f = place("t2f", [128, KC * TT], BF16, 164)

    wdq2 = place("wdq2", [128, ICN * D], BF16, 0)
    hq0 = place("hq0", [128, ICN * TT], BF16, 44)
    hq1 = place("hq1", [128, ICN * TT], BF16, 66)
    hl = [place(f"hl{i}", [128, TT], F32, 88 + 2 * i) for i in range(4)]
    xl20 = place("xl20", [128, KC * TT], F32, 96)
    xl21 = place("xl21", [128, KC * TT], F32, 112)
    yout = place("yout", [128, KC * TT], F32, 128)
    ysq2 = place("ysq2", [128, KC * TT], F32, 144)
    avf2 = [place(f"avf2_{i}", [128, TT], F32, 160 + 4 * i) for i in range(2)]
    sgnf2 = [place(f"sgnf2_{i}", [128, TT], BF16, 162 + 4 * i) for i in range(2)]
    t2f2 = [place(f"t2f2_{i}", [128, TT], BF16, 163 + 4 * i) for i in range(2)]

    p = Plan(nc)
    for s in ["LC", "LW", "LX0", "LX1", "LQ", "LH0", "LH1", "LH2", "LH3",
              "SX", "SH0", "SH1", "SH2", "SH3", "SW", "SC", "SO",
              "ZG", "TE", "VE", "SE", "GE", "GP"]:
        p.sem(s)
    G = p.g

    one = consts[:, C_ONE:C_ONE + 1]
    czero = consts[:, C_ZERO:C_ZERO + 1]

    def ld(dst, src, stream, waits=()):
        p.op("sync", lambda e, d=dst, s=src: e.dma_start(out=d, in_=s),
             sem=stream, by=16, waits=waits)

    def st(dst, src, stream, waits=()):
        p.op("gpsimd", lambda e, d=dst, s=src:
             e.dma_start(out=d, in_=s, allow_slow_non_contiguous=True),
             sem=stream, by=16, waits=waits)

    def ln_emit(ytile, ysqtile, outtile, outcb, psA, psB):
        p.op("scalar", lambda e, d=ysqtile, s=ytile:
             e.activation(d, s, AF.Square, bias=czero, scale=one),
             sem="SE", waits=[("VE", G["y_ve"])])
        ysq_se = p.c("SE")
        for kc in range(KC):
            p.op("tensor", lambda e, o=psA[0:1, :], l=one,
                 r=ytile[:, kc * TT:(kc + 1) * TT], s=(kc == 0), z=(kc == KC - 1):
                 e.matmul(o, l, r, start=s, stop=z),
                 sem="TE" if kc == KC - 1 else None,
                 waits=([("VE", G["y_ve"]), ("VE", G.get("ln_ps_free", 0)),
                         ("SE", G.get("ln_ps_free_se", 0))] if kc == 0 else []))
        for kc in range(KC):
            p.op("tensor", lambda e, o=psB[0:1, :], l=one,
                 r=ysqtile[:, kc * TT:(kc + 1) * TT], s=(kc == 0), z=(kc == KC - 1):
                 e.matmul(o, l, r, start=s, stop=z),
                 sem="TE" if kc == KC - 1 else None,
                 waits=([("SE", ysq_se)] if kc == 0 else []))
        te_stats = p.c("TE")
        p.op("vector", lambda e: e.tensor_scalar(zst[0:1, :], psA[0:1, :],
                                                 1.0 / D, None, ALU.mult),
             sem="VE", waits=[("TE", te_stats)])
        p.op("vector", lambda e: e.tensor_tensor(rsrow, zst[0:1, :], zst[0:1, :],
                                                 ALU.mult), sem="VE")
        p.op("vector", lambda e: e.scalar_tensor_tensor(
            rsrow, psB[0:1, :], 1.0 / D, rsrow, ALU.mult, ALU.subtract), sem="VE")
        ve_var = p.c("VE")
        p.op("scalar", lambda e: e.activation(nurow, rsrow, AF.Ln,
                                              bias=consts[0:1, C_EPS:C_EPS + 1],
                                              scale=one[0:1]),
             sem="SE", waits=[("VE", ve_var)])
        p.op("scalar", lambda e: e.activation(rsrow, nurow, AF.Exp,
                                              bias=czero[0:1],
                                              scale=consts[0:1, C_MHALF:C_MHALF + 1]),
             sem="SE")
        se_rs = p.c("SE")
        p.op("vector", lambda e: e.tensor_tensor(nurow, zst[0:1, :], rsrow, ALU.mult),
             sem="VE", waits=[("SE", se_rs)])
        ve_nu = p.c("VE")
        p.op("tensor", lambda e, o=psA: e.matmul(o, onesrow, rsrow,
                                                 start=True, stop=True),
             sem="TE", waits=[("VE", ve_nu)])
        p.op("tensor", lambda e, o=psB: e.matmul(o, onesrow, nurow,
                                                 start=True, stop=True), sem="TE")
        te_bc = p.c("TE")
        for oc in range(KC):
            p.op("vector", lambda e, d=sc512, a=ytile[:, oc * TT:(oc + 1) * TT], b=psA:
                 e.tensor_tensor(d, a, b, ALU.mult),
                 sem="VE", waits=[("TE", te_bc)])
            p.op("vector", lambda e, d=outtile[:, oc * TT:(oc + 1) * TT],
                 a=sc512, b=psB:
                 e.tensor_tensor(d, a, b, ALU.subtract), sem="VE")
            outcb(oc, p.c("VE"))
        G["ln_ps_free"] = p.c("VE")
        G["ln_ps_free_se"] = p.c("SE")

    for dst, src in [(consts, consts_i), (onesrow, onesrow_i), (ident, ident_i),
                     (rf_rep, rf_rep_i), (rf_bd, rf_bd_i),
                     (maskh, maskh_i), (sel_all, sel_i),
                     (clip_sb, clip2)]:
        ld(dst, src, "LC")
    setup_ld = p.c("LC")
    p.op("scalar", lambda e: e.activation(sig2, clip_sb, AF.Sigmoid,
                                          bias=czero[0:1], scale=one[0:1]),
         sem="SE", waits=[("LC", setup_ld)])
    p.op("vector", lambda e: e.memset(kvacc, 0.0), sem="VE")
    p.op("vector", lambda e: e.memset(zall, 0.0), sem="VE")
    kv_zero_ve = p.c("VE")

    for kc in range(KC):
        ld(wk_sb[:, kc * D:(kc + 1) * D], wk_t[kc * 128:(kc + 1) * 128, :], "LW")
        ld(wv_sb[:, kc * D:(kc + 1) * D], wv_t[kc * 128:(kc + 1) * 128, :], "LW")
    a_w_ld = p.c("LW")

    if stop == "A0":
        _finish(nc, p)
        return nc

    G["xa"] = {0: 0, 1: 0}
    G["ak_ve"] = {0: 0, 1: 0}
    G["ak_se"] = {0: 0, 1: 0}
    G["av_ve"] = {0: 0, 1: 0}
    G["akf_te"] = {0: 0, 1: 0, 2: 0, 3: 0}
    G["aphi"] = {0: 0, 1: 0, 2: 0, 3: 0}
    G["apt2"] = {0: 0, 1: 0, 2: 0, 3: 0}
    G["axa_te"] = 0
    G["asgn_gp"] = 0
    G["as1_gp"] = {0: 0, 1: 0}
    G["akvp"] = {0: 0, 1: 0}
    G["av65_prev"] = 0
    G["anorm_se"] = 0
    G["a_pre_se"] = 0

    for t in range(NT if stop != "A1" else 1):
        xt = xa0
        lx = "LX%d" % (t % 2)
        for kc in range(KC):
            ld(xt[:, kc * TT:(kc + 1) * TT],
               x_t[kc * 128:(kc + 1) * 128, t * TT:(t + 1) * TT], lx,
               waits=[("TE", G["axa_te"])])
        x_ld = p.c(lx)

        for oc in range(KC):
            pk = PS[oc % 2]
            for kc in range(KC):
                lhs = wk_sb[:, kc * D + oc * 128: kc * D + (oc + 1) * 128]
                rhs = xt[:, kc * TT:(kc + 1) * TT]
                p.op("tensor",
                     lambda e, o=pk, l=lhs, r=rhs, s=(kc == 0), z=(kc == KC - 1):
                     e.matmul(o, l, r, start=s, stop=z),
                     sem="TE" if kc == KC - 1 else None,
                     waits=([(lx, x_ld), ("LW", a_w_ld),
                             ("VE", G["ak_ve"][oc % 2]),
                             ("SE", G["ak_se"][oc % 2])] if kc == 0 else []))
            te_k = p.c("TE")
            p.op("vector", lambda e, d=ktile[:, oc * TT:(oc + 1) * TT], s=pk:
                 e.tensor_copy(d, s), sem="VE", waits=[("TE", te_k)])
            ve_kcp = p.c("VE")
            p.op("scalar", lambda e, d=ksq[:, oc * TT:(oc + 1) * TT],
                 s=ktile[:, oc * TT:(oc + 1) * TT]:
                 e.activation(d, s, AF.Square, bias=czero, scale=one),
                 sem="SE", waits=[("VE", ve_kcp)])
            G["ak_ve"][oc % 2] = p.c("VE")
            G["ak_se"][oc % 2] = p.c("SE")
        k_evac_ve = p.c("VE")
        k_evac_se = p.c("SE")

        for tb in range(NTB):
            if "n" not in amask:
                continue
            pn = PS[6]
            for kc in range(KC):
                lhs = ksq[:, kc * TT + tb * TB: kc * TT + (tb + 1) * TB]
                rhs = maskh[:, kc * H:(kc + 1) * H]
                p.op("tensor",
                     lambda e, o=pn[:, 0:H], l=lhs, r=rhs, s=(kc == 0), z=(kc == KC - 1):
                     e.matmul(o, l, r, start=s, stop=z),
                     sem="TE" if kc == KC - 1 else None,
                     waits=([("SE", k_evac_se), ("SE", G["anorm_se"]),
                             ("VE", G["akvp"][1])]
                            if kc == 0 else []))
            te_n = p.c("TE")
            p.op("scalar", lambda e, d=scale_a[:, tb * H:(tb + 1) * H], s=pn[:, 0:H]:
                 e.activation(d, s, AF.Exp,
                              bias=consts[:, C_MLN16:C_MLN16 + 1],
                              scale=consts[:, C_MHALF:C_MHALF + 1]),
                 sem="SE", waits=[("TE", te_n), ("VE", G["av65_prev"])])
            G["anorm_se"] = p.c("SE")
            se_scale = p.c("SE")

            if "v" not in amask:
                continue
            vb = v65a[:, tb * H * 65:(tb + 1) * H * 65]
            for nh in range(2):
                pv = PS[nh]
                for kc in range(KC):
                    lhs = xt[:, kc * TT + tb * TB: kc * TT + (tb + 1) * TB]
                    rhs = wv_sb[:, kc * D + nh * 512: kc * D + (nh + 1) * 512]
                    p.op("tensor",
                         lambda e, o=pv, l=lhs, r=rhs, s=(kc == 0), z=(kc == KC - 1):
                         e.matmul(o, l, r, start=s, stop=z),
                         sem="TE" if kc == KC - 1 else None,
                         waits=([("VE", G["av_ve"][nh]), ("VE", k_evac_ve),
                                 ("SE", k_evac_se)] if kc == 0 else []))
                te_v = p.c("TE")
                for hh in range(8):
                    h = nh * 8 + hh
                    p.op("vector",
                         lambda e, d=vb[:, h * 65: h * 65 + 64],
                         s=pv[:, hh * 64:(hh + 1) * 64],
                         sc=scale_a[:, tb * H + h:tb * H + h + 1]:
                         e.tensor_scalar(d, s, sc, None, ALU.mult),
                         sem="VE", waits=[("TE", te_v), ("SE", se_scale)])
                G["av_ve"][nh] = p.c("VE")
            p.op("vector",
                 lambda e, d=vb.rearrange("p (h c) -> p h c", c=65)[:, :, 64:65],
                 s=scale_a[:, tb * H:(tb + 1) * H].rearrange(
                     "p (h o) -> p h o", o=1):
                 e.tensor_copy(d, s), sem="VE")
        v65_ve = p.c("VE")
        G["a_pre_se"] = p.c("SE")
        G["axa_te"] = p.c("TE")

        if "v" in amask and "n" in amask:
            items = [(hp, tb) for hp in range(8) for tb in range(NTB)]
            te_phi = {}
            se_cos = {}
            gp_sin = {}
            te_stop = {}

            def emit_phi(i):
                hp, tb = items[i]
                par = i % 4
                pph = PS[(2, 3, 0, 1)[par]]
                lhs = ktile[:, hp * TT + tb * TB: hp * TT + (tb + 1) * TB]
                p.op("tensor", lambda e, o=pph, l=lhs:
                     e.matmul(o, l, rf_bd, start=True, stop=True),
                     sem="TE",
                     waits=[("VE", G["aphi"][par]), ("VE", k_evac_ve),
                            ("VE", v65_ve)])
                te_phi[i] = p.c("TE")

            def emit_epilogue(i):
                par = i % 4
                pph = PS[(2, 3, 0, 1)[par]]
                kfb = kfps[par]
                p0m = ap0[par]
                s1m = as1[i % 2]
                p.op("vector", lambda e, d=pti, s=pph:
                     e.tensor_scalar(d, s, 1.0 / (2 * PI), None, ALU.mult),
                     sem="VE", waits=[("TE", te_phi[i])])
                p.op("vector", lambda e, d=p0m, a=pti, s=pph:
                     e.scalar_tensor_tensor(d, a, -2 * PI, s, ALU.mult, ALU.add),
                     sem="VE", waits=[("SE", G["apt2"][par])])
                G["aphi"][par] = p.c("VE")
                ve_p0 = p.c("VE")
                p.op("vector", lambda e, d=actc, s=p0m:
                     e.tensor_scalar(d, s, 0.0, None, ALU.is_ge), sem="VE")
                p.op("vector", lambda e, d=asgn, s=actc:
                     e.tensor_scalar(d, s, 2.0, -1.0, ALU.mult, ALU.add),
                     sem="VE", waits=[("GP", G["asgn_gp"])])
                ve_sgn = p.c("VE")
                p.op("scalar", lambda e, d=au, s=p0m:
                     e.activation(d, s, AF.Abs, bias=czero, scale=one),
                     sem="SE", waits=[("VE", ve_p0)])
                p.op("scalar", lambda e, d=s1m, s=au:
                     e.activation(d, s, AF.Sin, bias=consts[:, C_PI:C_PI + 1],
                                  scale=consts[:, C_MONE:C_MONE + 1]),
                     sem="SE", waits=[("GP", G["as1_gp"][i % 2])])
                se_s1 = p.c("SE")
                for hh in range(2):
                    p.op("gpsimd", lambda e,
                         d=kfb[:, hh * M2 + MF:(hh + 1) * M2],
                         a=asgn[:, hh * MF:(hh + 1) * MF],
                         b=s1m[:, hh * MF:(hh + 1) * MF]:
                         e.tensor_tensor(d, a, b, ALU.mult), sem="GP",
                         waits=[("SE", se_s1), ("VE", ve_sgn),
                                ("GP", p.c("GP")), ("TE", G["akf_te"][par])])
                gp_sin[i] = p.c("GP")
                G["asgn_gp"] = gp_sin[i]
                G["as1_gp"][i % 2] = gp_sin[i]
                p.op("scalar", lambda e, d=p0m, s=au:
                     e.activation(d, s, AF.Abs, bias=consts[:, C_MPI:C_MPI + 1],
                                  scale=one),
                     sem="SE", waits=[("VE", ve_sgn)])
                for hh in range(2):
                    p.op("scalar", lambda e,
                         d=kfb[:, hh * M2:hh * M2 + MF],
                         s=p0m[:, hh * MF:(hh + 1) * MF]:
                         e.activation(d, s, AF.Sin,
                                      bias=consts[:, C_MHPI:C_MHPI + 1],
                                      scale=one),
                         sem="SE", waits=[("TE", G["akf_te"][par])])
                se_cos[i] = p.c("SE")
                G["apt2"][par] = se_cos[i]

            def emit_kv(i):
                hp, tb = items[i]
                par = i % 4
                kfb = kfps[par]
                bank = 4 + 2 * (hp % 2)
                for j in range(2):
                    h = 2 * hp + j
                    pk = PS[bank + j]
                    w = [("SE", se_cos[i]), ("GP", gp_sin[i])]
                    if tb == 0:
                        w += [("VE", G["akvp"][hp % 2]), ("VE", v65_ve),
                              ("SE", G["a_pre_se"])]
                    p.op("tensor", lambda e, o=pk[0:65, :],
                         l=v65a[:, (tb * H + h) * 65:(tb * H + h) * 65 + 65],
                         r=kfb[:, j * M2:(j + 1) * M2]:
                         e.matmul(o, l, r, start=(tb == 0), stop=(tb == NTB - 1)),
                         sem="TE" if j == 1 else None, waits=w)
                G["akf_te"][par] = p.c("TE")
                if tb == NTB - 1:
                    te_stop[hp] = p.c("TE")

            def emit_flush(hp):
                bank = 4 + 2 * (hp % 2)
                for j in range(2):
                    h = 2 * hp + j
                    p.op("vector", lambda e, d=kvacc[0:65, h * 512:(h + 1) * 512],
                         s=PS[bank + j][0:65, :]:
                         e.tensor_tensor(d, d, s, ALU.add),
                         sem="VE", waits=[("TE", te_stop[hp]), ("VE", kv_zero_ve)])
                G["akvp"][hp % 2] = p.c("VE")

            emit_phi(0)
            emit_phi(1)
            pend_flush = None
            for i, (hp, tb) in enumerate(items):
                if i + 2 < len(items):
                    emit_phi(i + 2)
                emit_epilogue(i)
                if pend_flush is not None:
                    emit_flush(pend_flush)
                    pend_flush = None
                emit_kv(i)
                if tb == NTB - 1:
                    pend_flush = hp
            if pend_flush is not None:
                emit_flush(pend_flush)
        G["xa"][t % 2] = p.c("TE")
        G["ak_ve"] = {0: p.c("VE"), 1: p.c("VE")}
        G["ak_se"] = {0: p.c("SE"), 1: p.c("SE")}
        G["av_ve"] = {0: p.c("VE"), 1: p.c("VE")}
        G["av65_prev"] = p.c("VE")

    p.op("vector", lambda e: e.memset(kvfm, 0.0), sem="VE",
         waits=[("TE", p.c("TE"))])
    for h in (range(H) if "t" in amask else []):
        for fc in range(4):
            pt_ps = PS[6]
            p.op("tensor", lambda e, o=pt_ps[:, 0:65],
                 i=kvacc[0:65, h * 512 + fc * 128: h * 512 + (fc + 1) * 128]:
                 e.transpose(o, i, ident[0:65, 0:65]),
                 sem="TE", waits=[("VE", p.c("VE")), ("SE", p.c("SE"))])
            te_t = p.c("TE")
            col = (h * 4 + fc) * 65
            if h % 2 == 0:
                p.op("vector", lambda e, d=kvfm[:, col:col + 64], s=pt_ps[:, 0:64]:
                     e.tensor_copy(d, s), sem="VE", waits=[("TE", te_t)])
                p.op("vector", lambda e, d=kvfm[:, col + 64:col + 65],
                     s=pt_ps[:, 64:65]:
                     e.tensor_copy(d, s), sem="VE")
            else:
                p.op("vector", lambda e, d=kvfm[:, col:col + 1],
                     s=pt_ps[:, 64:65]:
                     e.tensor_copy(d, s), sem="VE", waits=[("TE", te_t)])
                p.op("vector", lambda e, d=kvfm[:, col + 1:col + 65],
                     s=pt_ps[:, 0:64]:
                     e.tensor_copy(d, s), sem="VE")
    kvfm_ve = p.c("VE")

    if stop in ("A", "A1"):
        if debug:
            st(dbg[:, 0:H * 512], kvacc, "SO", waits=[("VE", kvfm_ve)])
            st(dbg[:, 8192:8192 + H * 4 * 65], kvfm, "SO", waits=[("VE", kvfm_ve)])
            st(dbg[:, 14336:14336 + 2048], ktile[:, 0:2048], "SO",
               waits=[("VE", kvfm_ve)])
        _finish(nc, p)
        return nc

    p.barrier_eng("sync")
    for kc in range(KC):
        ld(wq_sb[:, kc * D:(kc + 1) * D], wq_t[kc * 128:(kc + 1) * 128, :], "LW")
        ld(wo_sb[:, kc * D:(kc + 1) * D], wo_t[kc * 128:(kc + 1) * 128, :], "LW")
    b_w_ld = p.c("LW")

    G["xb"] = {0: 0, 1: 0}
    G["bxb"] = 0
    G["bqt_ve"] = {0: 0, 1: 0}
    G["bq_ve"] = {0: 0, 1: 0}
    G["bq_se"] = {0: 0, 1: 0}
    G["bqf_te"] = {0: 0, 1: 0}
    G["bqkv_ve"] = {0: 0, 1: 0}
    G["bphiB"] = {2: 0, 3: 0, 6: 0, 7: 0}
    G["bp0"] = {0: 0, 1: 0}
    G["bsgn_gp"] = {0: 0, 1: 0}
    G["bz_zg"] = {}
    G["bn_ve"] = 0
    G["brz_ve"] = 0
    G["bw_ve"] = {0: 0, 1: 0}
    G["y_ve"] = 0

    for t in range(NT):
        xt = xb0
        qtile = qtiles[t % 2]
        lx = "LX%d" % (t % 2)
        for kc in range(KC):
            ld(xt[:, kc * TT:(kc + 1) * TT],
               x_t[kc * 128:(kc + 1) * 128, t * TT:(t + 1) * TT], lx,
               waits=[("VE", kvfm_ve), ("VE", G["bxb"])])
        x_ld = p.c(lx)

        for oc in range(KC):
            pq = PS[oc % 2]
            for kc in range(KC):
                lhs = wq_sb[:, kc * D + oc * 128: kc * D + (oc + 1) * 128]
                rhs = xt[:, kc * TT:(kc + 1) * TT]
                p.op("tensor",
                     lambda e, o=pq, l=lhs, r=rhs, s=(kc == 0), z=(kc == KC - 1):
                     e.matmul(o, l, r, start=s, stop=z),
                     sem="TE" if kc == KC - 1 else None,
                     waits=([(lx, x_ld), ("LW", b_w_ld),
                             ("VE", G["bq_ve"][oc % 2]),
                             ("SE", G["bq_se"][oc % 2])] if kc == 0 else []))
            te_q = p.c("TE")
            p.op("vector", lambda e, d=qtile[:, oc * TT:(oc + 1) * TT], s=pq:
                 e.tensor_copy(d, s), sem="VE",
                 waits=[("TE", te_q), ("VE", G["bqt_ve"][t % 2])])
            ve_qcp = p.c("VE")
            p.op("scalar", lambda e, d=qsq[:, oc * TT:(oc + 1) * TT],
                 s=qtile[:, oc * TT:(oc + 1) * TT]:
                 e.activation(d, s, AF.Square, bias=czero, scale=one),
                 sem="SE", waits=[("VE", ve_qcp), ("SX", G.get("sx_prev", 0))])
            G["bq_ve"][oc % 2] = p.c("VE")
            G["bq_se"][oc % 2] = p.c("SE")
        q_evac_ve = p.c("VE")
        q_evac_se = p.c("SE")

        for tb in range(NTB):
            pn = PS[6]
            for kc in range(KC):
                lhs = qsq[:, kc * TT + tb * TB: kc * TT + (tb + 1) * TB]
                rhs = maskh[:, kc * H:(kc + 1) * H]
                p.op("tensor",
                     lambda e, o=pn[:, 0:H], l=lhs, r=rhs, s=(kc == 0), z=(kc == KC - 1):
                     e.matmul(o, l, r, start=s, stop=z),
                     sem="TE" if kc == KC - 1 else None,
                     waits=([("SE", q_evac_se), ("VE", G["bn_ve"]),
                             ("SE", G.get("bthr_se", 0))]
                            if kc == 0 else []))
            te_n = p.c("TE")
            p.op("vector", lambda e, d=sc512[:, 0:H], s=pn[:, 0:H]:
                 e.tensor_copy(d, s), sem="VE", waits=[("TE", te_n)])
            ve_nq = p.c("VE")
            p.op("tensor", lambda e, o=pn[0:H, 0:TB], i=sc512[:, 0:H]:
                 e.transpose(o, i, ident), sem="TE", waits=[("VE", ve_nq)])
            te_tr = p.c("TE")
            p.op("vector", lambda e, d=sc512[0:H, 0:TB], s=pn[0:H, 0:TB]:
                 e.tensor_scalar(d, s, 0.5, LN_THR, ALU.mult, ALU.add),
                 sem="VE", waits=[("TE", te_tr)])
            p.op("vector", lambda e, d=sc512[0:H, 0:TB]:
                 e.tensor_scalar(d, d, 80.0, None, ALU.min), sem="VE")
            ve_thr = p.c("VE")
            p.op("scalar", lambda e, d=thr_fm[:, tb * TB:(tb + 1) * TB],
                 s=sc512[0:H, 0:TB]:
                 e.activation(d, s, AF.Exp, bias=czero[0:H], scale=one[0:H]),
                 sem="SE", waits=[("VE", ve_thr), ("TE", te_tr)])
            G["bthr_se"] = p.c("SE")
            G["bn_ve"] = p.c("VE")
        thr_se = p.c("SE")

        def b_flush(pend):
            h_, te_qkv_, oc_ = pend
            even_ = (h_ % 2 == 0)
            pk_ = PS[4] if even_ else PS[5]
            if even_:
                pay = pk_[0:64, :]; zrow = pk_[64:65, :]
                drow = attn[0:64, oc_ * TT:(oc_ + 1) * TT]
                zdst = zst[64:65, 0:TT]
            else:
                pay = pk_[64:128, :]; zrow = pk_[0:1, :]
                drow = attn[64:128, oc_ * TT:(oc_ + 1) * TT]
                zdst = zst[0:1, 0:TT]
            p.op("vector", lambda e, d=drow, s=pay: e.tensor_copy(d, s),
                 sem="VE", waits=[("TE", te_qkv_)])
            p.op("vector", lambda e, d=zdst, s=zrow: e.tensor_copy(d, s),
                 sem="VE", waits=[("ZG", G["bz_zg"].get(h_ // 2, 0))])
            ve_z = p.c("VE")
            p.op("scalar", lambda e, d=zall[h_:h_ + 1, :], s=zdst:
                 e.dma_start(out=d, in_=s), sem="ZG", by=16,
                 waits=[("VE", ve_z)])
            G["bqkv_ve"][h_ % 2] = p.c("VE")
            if h_ % 2 == 1:
                G["bz_zg"][h_ // 2 + 1] = p.c("ZG")
        te_phiH = {}
        se_cosH = {}
        gp_sinH = {}

        def b_phi(h):
            oc_, hi_ = h // 2, h % 2
            base = 0 if hi_ == 0 else 64
            banks = (2, 3) if h % 2 == 0 else (6, 7)
            tp = (0, 0) if hi_ == 0 else (64, 0)
            tps = []
            for mc in range(2):
                pps = PS[banks[mc]]
                lhs = rf_rep[base:base + 64, mc * 128:(mc + 1) * 128]
                rhs = qtile[base:base + 64, oc_ * TT:(oc_ + 1) * TT]
                p.op("tensor", lambda e, o=pps, l=lhs, r=rhs, tp_=tp:
                     e.matmul(o, l, r, start=True, stop=True, tile_position=tp_),
                     sem="TE",
                     waits=[("VE", G["bphiB"][banks[mc]]), ("VE", q_evac_ve)])
                tps.append(p.c("TE"))
            te_phiH[h] = tps

        def b_epi(h):
            qf = qf0 if h % 2 == 0 else qf1
            banks = (2, 3) if h % 2 == 0 else (6, 7)
            for mc in range(2):
                pps = PS[banks[mc]]
                p0m, um, sgnm = bp0[mc], bu[mc], bsgn[mc]
                s1m, a2m = bs1[mc], ba2[mc]
                p.op("vector", lambda e, d=ptbi, s=pps:
                     e.tensor_scalar(d, s, 1.0 / (2 * PI), None, ALU.mult),
                     sem="VE", waits=[("TE", te_phiH[h][mc])])
                p.op("vector", lambda e, d=p0m, a=ptbi, s=pps:
                     e.scalar_tensor_tensor(d, a, -2 * PI, s, ALU.mult, ALU.add),
                     sem="VE", waits=[("SE", G["bp0"][mc])])
                G["bphiB"][banks[mc]] = p.c("VE")
                ve_p0 = p.c("VE")
                p.op("vector", lambda e, d=actcB, s=p0m:
                     e.tensor_scalar(d, s, 0.0, None, ALU.is_ge), sem="VE")
                p.op("vector", lambda e, d=sgnm, s=actcB:
                     e.tensor_scalar(d, s, 2.0, -1.0, ALU.mult, ALU.add),
                     sem="VE", waits=[("GP", G["bsgn_gp"][mc])])
                ve_sgn = p.c("VE")
                p.op("scalar", lambda e, d=um, s=p0m:
                     e.activation(d, s, AF.Abs, bias=czero, scale=one),
                     sem="SE", waits=[("VE", ve_p0)])
                G["bp0"][mc] = p.c("SE")
                p.op("scalar", lambda e, d=s1m, s=um:
                     e.activation(d, s, AF.Sin, bias=consts[:, C_PI:C_PI + 1],
                                  scale=consts[:, C_MONE:C_MONE + 1]),
                     sem="SE", waits=[("GP", G["bsgn_gp"][mc])])
                se_s1 = p.c("SE")
                p.op("gpsimd", lambda e, d=qf[:, (2 + mc) * TT:(3 + mc) * TT],
                     a=sgnm, b=s1m:
                     e.tensor_tensor(d, a, b, ALU.mult), sem="GP",
                     waits=[("SE", se_s1), ("VE", ve_sgn), ("GP", p.c("GP")),
                            ("TE", G["bqf_te"][h % 2])])
                G["bsgn_gp"][mc] = p.c("GP")
                p.op("scalar", lambda e, d=a2m, s=um:
                     e.activation(d, s, AF.Abs, bias=consts[:, C_MPI:C_MPI + 1],
                                  scale=one), sem="SE")
                p.op("scalar", lambda e, d=qf[:, mc * TT:(mc + 1) * TT], s=a2m:
                     e.activation(d, s, AF.Sin, bias=consts[:, C_MHPI:C_MHPI + 1],
                                  scale=one),
                     sem="SE", waits=[("TE", G["bqf_te"][h % 2])])
            se_cosH[h] = p.c("SE")
            gp_sinH[h] = p.c("GP")

        def b_qkv(h):
            qf = qf0 if h % 2 == 0 else qf1
            even = (h % 2 == 0)
            pk = PS[4] if even else PS[5]
            w0 = [("SE", se_cosH[h]), ("GP", gp_sinH[h]),
                  ("VE", G["bqkv_ve"][h % 2])]
            if even:
                for fc in range(4):
                    lhs = kvfm[:, (h * 4 + fc) * 65:(h * 4 + fc) * 65 + 65]
                    rhs = qf[:, fc * TT:(fc + 1) * TT]
                    p.op("tensor",
                         lambda e, o=pk[0:65, :], l=lhs, r=rhs,
                         s=(fc == 0), z=(fc == 3):
                         e.matmul(o, l, r, start=s, stop=z),
                         sem="TE" if fc == 3 else None,
                         waits=(w0 if fc == 0 else []))
            else:
                for fc in range(4):
                    lhs = kvfm[:, (h * 4 + fc) * 65 + 1:(h * 4 + fc) * 65 + 65]
                    rhs = qf[:, fc * TT:(fc + 1) * TT]
                    p.op("tensor",
                         lambda e, o=pk[64:128, :], l=lhs, r=rhs,
                         s=(fc == 0), z=(fc == 3):
                         e.matmul(o, l, r, start=s, stop=z),
                         sem=None, waits=(w0 if fc == 0 else []))
                for fc in range(4):
                    lhs = kvfm[:, (h * 4 + fc) * 65:(h * 4 + fc) * 65 + 1]
                    rhs = qf[:, fc * TT:(fc + 1) * TT]
                    p.op("tensor",
                         lambda e, o=pk[0:1, :], l=lhs, r=rhs,
                         s=(fc == 0), z=(fc == 3):
                         e.matmul(o, l, r, start=s, stop=z),
                         sem="TE" if fc == 3 else None)
            te_qkv = p.c("TE")
            G["bqf_te"][h % 2] = te_qkv
            return te_qkv

        G["bphiB"][6] = p.c("VE")
        b_phi(0)
        b_phi(1)
        pend = None
        for h in range(H):
            b_epi(h)
            if h + 2 < H:
                b_phi(h + 2)
            if pend is not None:
                b_flush(pend)
                pend = None
            te_qkv = b_qkv(h)
            pend = (h, te_qkv, h // 2)
        b_flush(pend)
        pend = None
        zg_all = p.c("ZG")
        attn_ve = p.c("VE")

        p.op("vector", lambda e: e.tensor_tensor(zall[0:H, :], zall[0:H, :], thr_fm, ALU.max),
             sem="VE", waits=[("ZG", zg_all), ("SE", thr_se)])
        p.op("vector", lambda e: e.reciprocal(zall[0:H, :], zall[0:H, :]), sem="VE")
        rz_ve = p.c("VE")

        for oc in range(KC):
            prz = PS[7]
            p.op("tensor", lambda e, o=prz, l=sel_all[:, oc * 128:(oc + 1) * 128],
                 r=zall:
                 e.matmul(o, l, r, start=True, stop=True),
                 sem="TE", waits=[("VE", rz_ve), ("VE", G["brz_ve"])])
            te_rz = p.c("TE")
            p.op("vector", lambda e, d=attn[:, oc * TT:(oc + 1) * TT], s=prz:
                 e.tensor_tensor(d, d, s, ALU.mult),
                 sem="VE", waits=[("TE", te_rz), ("VE", attn_ve)])
            G["brz_ve"] = p.c("VE")
        G["bphiB"][7] = p.c("VE")
        attn_scaled_ve = p.c("VE")

        for oc in range(KC):
            pw = PS[oc % 2]
            for kc in range(KC):
                lhs = wo_sb[:, kc * D + oc * 128: kc * D + (oc + 1) * 128]
                rhs = attn[:, kc * TT:(kc + 1) * TT]
                p.op("tensor",
                     lambda e, o=pw, l=lhs, r=rhs, s=(kc == 0), z=(kc == KC - 1):
                     e.matmul(o, l, r, start=s, stop=z),
                     sem="TE" if kc == KC - 1 else None,
                     waits=([("VE", attn_scaled_ve), ("VE", G["bw_ve"][oc % 2])]
                            if kc == 0 else []))
            te_w = p.c("TE")
            p.op("vector", lambda e, d=qtile[:, oc * TT:(oc + 1) * TT],
                 s=pw, xx=xt[:, oc * TT:(oc + 1) * TT]:
                 e.tensor_tensor(d, xx, s, ALU.add),
                 sem="VE", waits=[("TE", te_w)])
            G["bw_ve"][oc % 2] = p.c("VE")
        G["y_ve"] = p.c("VE")
        G["bxb"] = p.c("VE")

        def ln1_out(oc, ve_x, _t=t):
            st(xln_s[oc * 128:(oc + 1) * 128, _t * TT:(_t + 1) * TT],
               qsq[:, oc * TT:(oc + 1) * TT], "SX", waits=[("VE", ve_x)])
            p.op("vector", lambda e, d=xmaxst[:, _t * KC + oc: _t * KC + oc + 1],
                 s=qsq[:, oc * TT:(oc + 1) * TT]:
                 e.tensor_reduce(d, s, AX.X, ALU.max, apply_absolute_value=True),
                 sem="VE")
        ln_emit(qtile, qsq, qsq, ln1_out, PS[6], PS[7])
        G["sx_prev"] = p.c("SX")
        G["bqt_ve"][t % 2] = p.c("VE")
        G["bn_ve"] = p.c("VE")
        G["brz_ve"] = p.c("VE")
        G["bw_ve"] = {0: p.c("VE"), 1: p.c("VE")}
        G["bqkv_ve"] = {0: p.c("VE"), 1: p.c("VE")}
        G["bqf_te"] = {0: p.c("TE"), 1: p.c("TE")}
    b_st_all = p.c("SX")
    b_ve_all = p.c("VE")

    if stop == "B":
        if debug:
            st(dbg[:, 0:KC * TT], attn, "SO", waits=[("VE", b_ve_all)])
            st(dbg[0:H, 4096:4096 + TT], zall[0:H, :], "SO",
               waits=[("VE", b_ve_all)])
            st(dbg[0:H, 4608:4608 + TT], thr_fm, "SO", waits=[("VE", b_ve_all)])
            st(dbg[:, 5120:5120 + KC], xmaxst[:, (NT - 1) * KC:NT * KC],
               "SO", waits=[("VE", b_ve_all)])
        p.wait("gpsimd", [("SX", b_st_all)])
        for kc in range(KC):
            st(y_t[kc * 128:(kc + 1) * 128, :], xln_s[kc * 128:(kc + 1) * 128, :],
               "SO")
        _finish(nc, p)
        return nc

    p.op("vector", lambda e: e.tensor_reduce(m1col, xmaxst, AX.X, ALU.max),
         sem="VE", waits=[("VE", b_ve_all)])
    ve_m = p.c("VE")
    p.op("tensor", lambda e, o=PS[6][0:1, 0:128], i=m1col:
         e.transpose(o, i, ident), sem="TE",
         waits=[("VE", ve_m), ("SE", p.c("SE"))])
    te_m = p.c("TE")
    p.op("vector", lambda e: e.tensor_reduce(r1x1, PS[6][0:1, 0:128], AX.X, ALU.max),
         sem="VE", waits=[("TE", te_m)])
    p.op("vector", lambda e: e.tensor_scalar(ccst, onesrow[0:1, 0:8],
                                             r1x1[0:1, 0:1], None, ALU.mult),
         sem="VE")
    ve_cc = p.c("VE")
    p.op("gpsimd", lambda e: e.dma_start(out=cc_in1.ap(), in_=ccst),
         sem="SC", by=16, waits=[("VE", ve_cc)])
    p.wait("gpsimd", [("SC", p.c("SC"))])
    p.op("gpsimd", lambda e: e.collective_compute(
        "AllReduce", ALU.max, replica_groups=[[0, 1, 2, 3, 4, 5, 6, 7]],
        ins=[cc_in1.ap().opt()], outs=[cc_out1.ap().opt()]), sem="GE")
    p.wait("gpsimd", [("GE", p.c("GE"))])
    p.op("gpsimd", lambda e: e.dma_start(out=gmax1, in_=cc_out1.ap()[0:1, 0:1]),
         sem="SC", by=16)
    g1_st = p.c("SC")
    p.op("vector", lambda e: e.tensor_tensor(r1x1, gmax1, sig2[0:1, 0:1], ALU.mult),
         sem="VE", waits=[("SC", g1_st)])
    p.op("vector", lambda e: e.reciprocal(r1x1b, r1x1), sem="VE")
    p.op("vector", lambda e: e.tensor_scalar(r1x1b, r1x1b, 127.0, None, ALU.mult),
         sem="VE")
    p.op("vector", lambda e: e.tensor_scalar(r1x1, r1x1, INV127 * INV127, None,
                                             ALU.mult), sem="VE")
    ve_rx = p.c("VE")
    p.op("tensor", lambda e, o=PS[6][:, 0:1]: e.matmul(o, onesrow, r1x1b,
                                                       start=True, stop=True),
         sem="TE", waits=[("VE", ve_rx)])
    p.op("tensor", lambda e, o=PS[7][:, 0:1]: e.matmul(o, onesrow, r1x1,
                                                       start=True, stop=True),
         sem="TE")
    te_rx = p.c("TE")
    p.op("vector", lambda e: e.tensor_copy(rx_col, PS[6][:, 0:1]),
         sem="VE", waits=[("TE", te_rx)])
    p.op("vector", lambda e: e.tensor_copy(sxs_col, PS[7][:, 0:1]), sem="VE")
    rx_ve = p.c("VE")

    if stop == "AR1":
        if debug:
            st(dbg[0:1, 0:1], gmax1, "SO", waits=[("VE", rx_ve)])
            st(dbg[:, 4:5], rx_col, "SO", waits=[("VE", rx_ve)])
            st(dbg[:, 8:9], m1col, "SO", waits=[("VE", rx_ve)])
            st(dbg[0:1, 16:24], ccst, "SO", waits=[("VE", rx_ve)])
            st(dbg[:, 32:96], xmaxst, "SO", waits=[("VE", rx_ve)])
        _finish(nc, p)
        return nc

    p.barrier_eng("sync")

    def quant_weight(wnat, wtt, nch_nat, nat_w, sall, tgt, nbufs):
        wout = nch_nat * 128
        kcin = nat_w // 128
        nb = len(nbufs)
        G["fqn_ve"] = {i: 0 for i in range(nb)}
        for j in range(nch_nat):
            buf = nbufs[j % nb]
            lh = "LH%d" % (j % 4)
            ld(buf, wnat[j * 128:(j + 1) * 128, :], lh,
               waits=[("VE", G["fqn_ve"][j % nb]),
                      ("VE", G.get("fq_ve", 0)), ("VE", G.get("fqm_ve", 0)),
                      ("VE", G.get("fqq_ve", 0)),
                      ("SE", G.get("fq_se", 0))])
            ldj = p.c(lh)
            p.op("vector", lambda e, d=sall[:, j:j + 1], s=buf:
                 e.tensor_reduce(d, s, AX.X, ALU.max, apply_absolute_value=True),
                 sem="VE", waits=[(lh, ldj)])
            G["fqn_ve"][j % nb] = p.c("VE")
        G["fq_ve"] = p.c("VE")
        p.op("vector", lambda e, d=mred[:, 0:nch_nat], s=sall[:, 0:nch_nat]:
             e.reciprocal(d, s), sem="VE")
        p.op("vector", lambda e, d=mred[:, 0:nch_nat]:
             e.tensor_scalar(d, d, 127.0, None, ALU.mult), sem="VE")
        ve_r = p.c("VE")
        p.op("tensor", lambda e, o=PS[6][0:nch_nat, 0:128], i=mred[:, 0:nch_nat]:
             e.transpose(o, i, ident), sem="TE",
             waits=[("VE", ve_r), ("VE", G.get("fqb_ve", 0))])
        te_r = p.c("TE")
        p.op("vector", lambda e, d=tp22[0:nch_nat, :], s=PS[6][0:nch_nat, 0:128]:
             e.tensor_copy(d, s), sem="VE", waits=[("TE", te_r)])
        ve_t = p.c("VE")
        for j in range(nch_nat):
            p.op("scalar", lambda e, d=rb[0:1, j * 128:(j + 1) * 128],
                 s=tp22[j:j + 1, :]:
                 e.dma_start(out=d, in_=s), sem="ZG", by=16,
                 waits=[("VE", ve_t)])
        zg_r = p.c("ZG")
        nslab = (wout + 511) // 512
        for s_ in range(nslab):
            wlen = min(512, wout - s_ * 512)
            p.op("tensor", lambda e, o=PS[6][:, 0:wlen], l=onesrow,
                 r=rb[0:1, s_ * 512:s_ * 512 + wlen]:
                 e.matmul(o, l, r, start=True, stop=True),
                 sem="TE", waits=[("ZG", zg_r), ("VE", G.get("fqb_ve", 0))])
            te_b = p.c("TE")
            p.op("vector", lambda e, d=rb[:, s_ * 512:s_ * 512 + wlen],
                 s=PS[6][:, 0:wlen]:
                 e.tensor_copy(d, s), sem="VE", waits=[("TE", te_b)])
            G["fqb_ve"] = p.c("VE")
        rb_ve = p.c("VE")
        hw = wout // 2
        G["fqs"] = {0: 0, 1: 0}
        gidx = 0
        for kc in range(kcin):
            mults = []
            for gi in range(2):
                slot = gidx % 2
                gidx += 1
                sbuf = wtc0[:, slot * hw:(slot + 1) * hw]
                lh = "LH%d" % (gidx % 4)
                ld(sbuf, wtt[kc * 128:(kc + 1) * 128, gi * hw:(gi + 1) * hw],
                   lh, waits=[("VE", G["fqs"][slot]), ("VE", G.get("fq_ve", 0))])
                ldc = p.c(lh)
                p.op("vector", lambda e, d=avq[:, gi * hw:(gi + 1) * hw],
                     a=sbuf, b=rb[:, gi * hw:(gi + 1) * hw]:
                     e.tensor_tensor(d, a, b, ALU.mult),
                     sem="VE", waits=[(lh, ldc), ("VE", rb_ve),
                                      ("VE", G.get("fqq_ve", 0))])
                G["fqs"][slot] = p.c("VE")
            ve_m_ = p.c("VE")
            G["fqm_ve"] = ve_m_
            p.op("scalar", lambda e, d=sgnq[:, 0:wout], s=avq[:, 0:wout]:
                 e.activation(d, s, AF.Sign, bias=czero, scale=one),
                 sem="SE", waits=[("VE", ve_m_), ("VE", G.get("fqq_ve", 0))])
            se_a = p.c("SE")
            G["fq_se"] = se_a
            p.op("vector", lambda e, d=t2q[:, 0:wout], a=sgnq[:, 0:wout],
                 b=avq[:, 0:wout]:
                 e.scalar_tensor_tensor(d, a, 128.0, b, ALU.mult, ALU.add),
                 sem="VE", waits=[("SE", se_a)])
            p.op("vector", lambda e, d=tgt[:, kc * wout:(kc + 1) * wout],
                 a=sgnq[:, 0:wout], b=t2q[:, 0:wout]:
                 e.scalar_tensor_tensor(d, a, -128.0, b, ALU.mult, ALU.add),
                 sem="VE")
            G["fqq_ve"] = p.c("VE")
        return p.c("VE")

    quant_weight(wg_n, wg_tt, ICN, D, sall_g, wgq,
                 [wtc0[:, 0:D], wtc0[:, D:2 * D], avq[:, 0:D], avq[:, D:2 * D]])
    quant_weight(wu_n, wu_tt, ICN, D, sall_u, wuq,
                 [wtc0[:, 0:D], wtc0[:, D:2 * D], avq[:, 0:D], avq[:, D:2 * D]])
    wdq_ve = quant_weight(wd_n, wd_tt, KC, I, sall_d, wdq,
                          [wtc0[:, 0:I], avq[:, 0:I], natd2, natd3])
    for kc in range(ICN):
        st(wdq_s[kc * 128:(kc + 1) * 128, :], wdq[:, kc * D:(kc + 1) * D],
           "SW", waits=[("VE", wdq_ve)])
    wq_st = p.c("SW")
    wq_ve = p.c("VE")

    p.op("vector", lambda e: e.tensor_scalar(ax_all, sall_g, sxs_col, None, ALU.mult),
         sem="VE", waits=[("VE", rx_ve)])
    p.op("vector", lambda e: e.tensor_scalar(bu_all, sall_u, sxs_col, None, ALU.mult),
         sem="VE")
    scales_ve = p.c("VE")

    if stop == "FFNQ":
        if debug:
            st(dbg[:, 0:5632], wgq[:, 0:5632], "SO", waits=[("VE", scales_ve)])
            st(dbg[:, 6000:6000 + ICN], sall_g, "SO", waits=[("VE", scales_ve)])
            st(dbg[:, 6030:6030 + ICN], ax_all, "SO", waits=[("VE", scales_ve)])
        _finish(nc, p)
        return nc

    p.barrier_eng("sync")
    G["f1x"] = {0: 0, 1: 0}
    G["f1g_se"] = [0, 0, 0, 0]
    G["f1h_ve"] = [0, 0, 0, 0]
    G["f1s_st"] = {}
    G["f1q_ve"] = 0
    for t in range(NT):
        xt = xl0 if t % 2 == 0 else xl1
        lx = "LX%d" % (t % 2)
        for kc in range(KC):
            ld(xt[:, kc * TT:(kc + 1) * TT],
               xln_s[kc * 128:(kc + 1) * 128, t * TT:(t + 1) * TT], lx,
               waits=[("TE", G["f1x"][t % 2]), ("SW", wq_st), ("VE", wq_ve),
                      ("SX", b_st_all), ("SE", G.get("f1a_se", 0))])
        x_ld = p.c(lx)
        p.op("scalar", lambda e, d=avf, s=xt:
             e.activation(d, s, AF.Abs, bias=czero, scale=rx_col),
             sem="SE", waits=[(lx, x_ld), ("VE", scales_ve), ("VE", G["f1q_ve"])])
        p.op("scalar", lambda e, d=sgnf, s=xt:
             e.activation(d, s, AF.Sign, bias=czero, scale=one), sem="SE")
        se_q = p.c("SE")
        G["f1a_se"] = se_q
        xqb = xqb2[t % 2]
        p.op("vector", lambda e, d=t2f, s=avf:
             e.tensor_scalar(d, s, 128.0, 256.49, ALU.add, ALU.min),
             sem="VE", waits=[("SE", se_q)])
        p.op("vector", lambda e, d=xqb, a=t2f, b=sgnf:
             e.scalar_tensor_tensor(d, a, -128.0, b, ALU.add, ALU.mult),
             sem="VE", waits=[("TE", G["f1x"][t % 2])])
        p.op("vector", lambda e, d=xqb: e.tensor_scalar(d, d, 127.0, None, ALU.min),
             sem="VE")
        xq_ve = p.c("VE")
        G["f1q_ve"] = xq_ve

        for ic in range(ICN):
            pg = PS[(ic % 2) * 2]
            pu = PS[(ic % 2) * 2 + 1]
            for kc in range(KC):
                lg = wgq[:, kc * I + ic * 128: kc * I + (ic + 1) * 128]
                lu = wuq[:, kc * I + ic * 128: kc * I + (ic + 1) * 128]
                rr = xqb[:, kc * TT:(kc + 1) * TT]
                p.op("tensor", lambda e, o=pg, l=lg, r=rr, s=(kc == 0),
                     z=(kc == KC - 1):
                     e.matmul(o, l, r, start=s, stop=z),
                     sem="TE" if kc == KC - 1 else None,
                     waits=([("VE", xq_ve), ("SE", G["f1g_se"][(ic % 2) * 2])]
                            if kc == 0 else []))
                p.op("tensor", lambda e, o=pu, l=lu, r=rr, s=(kc == 0),
                     z=(kc == KC - 1):
                     e.matmul(o, l, r, start=s, stop=z),
                     sem="TE" if kc == KC - 1 else None,
                     waits=([("VE", G["f1h_ve"][(ic % 2) * 2 + 1])]
                            if kc == 0 else []))
            te_gu = p.c("TE")
            sl = silu0 if ic % 2 == 0 else silu1
            hbuf = hb[ic % 4]
            p.op("scalar", lambda e, d=sl, s=pg, sc=ax_all[:, ic:ic + 1]:
                 e.activation(d, s, AF.Silu, bias=czero, scale=sc),
                 sem="SE", waits=[("TE", te_gu)])
            se_s = p.c("SE")
            G["f1g_se"][(ic % 2) * 2] = se_s
            p.op("vector", lambda e, d=hbuf, a=sl, b=pu:
                 e.tensor_tensor(d, a, b, ALU.mult),
                 sem="VE", waits=[("SE", se_s),
                                  ("SH%d" % (ic % 4), G["f1s_st"].get(ic % 4, 0))])
            ve_h = p.c("VE")
            G["f1h_ve"][(ic % 2) * 2] = ve_h
            G["f1h_ve"][(ic % 2) * 2 + 1] = ve_h
            st(h_s[ic * 128:(ic + 1) * 128, t * TT:(t + 1) * TT], hbuf,
               "SH%d" % (ic % 4), waits=[("VE", ve_h)])
            G["f1s_st"][ic % 4] = p.c("SH%d" % (ic % 4))
            p.op("vector", lambda e, d=hmaxst[:, t * ICN + ic: t * ICN + ic + 1],
                 s=hbuf:
                 e.tensor_reduce(d, s, AX.X, ALU.max, apply_absolute_value=True),
                 sem="VE")
        G["f1x"][t % 2] = p.c("TE")
        G["f1g_se"] = [p.c("SE")] * 4
        G["f1h_ve"] = [p.c("VE")] * 4
    f1_ve = p.c("VE")
    f1_st = tuple(p.c("SH%d" % i) for i in range(4))

    hview = hmaxst.rearrange("p (t c) -> p c t", c=ICN)
    p.op("vector", lambda e: e.tensor_reduce(mred, hview, AX.X, ALU.max),
         sem="VE", waits=[("VE", f1_ve)])
    p.op("vector", lambda e: e.tensor_tensor(mred, mred, bu_all, ALU.mult), sem="VE")
    p.op("vector", lambda e: e.tensor_reduce(m1col, mred, AX.X, ALU.max), sem="VE")
    ve_m2 = p.c("VE")
    p.op("tensor", lambda e, o=PS[6][0:1, 0:128], i=m1col:
         e.transpose(o, i, ident), sem="TE",
         waits=[("VE", ve_m2), ("SE", p.c("SE"))])
    te_m2 = p.c("TE")
    p.op("vector", lambda e: e.tensor_reduce(r1x1, PS[6][0:1, 0:128], AX.X, ALU.max),
         sem="VE", waits=[("TE", te_m2)])
    p.op("vector", lambda e: e.tensor_scalar(ccst, onesrow[0:1, 0:8],
                                             r1x1[0:1, 0:1], None, ALU.mult),
         sem="VE")
    ve_cc2 = p.c("VE")
    p.op("gpsimd", lambda e: e.dma_start(out=cc_in2.ap(), in_=ccst),
         sem="SC", by=16, waits=[("VE", ve_cc2)])
    p.wait("gpsimd", [("SC", p.c("SC"))])
    p.op("gpsimd", lambda e: e.collective_compute(
        "AllReduce", ALU.max, replica_groups=[[0, 1, 2, 3, 4, 5, 6, 7]],
        ins=[cc_in2.ap().opt()], outs=[cc_out2.ap().opt()]), sem="GE")
    p.wait("gpsimd", [("GE", p.c("GE"))])
    p.op("gpsimd", lambda e: e.dma_start(out=gmax2, in_=cc_out2.ap()[0:1, 0:1]),
         sem="SC", by=16)
    g2_st = p.c("SC")
    p.op("vector", lambda e: e.tensor_tensor(r1x1, gmax2, sig2[0:1, 1:2], ALU.mult),
         sem="VE", waits=[("SC", g2_st)])
    p.op("vector", lambda e: e.reciprocal(r1x1b, r1x1), sem="VE")
    p.op("vector", lambda e: e.tensor_scalar(r1x1b, r1x1b, 127.0, None, ALU.mult),
         sem="VE")
    p.op("vector", lambda e: e.tensor_scalar(r1x1, r1x1, INV127 * INV127, None,
                                             ALU.mult), sem="VE")
    ve_rh = p.c("VE")
    p.op("tensor", lambda e, o=PS[6][:, 0:1]: e.matmul(o, onesrow, r1x1b,
                                                       start=True, stop=True),
         sem="TE", waits=[("VE", ve_rh)])
    p.op("tensor", lambda e, o=PS[7][:, 0:1]: e.matmul(o, onesrow, r1x1,
                                                       start=True, stop=True),
         sem="TE")
    te_rh = p.c("TE")
    p.op("vector", lambda e: e.tensor_copy(rh_col, PS[6][:, 0:1]),
         sem="VE", waits=[("TE", te_rh)])
    p.op("vector", lambda e: e.tensor_copy(shs_col, PS[7][:, 0:1]), sem="VE")
    p.op("vector", lambda e: e.tensor_scalar(reff_all, bu_all, rh_col, None,
                                             ALU.mult), sem="VE")
    p.op("vector", lambda e: e.tensor_scalar(cd_all, sall_d, shs_col, None,
                                             ALU.mult), sem="VE")
    rh_ve = p.c("VE")

    if stop == "FFN1":
        if debug:
            st(dbg[0:1, 0:1], gmax2, "SO", waits=[("VE", rh_ve)])
            st(dbg[:, 4:4 + ICN], reff_all, "SO", waits=[("VE", rh_ve)])
            st(dbg[:, 30:30 + KC], cd_all, "SO", waits=[("VE", rh_ve)])
            st(dbg[:, 40:40 + NT * ICN], hmaxst, "SO", waits=[("VE", rh_ve)])
        _finish(nc, p)
        return nc

    p.barrier_eng("sync")
    for kc in range(ICN):
        ld(wdq2[:, kc * D:(kc + 1) * D], wdq_s[kc * 128:(kc + 1) * 128, :],
           "LW", waits=[("SW", wq_st)])
    wd_ld = p.c("LW")

    G["f2x"] = {0: 0, 1: 0}
    G["f2h_se"] = {}
    G["f2d_ve"] = {0: 0, 1: 0}
    G["f2q_ve"] = {0: 0, 1: 0}
    G["f2hq_te"] = {0: 0, 1: 0}
    G["f2s_ve"] = 0
    for t in range(NT):
        xt = xl20 if t % 2 == 0 else xl21
        hqt = hq0 if t % 2 == 0 else hq1
        lx = "LX%d" % (t % 2)
        for kc in range(KC):
            ld(xt[:, kc * TT:(kc + 1) * TT],
               xln_s[kc * 128:(kc + 1) * 128, t * TT:(t + 1) * TT], lx,
               waits=[("VE", G["f2x"][t % 2])])
        x_ld = p.c(lx)
        for ic in range(ICN):
            hbuf = hl[ic % 4]
            lh = "LH%d" % (ic % 4)
            ld(hbuf, h_s[ic * 128:(ic + 1) * 128, t * TT:(t + 1) * TT], lh,
               waits=[("SE", G["f2h_se"].get(ic % 4, 0)),
                      ("SH0", f1_st[0]), ("SH1", f1_st[1]),
                      ("SH2", f1_st[2]), ("SH3", f1_st[3])])
            ld_h = p.c(lh)
            par = ic % 2
            av_, sg_, t2_ = avf2[par], sgnf2[par], t2f2[par]
            p.op("scalar", lambda e, d=av_, s=hbuf, sc=reff_all[:, ic:ic + 1]:
                 e.activation(d, s, AF.Abs, bias=czero, scale=sc),
                 sem="SE", waits=[(lh, ld_h), ("VE", rh_ve),
                                  ("VE", G["f2q_ve"][par])])
            p.op("scalar", lambda e, d=sg_, s=hbuf:
                 e.activation(d, s, AF.Sign, bias=czero, scale=one), sem="SE")
            se_q = p.c("SE")
            G["f2h_se"][ic % 4] = se_q
            p.op("vector", lambda e, d=t2_, s=av_:
                 e.tensor_scalar(d, s, 128.0, 256.49, ALU.add, ALU.min),
                 sem="VE", waits=[("SE", se_q)])
            p.op("vector", lambda e, d=hqt[:, ic * TT:(ic + 1) * TT], a=t2_,
                 b=sg_:
                 e.scalar_tensor_tensor(d, a, -128.0, b, ALU.add, ALU.mult),
                 sem="VE", waits=[("TE", G["f2hq_te"][t % 2])])
            p.op("vector", lambda e, d=hqt[:, ic * TT:(ic + 1) * TT]:
                 e.tensor_scalar(d, d, 127.0, None, ALU.min), sem="VE")
            G["f2q_ve"][par] = p.c("VE")
        hq_ve = p.c("VE")

        for oc in range(KC):
            pd = PS[oc % 2]
            for kc in range(ICN):
                lhs = wdq2[:, kc * D + oc * 128: kc * D + (oc + 1) * 128]
                rhs = hqt[:, kc * TT:(kc + 1) * TT]
                p.op("tensor",
                     lambda e, o=pd, l=lhs, r=rhs, s=(kc == 0), z=(kc == ICN - 1):
                     e.matmul(o, l, r, start=s, stop=z),
                     sem="TE" if kc == ICN - 1 else None,
                     waits=([("VE", hq_ve), ("LW", wd_ld), (lx, x_ld),
                             ("VE", G["f2d_ve"][oc % 2])] if kc == 0 else []))
            te_d = p.c("TE")
            p.op("vector", lambda e, d=yout[:, oc * TT:(oc + 1) * TT],
                 a=pd, sc=cd_all[:, oc:oc + 1], b=xt[:, oc * TT:(oc + 1) * TT]:
                 e.scalar_tensor_tensor(d, a, sc, b, ALU.mult, ALU.add),
                 sem="VE", waits=[("TE", te_d)])
            G["f2d_ve"][oc % 2] = p.c("VE")
        G["y_ve"] = p.c("VE")

        def ln2_out(oc, ve_o, _t=t):
            st(y_t[oc * 128:(oc + 1) * 128, _t * TT:(_t + 1) * TT],
               ysq2[:, oc * TT:(oc + 1) * TT], "SO", waits=[("VE", ve_o)])
        ln_emit(yout, ysq2, ysq2, ln2_out, PS[6], PS[7])
        G["f2x"][t % 2] = p.c("VE")
        G["f2d_ve"] = {0: p.c("VE"), 1: p.c("VE")}
        G["f2hq_te"][t % 2] = p.c("TE")

    _finish(nc, p)
    return nc


def _finish(nc, p):
    p.wait("gpsimd", [(s, p.c(s)) for s in
                      ("SO", "SX", "SH0", "SH1", "SH2", "SH3", "SW", "SC", "ZG")])
    with nc.Block() as block:
        for eng in ENGS:
            def run(e, _ops=p.ops[eng]):
                for f in _ops:
                    f(e)
            getattr(block, eng)(run)


def host_consts():
    c = np.zeros((128, 13), dtype=np.float32)
    c[:, C_ONE] = 1.0
    c[:, C_HALFPI] = PI / 2
    c[:, C_MHALF] = -0.5
    c[:, C_EPS] = 1e-5
    c[:, C_LNTHR] = LN_THR
    c[:, C_PHALF] = 0.5
    c[:, C_MONE] = -1.0
    c[:, C_MLN16] = MLN16
    c[:, C_PI] = PI
    c[:, C_MPI] = -PI
    c[:, C_MHPI] = -PI / 2
    return c


def kernel(L0_attn_x, L0_attn_wq, L0_attn_wk, L0_attn_wv, L0_attn_wo, L0_attn_rf,
           L0_ffn_clip_x, L0_ffn_clip_down, L0_ffn_w_gate, L0_ffn_w_up,
           L0_ffn_w_down):
    x = np.ascontiguousarray(np.asarray(L0_attn_x, dtype=np.float32))
    wq = np.ascontiguousarray(np.asarray(L0_attn_wq, dtype=np.float32).T)
    wk = np.ascontiguousarray(np.asarray(L0_attn_wk, dtype=np.float32).T)
    wv = np.ascontiguousarray(np.asarray(L0_attn_wv, dtype=np.float32).T)
    wo = np.ascontiguousarray(np.asarray(L0_attn_wo, dtype=np.float32).T)
    rf = np.asarray(L0_attn_rf, dtype=np.float32)
    rf_rep = np.ascontiguousarray(np.concatenate([rf.T, rf.T], axis=0))
    rf_bd = np.zeros((128, 2 * MF), dtype=np.float32)
    rf_bd[0:64, 0:MF] = rf.T
    rf_bd[64:128, MF:2 * MF] = rf.T
    wg = np.ascontiguousarray(np.asarray(L0_ffn_w_gate, dtype=np.float32))
    wu = np.ascontiguousarray(np.asarray(L0_ffn_w_up, dtype=np.float32))
    wd = np.ascontiguousarray(np.asarray(L0_ffn_w_down, dtype=np.float32))
    clip2 = np.array([[float(np.asarray(L0_ffn_clip_x).reshape(-1)[0]),
                       float(np.asarray(L0_ffn_clip_down).reshape(-1)[0])]],
                     dtype=np.float32)
    maskh = np.zeros((128, KC * H), dtype=np.float32)
    for kc in range(KC):
        for pl in range(128):
            maskh[pl, kc * H + 2 * kc + pl // 64] = 1.0
    sel_all = np.zeros((128, KC * 128), dtype=np.float32)
    for oc in range(KC):
        for pl in range(128):
            sel_all[2 * oc + pl // 64, oc * 128 + pl] = 1.0

    shared = {
        "wq_t": wq, "wk_t": wk, "wv_t": wv, "wo_t": wo, "rf_rep": rf_rep,
        "rf_bd": rf_bd,
        "wg_n": wg, "wu_n": wu, "wd_n": wd,
        "wg_tt": np.ascontiguousarray(wg.T),
        "wu_tt": np.ascontiguousarray(wu.T),
        "wd_tt": np.ascontiguousarray(wd.T),
        "clip2": clip2, "consts": host_consts(),
        "onesrow": np.ones((1, 128), dtype=np.float32),
        "ident": np.eye(128, dtype=np.float32),
        "maskh": maskh, "sel_all": sel_all,
    }
    in_maps = []
    for b in range(8):
        m = dict(shared)
        m["x_t"] = np.ascontiguousarray(x[b].T)
        in_maps.append(m)

    nc = build_core_program()
    res = run_bass_kernel_spmd(nc, in_maps, core_ids=list(range(8)))
    return np.stack([np.ascontiguousarray(r["y_t"].T) for r in res.results], axis=0)


# version 15
# speedup vs baseline: 1.0082x; 1.0082x over previous
import math
import numpy as np
import concourse.bass as bass
import concourse.mybir as mybir
from concourse.bass_utils import run_bass_kernel_spmd

F32 = mybir.dt.float32
BF16 = mybir.dt.bfloat16
I32 = mybir.dt.int32
AF = mybir.ActivationFunctionType
ALU = mybir.AluOpType
AX = mybir.AxisListType

T = 4096
D = 1024
H = 16
DH = 64
MF = 256
M2 = 512
I = 2816
KC = D // 128
ICN = I // 128
TT = 512
NT = T // TT
TB = 128
NTB = TT // TB

PI = math.pi
INV127 = 1.0 / 127.0
LN_THR = math.log(1e-6 * math.sqrt(MF))
MLN16 = -math.log(math.sqrt(MF))

(C_ONE, C_HALFPI, C_MHALF, C_EPS, C_LNTHR, C_PHALF, C_ZERO, C_MONE,
 C_MLN16, C_SPARE, C_PI, C_MPI, C_MHPI) = range(13)

ENGS = ("sync", "tensor", "vector", "scalar", "gpsimd")


class Plan:
    def __init__(self, nc):
        self.nc = nc
        self.ops = {e: [] for e in ENGS}
        self.sems = {}
        self.cnt = {}
        self.mark = {e: {} for e in ENGS}
        self.g = {}

    def sem(self, name):
        s = self.nc.alloc_semaphore(name)
        self.sems[name] = s
        self.cnt[name] = 0

    def op(self, eng, f, sem=None, by=1, waits=()):
        if eng == "vector":
            waits = list(waits) + [("VE", self.cnt["VE"])]
        elif eng == "scalar":
            waits = list(waits) + [("SE", self.cnt["SE"])]
        real = []
        wm = self.mark[eng]
        for sn, val in waits:
            if val > 0 and wm.get(sn, 0) < val:
                wm[sn] = val
                real.append((self.sems[sn], val))
        semh = None
        if sem is not None:
            self.cnt[sem] += by
            semh = self.sems[sem]
        def fn(e, _w=real, _f=f, _s=semh, _b=by):
            for sh, v in _w:
                e.wait_ge(sh, v)
            ins = _f(e)
            if _s is not None:
                ins.then_inc(_s, _b)
        self.ops[eng].append(fn)

    def wait(self, eng, waits):
        wm = self.mark[eng]
        for sn, val in waits:
            if val > 0 and wm.get(sn, 0) < val:
                wm[sn] = val
                s = self.sems[sn]
                self.ops[eng].append(lambda e, _s=s, _v=val: e.wait_ge(_s, _v))

    def c(self, name):
        return self.cnt[name]

    def barrier_eng(self, eng):
        self.wait(eng, [(s, self.cnt[s]) for s in self.cnt
                        if s != "GE"])


def build_core_program(stop=None, debug=False, amask="nvpkt"):
    nc = bass.Bass("TRN2", target_bir_lowering=False)

    x_t = nc.dram_tensor("x_t", [D, T], F32, kind="ExternalInput").ap()
    wq_t = nc.dram_tensor("wq_t", [D, D], F32, kind="ExternalInput").ap()
    wk_t = nc.dram_tensor("wk_t", [D, D], F32, kind="ExternalInput").ap()
    wv_t = nc.dram_tensor("wv_t", [D, D], F32, kind="ExternalInput").ap()
    wo_t = nc.dram_tensor("wo_t", [D, D], F32, kind="ExternalInput").ap()
    rf_rep_i = nc.dram_tensor("rf_rep", [128, MF], F32, kind="ExternalInput").ap()
    rf_bd_i = nc.dram_tensor("rf_bd", [128, M2], F32, kind="ExternalInput").ap()
    wg_n = nc.dram_tensor("wg_n", [I, D], F32, kind="ExternalInput").ap()
    wu_n = nc.dram_tensor("wu_n", [I, D], F32, kind="ExternalInput").ap()
    wd_n = nc.dram_tensor("wd_n", [D, I], F32, kind="ExternalInput").ap()
    wg_tt = nc.dram_tensor("wg_tt", [D, I], F32, kind="ExternalInput").ap()
    wu_tt = nc.dram_tensor("wu_tt", [D, I], F32, kind="ExternalInput").ap()
    wd_tt = nc.dram_tensor("wd_tt", [I, D], F32, kind="ExternalInput").ap()
    clip2 = nc.dram_tensor("clip2", [1, 2], F32, kind="ExternalInput").ap()
    consts_i = nc.dram_tensor("consts", [128, 13], F32, kind="ExternalInput").ap()
    onesrow_i = nc.dram_tensor("onesrow", [1, 128], F32, kind="ExternalInput").ap()
    ident_i = nc.dram_tensor("ident", [128, 128], F32, kind="ExternalInput").ap()
    maskh_i = nc.dram_tensor("maskh", [128, KC * H], F32, kind="ExternalInput").ap()
    sel_i = nc.dram_tensor("sel_all", [128, KC * 128], F32, kind="ExternalInput").ap()

    y_t = nc.dram_tensor("y_t", [D, T], F32, kind="ExternalOutput").ap()
    dbg = (nc.dram_tensor("dbg", [128, 16384], F32, kind="ExternalOutput").ap()
           if debug else None)

    xln_s = nc.dram_tensor("xln_s", [D, T], F32).ap()
    h_s = nc.dram_tensor("h_s", [I, T], F32).ap()
    wdq_s = nc.dram_tensor("wdq_s", [I, D], BF16).ap()
    cc_in1 = nc.dram_tensor("cc_in1", [1, 8], F32)
    cc_out1 = nc.dram_tensor("cc_out1", [1, 8], F32)
    cc_in2 = nc.dram_tensor("cc_in2", [1, 8], F32)
    cc_out2 = nc.dram_tensor("cc_out2", [1, 8], F32)

    A = lambda *a: nc.alloc_sbuf_tensor(*a).ap()
    consts = A("consts_sb", [128, 13], F32)
    onesrow = A("onesrow_sb", [1, 128], F32)
    ident = A("ident_sb", [128, 128], F32)
    rf_rep = A("rf_rep_sb", [128, MF], F32)
    rf_bd = A("rf_bd_sb", [128, M2], F32)
    maskh = A("maskh_sb", [128, KC * H], F32)
    sel_all = A("sel_sb", [128, KC * 128], F32)
    clip_sb = A("clip_sb", [1, 2], F32)
    sig2 = A("sig2_sb", [1, 2], F32)
    scale_sb = A("scaleK_sb", [128, H], F32)
    zst = A("zst_sb", [128, TT], F32)
    zall = A("zall_sb", [128, TT], F32)
    thr_fm = A("thr_sb", [H, TT], F32)
    rsrow = A("rsrow", [1, TT], F32)
    nurow = A("nurow", [1, TT], F32)
    sc512 = A("sc512", [128, TT], F32)
    xmaxst = A("xmaxst", [128, NT * KC], F32)
    hmaxst = A("hmaxst", [128, NT * ICN], F32)
    sall_g = A("sall_g", [128, ICN], F32)
    sall_u = A("sall_u", [128, ICN], F32)
    sall_d = A("sall_d", [128, KC], F32)
    ax_all = A("ax_all", [128, ICN], F32)
    bu_all = A("bu_all", [128, ICN], F32)
    reff_all = A("reff_all", [128, ICN], F32)
    cd_all = A("cd_all", [128, KC], F32)
    rx_col = A("rx_col", [128, 1], F32)
    sxs_col = A("sxs_col", [128, 1], F32)
    rh_col = A("rh_col", [128, 1], F32)
    shs_col = A("shs_col", [128, 1], F32)
    r1x1 = A("r1x1", [1, 1], F32)
    r1x1b = A("r1x1b", [1, 1], F32)
    gmax1 = A("gmax1", [1, 1], F32)
    gmax2 = A("gmax2", [1, 1], F32)
    ccst = A("ccst", [1, 8], F32)
    tp22 = A("tp22", [ICN, 128], F32)
    mred = A("mred", [128, ICN], F32)
    m1col = A("m1col", [128, 1], F32)

    PS = [nc.alloc_psum_tensor(f"ps{i}", [128, 512], F32).ap() for i in range(8)]

    BASE = 40960
    TOP = 229344

    def place(name, shape, dtype, off_kb):
        off = int(off_kb * 1024)
        nbytes = int(np.prod(shape[1:])) * mybir.dt.size(dtype)
        assert BASE + off + nbytes <= TOP, (name, off_kb, nbytes)
        return nc.alloc_sbuf_tensor_at(name, shape, dtype, offset=BASE + off).ap()

    wk_sb = place("wk_sb", [128, KC * D], F32, 0)
    wv_sb = place("wv_sb", [128, KC * D], F32, 32)
    kvacc = place("kvacc", [128, H * 512], F32, 64)
    xa0 = place("xa0", [128, KC * TT], F32, 96)
    ktile = place("ktile", [128, KC * TT], F32, 128)
    ksq = place("ksq", [128, KC * TT], F32, 144)
    kfps = [place("kfp0", [128, 2 * M2], F32, 144),
            place("kfp1", [128, 2 * M2], F32, 112),
            place("kfp2", [128, 2 * M2], F32, 116),
            place("kfp3", [128, 2 * M2], F32, 120)]
    pti = place("pti", [128, M2], I32, 148)
    actc = place("actc", [128, M2], F32, 148)
    ap0 = [place("ap0_0", [128, M2], F32, 150),
           place("ap0_1", [128, M2], F32, 152),
           place("ap0_2", [128, M2], F32, 124),
           place("ap0_3", [128, M2], F32, 126)]
    au = place("au", [128, M2], F32, 154)
    as1 = [place("as1_0", [128, M2], F32, 156),
           place("as1_1", [128, M2], F32, 177)]
    asgn = place("asgn", [128, M2], F32, 158)
    v65a = place("v65a", [128, NTB * H * 65], F32, 160)
    scale_a = place("scale_a", [128, NTB * H], F32, 176.5)
    kvfm = place("kvfm", [128, H * 4 * 65], F32, 96)

    wq_sb = place("wq_sb", [128, KC * D], F32, 0)
    wo_sb = place("wo_sb", [128, KC * D], F32, 32)
    xb0 = place("xb0", [128, KC * TT], F32, 64)
    qtiles = [place("qtileA", [128, KC * TT], F32, 113),
              place("qtileB", [128, KC * TT], F32, 80)]
    qsq = place("qsq", [128, KC * TT], F32, 129)
    attn = place("attn", [128, KC * TT], F32, 145)
    qf0 = place("qf0", [128, 4 * TT], F32, 161)
    qf1 = place("qf1", [128, 4 * TT], F32, 169)
    bp0 = [place(f"bp0_{m}", [128, M2], F32, 129 + 2 * m) for m in range(2)]
    bu = [place(f"bu_{m}", [128, M2], F32, 133 + 2 * m) for m in range(2)]
    bsgn = [place(f"bsgn_{m}", [128, M2], F32, 137 + 2 * m) for m in range(2)]
    bs1 = [place(f"bs1_{m}", [128, M2], F32, 141 + 2 * m) for m in range(2)]
    ptbi = place("ptbi", [128, 512], I32, 177)
    actcB = place("actcB", [128, 512], F32, 177)
    ba2 = [place(f"ba2_{m}", [128, M2], F32, 179 + 2 * m) for m in range(2)]

    wgq = place("wgq", [128, KC * I], BF16, 0)
    wuq = place("wuq", [128, KC * I], BF16, 44)
    wdq = place("wdq", [128, ICN * D], BF16, 88)
    rb = place("rb", [128, I], F32, 132)
    wtc0 = place("wtc0", [128, I], F32, 143)
    avq = place("avq", [128, I], F32, 154)
    sgnq = place("sgnq", [128, I], BF16, 165)
    t2q = place("t2q", [128, I], BF16, 171)
    natd2 = place("natd2", [128, I], F32, 165)
    natd3 = place("natd3", [128, I], F32, 132)

    xl0 = place("xl0", [128, KC * TT], F32, 88)
    xl1 = place("xl1", [128, KC * TT], F32, 104)
    xqb2 = [place("xqb0", [128, KC * TT], BF16, 120),
            place("xqb1", [128, KC * TT], BF16, 172)]
    hb = [place(f"hb{i}", [128, TT], F32, 128 + 2 * i) for i in range(4)]
    silu0 = place("silu0", [128, TT], F32, 136)
    silu1 = place("silu1", [128, TT], F32, 138)
    avf = place("avf", [128, KC * TT], F32, 140)
    sgnf = place("sgnf", [128, KC * TT], BF16, 156)
    t2f = place("t2f", [128, KC * TT], BF16, 164)

    wdq2 = place("wdq2", [128, ICN * D], BF16, 0)
    hq0 = place("hq0", [128, ICN * TT], BF16, 44)
    hq1 = place("hq1", [128, ICN * TT], BF16, 66)
    hl = [place(f"hl{i}", [128, TT], F32, 88 + 2 * i) for i in range(4)]
    xl20 = place("xl20", [128, KC * TT], F32, 96)
    xl21 = place("xl21", [128, KC * TT], F32, 112)
    yout = place("yout", [128, KC * TT], F32, 128)
    ysq2 = place("ysq2", [128, KC * TT], F32, 144)
    avf2 = [place(f"avf2_{i}", [128, TT], F32, 160 + 4 * i) for i in range(2)]
    sgnf2 = [place(f"sgnf2_{i}", [128, TT], BF16, 162 + 4 * i) for i in range(2)]
    t2f2 = [place(f"t2f2_{i}", [128, TT], BF16, 163 + 4 * i) for i in range(2)]

    p = Plan(nc)
    for s in ["LC", "LW", "LX0", "LX1", "LQ", "LH0", "LH1", "LH2", "LH3",
              "SX", "SH0", "SH1", "SH2", "SH3", "SW", "SC", "SO",
              "ZG", "TE", "VE", "SE", "GE", "GP"]:
        p.sem(s)
    G = p.g

    one = consts[:, C_ONE:C_ONE + 1]
    czero = consts[:, C_ZERO:C_ZERO + 1]

    def ld(dst, src, stream, waits=()):
        p.op("sync", lambda e, d=dst, s=src: e.dma_start(out=d, in_=s),
             sem=stream, by=16, waits=waits)

    def st(dst, src, stream, waits=()):
        p.op("gpsimd", lambda e, d=dst, s=src:
             e.dma_start(out=d, in_=s, allow_slow_non_contiguous=True),
             sem=stream, by=16, waits=waits)

    def ln_emit(ytile, ysqtile, outtile, outcb, psA, psB):
        p.op("scalar", lambda e, d=ysqtile, s=ytile:
             e.activation(d, s, AF.Square, bias=czero, scale=one),
             sem="SE", waits=[("VE", G["y_ve"])])
        ysq_se = p.c("SE")
        for kc in range(KC):
            p.op("tensor", lambda e, o=psA[0:1, :], l=one,
                 r=ytile[:, kc * TT:(kc + 1) * TT], s=(kc == 0), z=(kc == KC - 1):
                 e.matmul(o, l, r, start=s, stop=z),
                 sem="TE" if kc == KC - 1 else None,
                 waits=([("VE", G["y_ve"]), ("VE", G.get("ln_ps_free", 0)),
                         ("SE", G.get("ln_ps_free_se", 0))] if kc == 0 else []))
        for kc in range(KC):
            p.op("tensor", lambda e, o=psB[0:1, :], l=one,
                 r=ysqtile[:, kc * TT:(kc + 1) * TT], s=(kc == 0), z=(kc == KC - 1):
                 e.matmul(o, l, r, start=s, stop=z),
                 sem="TE" if kc == KC - 1 else None,
                 waits=([("SE", ysq_se)] if kc == 0 else []))
        te_stats = p.c("TE")
        p.op("vector", lambda e: e.tensor_scalar(zst[0:1, :], psA[0:1, :],
                                                 1.0 / D, None, ALU.mult),
             sem="VE", waits=[("TE", te_stats)])
        p.op("vector", lambda e: e.tensor_tensor(rsrow, zst[0:1, :], zst[0:1, :],
                                                 ALU.mult), sem="VE")
        p.op("vector", lambda e: e.scalar_tensor_tensor(
            rsrow, psB[0:1, :], 1.0 / D, rsrow, ALU.mult, ALU.subtract), sem="VE")
        ve_var = p.c("VE")
        p.op("scalar", lambda e: e.activation(nurow, rsrow, AF.Ln,
                                              bias=consts[0:1, C_EPS:C_EPS + 1],
                                              scale=one[0:1]),
             sem="SE", waits=[("VE", ve_var)])
        p.op("scalar", lambda e: e.activation(rsrow, nurow, AF.Exp,
                                              bias=czero[0:1],
                                              scale=consts[0:1, C_MHALF:C_MHALF + 1]),
             sem="SE")
        se_rs = p.c("SE")
        p.op("vector", lambda e: e.tensor_tensor(nurow, zst[0:1, :], rsrow, ALU.mult),
             sem="VE", waits=[("SE", se_rs)])
        ve_nu = p.c("VE")
        p.op("tensor", lambda e, o=psA: e.matmul(o, onesrow, rsrow,
                                                 start=True, stop=True),
             sem="TE", waits=[("VE", ve_nu)])
        p.op("tensor", lambda e, o=psB: e.matmul(o, onesrow, nurow,
                                                 start=True, stop=True), sem="TE")
        te_bc = p.c("TE")
        for oc in range(KC):
            p.op("vector", lambda e, d=sc512, a=ytile[:, oc * TT:(oc + 1) * TT], b=psA:
                 e.tensor_tensor(d, a, b, ALU.mult),
                 sem="VE", waits=[("TE", te_bc)])
            p.op("vector", lambda e, d=outtile[:, oc * TT:(oc + 1) * TT],
                 a=sc512, b=psB:
                 e.tensor_tensor(d, a, b, ALU.subtract), sem="VE")
            outcb(oc, p.c("VE"))
        G["ln_ps_free"] = p.c("VE")
        G["ln_ps_free_se"] = p.c("SE")

    for dst, src in [(consts, consts_i), (onesrow, onesrow_i), (ident, ident_i),
                     (rf_rep, rf_rep_i), (rf_bd, rf_bd_i),
                     (maskh, maskh_i), (sel_all, sel_i),
                     (clip_sb, clip2)]:
        ld(dst, src, "LC")
    setup_ld = p.c("LC")
    p.op("scalar", lambda e: e.activation(sig2, clip_sb, AF.Sigmoid,
                                          bias=czero[0:1], scale=one[0:1]),
         sem="SE", waits=[("LC", setup_ld)])
    p.op("vector", lambda e: e.memset(kvacc, 0.0), sem="VE")
    p.op("vector", lambda e: e.memset(zall, 0.0), sem="VE")
    kv_zero_ve = p.c("VE")

    for kc in range(KC):
        ld(wk_sb[:, kc * D:(kc + 1) * D], wk_t[kc * 128:(kc + 1) * 128, :], "LW")
        ld(wv_sb[:, kc * D:(kc + 1) * D], wv_t[kc * 128:(kc + 1) * 128, :], "LW")
    a_w_ld = p.c("LW")

    if stop == "A0":
        _finish(nc, p)
        return nc

    G["xa"] = {0: 0, 1: 0}
    G["ak_ve"] = {0: 0, 1: 0}
    G["ak_se"] = {0: 0, 1: 0}
    G["av_ve"] = {0: 0, 1: 0}
    G["akf_te"] = {0: 0, 1: 0, 2: 0, 3: 0}
    G["aphi"] = {0: 0, 1: 0, 2: 0, 3: 0}
    G["apt2"] = {0: 0, 1: 0, 2: 0, 3: 0}
    G["axa_te"] = 0
    G["asgn_gp"] = 0
    G["as1_gp"] = {0: 0, 1: 0}
    G["akvp"] = {0: 0, 1: 0}
    G["av65_prev"] = 0
    G["anorm_se"] = 0
    G["a_pre_se"] = 0

    for t in range(NT if stop != "A1" else 1):
        xt = xa0
        lx = "LX%d" % (t % 2)
        for kc in range(KC):
            ld(xt[:, kc * TT:(kc + 1) * TT],
               x_t[kc * 128:(kc + 1) * 128, t * TT:(t + 1) * TT], lx,
               waits=[("TE", G["axa_te"])])
        x_ld = p.c(lx)

        for oc in range(KC):
            pk = PS[oc % 2]
            for kc in range(KC):
                lhs = wk_sb[:, kc * D + oc * 128: kc * D + (oc + 1) * 128]
                rhs = xt[:, kc * TT:(kc + 1) * TT]
                p.op("tensor",
                     lambda e, o=pk, l=lhs, r=rhs, s=(kc == 0), z=(kc == KC - 1):
                     e.matmul(o, l, r, start=s, stop=z),
                     sem="TE" if kc == KC - 1 else None,
                     waits=([(lx, x_ld), ("LW", a_w_ld),
                             ("VE", G["ak_ve"][oc % 2]),
                             ("SE", G["ak_se"][oc % 2])] if kc == 0 else []))
            te_k = p.c("TE")
            p.op("vector", lambda e, d=ktile[:, oc * TT:(oc + 1) * TT], s=pk:
                 e.tensor_copy(d, s), sem="VE", waits=[("TE", te_k)])
            ve_kcp = p.c("VE")
            p.op("scalar", lambda e, d=ksq[:, oc * TT:(oc + 1) * TT],
                 s=ktile[:, oc * TT:(oc + 1) * TT]:
                 e.activation(d, s, AF.Square, bias=czero, scale=one),
                 sem="SE", waits=[("VE", ve_kcp)])
            G["ak_ve"][oc % 2] = p.c("VE")
            G["ak_se"][oc % 2] = p.c("SE")
        k_evac_ve = p.c("VE")
        k_evac_se = p.c("SE")

        for tb in range(NTB):
            if "n" not in amask:
                continue
            pn = PS[6]
            for kc in range(KC):
                lhs = ksq[:, kc * TT + tb * TB: kc * TT + (tb + 1) * TB]
                rhs = maskh[:, kc * H:(kc + 1) * H]
                p.op("tensor",
                     lambda e, o=pn[:, 0:H], l=lhs, r=rhs, s=(kc == 0), z=(kc == KC - 1):
                     e.matmul(o, l, r, start=s, stop=z),
                     sem="TE" if kc == KC - 1 else None,
                     waits=([("SE", k_evac_se), ("SE", G["anorm_se"]),
                             ("VE", G["akvp"][1])]
                            if kc == 0 else []))
            te_n = p.c("TE")
            p.op("scalar", lambda e, d=scale_a[:, tb * H:(tb + 1) * H], s=pn[:, 0:H]:
                 e.activation(d, s, AF.Exp,
                              bias=consts[:, C_MLN16:C_MLN16 + 1],
                              scale=consts[:, C_MHALF:C_MHALF + 1]),
                 sem="SE", waits=[("TE", te_n), ("VE", G["av65_prev"])])
            G["anorm_se"] = p.c("SE")
            se_scale = p.c("SE")

            if "v" not in amask:
                continue
            vb = v65a[:, tb * H * 65:(tb + 1) * H * 65]
            for nh in range(2):
                pv = PS[nh]
                for kc in range(KC):
                    lhs = xt[:, kc * TT + tb * TB: kc * TT + (tb + 1) * TB]
                    rhs = wv_sb[:, kc * D + nh * 512: kc * D + (nh + 1) * 512]
                    p.op("tensor",
                         lambda e, o=pv, l=lhs, r=rhs, s=(kc == 0), z=(kc == KC - 1):
                         e.matmul(o, l, r, start=s, stop=z),
                         sem="TE" if kc == KC - 1 else None,
                         waits=([("VE", G["av_ve"][nh]), ("VE", k_evac_ve),
                                 ("SE", k_evac_se)] if kc == 0 else []))
                te_v = p.c("TE")
                for hh in range(8):
                    h = nh * 8 + hh
                    p.op("vector",
                         lambda e, d=vb[:, h * 65: h * 65 + 64],
                         s=pv[:, hh * 64:(hh + 1) * 64],
                         sc=scale_a[:, tb * H + h:tb * H + h + 1]:
                         e.tensor_scalar(d, s, sc, None, ALU.mult),
                         sem="VE", waits=[("TE", te_v), ("SE", se_scale)])
                G["av_ve"][nh] = p.c("VE")
            p.op("vector",
                 lambda e, d=vb.rearrange("p (h c) -> p h c", c=65)[:, :, 64:65],
                 s=scale_a[:, tb * H:(tb + 1) * H].rearrange(
                     "p (h o) -> p h o", o=1):
                 e.tensor_copy(d, s), sem="VE")
        v65_ve = p.c("VE")
        G["a_pre_se"] = p.c("SE")
        G["axa_te"] = p.c("TE")

        if "v" in amask and "n" in amask:
            items = [(hp, tb) for hp in range(8) for tb in range(NTB)]
            te_phi = {}
            se_cos = {}
            gp_sin = {}
            te_stop = {}

            def emit_phi(i):
                hp, tb = items[i]
                par = i % 4
                pph = PS[(2, 3, 0, 1)[par]]
                lhs = ktile[:, hp * TT + tb * TB: hp * TT + (tb + 1) * TB]
                p.op("tensor", lambda e, o=pph, l=lhs:
                     e.matmul(o, l, rf_bd, start=True, stop=True),
                     sem="TE",
                     waits=[("VE", G["aphi"][par]), ("VE", k_evac_ve),
                            ("VE", v65_ve)])
                te_phi[i] = p.c("TE")

            def emit_epilogue(i):
                par = i % 4
                pph = PS[(2, 3, 0, 1)[par]]
                kfb = kfps[par]
                p0m = ap0[par]
                s1m = as1[i % 2]
                p.op("vector", lambda e, d=pti, s=pph:
                     e.tensor_scalar(d, s, 1.0 / (2 * PI), None, ALU.mult),
                     sem="VE", waits=[("TE", te_phi[i])])
                p.op("vector", lambda e, d=p0m, a=pti, s=pph:
                     e.scalar_tensor_tensor(d, a, -2 * PI, s, ALU.mult, ALU.add),
                     sem="VE", waits=[("SE", G["apt2"][par])])
                G["aphi"][par] = p.c("VE")
                ve_p0 = p.c("VE")
                p.op("vector", lambda e, d=actc, s=p0m:
                     e.tensor_scalar(d, s, 0.0, None, ALU.is_ge), sem="VE")
                p.op("vector", lambda e, d=asgn, s=actc:
                     e.tensor_scalar(d, s, 2.0, -1.0, ALU.mult, ALU.add),
                     sem="VE", waits=[("GP", G["asgn_gp"])])
                ve_sgn = p.c("VE")
                p.op("scalar", lambda e, d=au, s=p0m:
                     e.activation(d, s, AF.Abs, bias=czero, scale=one),
                     sem="SE", waits=[("VE", ve_p0)])
                p.op("scalar", lambda e, d=s1m, s=au:
                     e.activation(d, s, AF.Sin, bias=consts[:, C_PI:C_PI + 1],
                                  scale=consts[:, C_MONE:C_MONE + 1]),
                     sem="SE", waits=[("GP", G["as1_gp"][i % 2])])
                se_s1 = p.c("SE")
                gp_h = []
                se_c = []

                def gp_half(hh):
                    p.op("gpsimd", lambda e,
                         d=kfb[:, hh * M2 + MF:(hh + 1) * M2],
                         a=asgn[:, hh * MF:(hh + 1) * MF],
                         b=s1m[:, hh * MF:(hh + 1) * MF]:
                         e.tensor_tensor(d, a, b, ALU.mult), sem="GP",
                         waits=[("SE", se_s1), ("VE", ve_sgn),
                                ("GP", p.c("GP")), ("TE", G["akf_te"][par])])
                    gp_h.append(p.c("GP"))

                def cos_half(hh):
                    p.op("scalar", lambda e,
                         d=kfb[:, hh * M2:hh * M2 + MF],
                         s=p0m[:, hh * MF:(hh + 1) * MF]:
                         e.activation(d, s, AF.Sin,
                                      bias=consts[:, C_MHPI:C_MHPI + 1],
                                      scale=one),
                         sem="SE", waits=[("TE", G["akf_te"][par])])
                    se_c.append(p.c("SE"))

                gp_half(0)
                p.op("scalar", lambda e, d=p0m, s=au:
                     e.activation(d, s, AF.Abs, bias=consts[:, C_MPI:C_MPI + 1],
                                  scale=one),
                     sem="SE", waits=[("VE", ve_sgn)])
                cos_half(0)
                gp_half(1)
                cos_half(1)
                gp_sin[i] = tuple(gp_h)
                se_cos[i] = tuple(se_c)
                G["asgn_gp"] = gp_h[1]
                G["as1_gp"][i % 2] = gp_h[1]
                G["apt2"][par] = se_c[1]

            def emit_kv(i):
                hp, tb = items[i]
                par = i % 4
                kfb = kfps[par]
                bank = 4 + 2 * (hp % 2)
                for j in range(2):
                    h = 2 * hp + j
                    pk = PS[bank + j]
                    w = [("SE", se_cos[i][j]), ("GP", gp_sin[i][j])]
                    if tb == 0:
                        w += [("VE", G["akvp"][hp % 2]), ("VE", v65_ve),
                              ("SE", G["a_pre_se"])]
                    p.op("tensor", lambda e, o=pk[0:65, :],
                         l=v65a[:, (tb * H + h) * 65:(tb * H + h) * 65 + 65],
                         r=kfb[:, j * M2:(j + 1) * M2]:
                         e.matmul(o, l, r, start=(tb == 0), stop=(tb == NTB - 1)),
                         sem="TE" if j == 1 else None, waits=w)
                G["akf_te"][par] = p.c("TE")
                if tb == NTB - 1:
                    te_stop[hp] = p.c("TE")

            def emit_flush(hp):
                bank = 4 + 2 * (hp % 2)
                for j in range(2):
                    h = 2 * hp + j
                    p.op("vector", lambda e, d=kvacc[0:65, h * 512:(h + 1) * 512],
                         s=PS[bank + j][0:65, :]:
                         e.tensor_tensor(d, d, s, ALU.add),
                         sem="VE", waits=[("TE", te_stop[hp]), ("VE", kv_zero_ve)])
                G["akvp"][hp % 2] = p.c("VE")

            emit_phi(0)
            emit_phi(1)
            pend_flush = None
            for i, (hp, tb) in enumerate(items):
                if i + 2 < len(items):
                    emit_phi(i + 2)
                emit_epilogue(i)
                if pend_flush is not None:
                    emit_flush(pend_flush)
                    pend_flush = None
                emit_kv(i)
                if tb == NTB - 1:
                    pend_flush = hp
            if pend_flush is not None:
                emit_flush(pend_flush)
        G["xa"][t % 2] = p.c("TE")
        G["ak_ve"] = {0: p.c("VE"), 1: p.c("VE")}
        G["ak_se"] = {0: p.c("SE"), 1: p.c("SE")}
        G["av_ve"] = {0: p.c("VE"), 1: p.c("VE")}
        G["av65_prev"] = p.c("VE")

    p.op("vector", lambda e: e.memset(kvfm, 0.0), sem="VE",
         waits=[("TE", p.c("TE"))])
    for h in (range(H) if "t" in amask else []):
        for fc in range(4):
            pt_ps = PS[6]
            p.op("tensor", lambda e, o=pt_ps[:, 0:65],
                 i=kvacc[0:65, h * 512 + fc * 128: h * 512 + (fc + 1) * 128]:
                 e.transpose(o, i, ident[0:65, 0:65]),
                 sem="TE", waits=[("VE", p.c("VE")), ("SE", p.c("SE"))])
            te_t = p.c("TE")
            col = (h * 4 + fc) * 65
            if h % 2 == 0:
                p.op("vector", lambda e, d=kvfm[:, col:col + 64], s=pt_ps[:, 0:64]:
                     e.tensor_copy(d, s), sem="VE", waits=[("TE", te_t)])
                p.op("vector", lambda e, d=kvfm[:, col + 64:col + 65],
                     s=pt_ps[:, 64:65]:
                     e.tensor_copy(d, s), sem="VE")
            else:
                p.op("vector", lambda e, d=kvfm[:, col:col + 1],
                     s=pt_ps[:, 64:65]:
                     e.tensor_copy(d, s), sem="VE", waits=[("TE", te_t)])
                p.op("vector", lambda e, d=kvfm[:, col + 1:col + 65],
                     s=pt_ps[:, 0:64]:
                     e.tensor_copy(d, s), sem="VE")
    kvfm_ve = p.c("VE")

    if stop in ("A", "A1"):
        if debug:
            st(dbg[:, 0:H * 512], kvacc, "SO", waits=[("VE", kvfm_ve)])
            st(dbg[:, 8192:8192 + H * 4 * 65], kvfm, "SO", waits=[("VE", kvfm_ve)])
            st(dbg[:, 14336:14336 + 2048], ktile[:, 0:2048], "SO",
               waits=[("VE", kvfm_ve)])
        _finish(nc, p)
        return nc

    p.barrier_eng("sync")
    for kc in range(KC):
        ld(wq_sb[:, kc * D:(kc + 1) * D], wq_t[kc * 128:(kc + 1) * 128, :], "LW")
        ld(wo_sb[:, kc * D:(kc + 1) * D], wo_t[kc * 128:(kc + 1) * 128, :], "LW")
    b_w_ld = p.c("LW")

    G["xb"] = {0: 0, 1: 0}
    G["bxb"] = 0
    G["bqt_ve"] = {0: 0, 1: 0}
    G["bq_ve"] = {0: 0, 1: 0}
    G["bq_se"] = {0: 0, 1: 0}
    G["bqf_te"] = {0: 0, 1: 0}
    G["bqkv_ve"] = {0: 0, 1: 0}
    G["bphiB"] = {2: 0, 3: 0, 6: 0, 7: 0}
    G["bp0"] = {0: 0, 1: 0}
    G["bsgn_gp"] = {0: 0, 1: 0}
    G["bz_zg"] = {}
    G["bn_ve"] = 0
    G["brz_ve"] = 0
    G["bw_ve"] = {0: 0, 1: 0}
    G["y_ve"] = 0

    for t in range(NT):
        xt = xb0
        qtile = qtiles[t % 2]
        lx = "LX%d" % (t % 2)
        for kc in range(KC):
            ld(xt[:, kc * TT:(kc + 1) * TT],
               x_t[kc * 128:(kc + 1) * 128, t * TT:(t + 1) * TT], lx,
               waits=[("VE", kvfm_ve), ("VE", G["bxb"])])
        x_ld = p.c(lx)

        for oc in range(KC):
            pq = PS[oc % 2]
            for kc in range(KC):
                lhs = wq_sb[:, kc * D + oc * 128: kc * D + (oc + 1) * 128]
                rhs = xt[:, kc * TT:(kc + 1) * TT]
                p.op("tensor",
                     lambda e, o=pq, l=lhs, r=rhs, s=(kc == 0), z=(kc == KC - 1):
                     e.matmul(o, l, r, start=s, stop=z),
                     sem="TE" if kc == KC - 1 else None,
                     waits=([(lx, x_ld), ("LW", b_w_ld),
                             ("VE", G["bq_ve"][oc % 2]),
                             ("SE", G["bq_se"][oc % 2])] if kc == 0 else []))
            te_q = p.c("TE")
            p.op("vector", lambda e, d=qtile[:, oc * TT:(oc + 1) * TT], s=pq:
                 e.tensor_copy(d, s), sem="VE",
                 waits=[("TE", te_q), ("VE", G["bqt_ve"][t % 2])])
            ve_qcp = p.c("VE")
            p.op("scalar", lambda e, d=qsq[:, oc * TT:(oc + 1) * TT],
                 s=qtile[:, oc * TT:(oc + 1) * TT]:
                 e.activation(d, s, AF.Square, bias=czero, scale=one),
                 sem="SE", waits=[("VE", ve_qcp), ("SX", G.get("sx_prev", 0))])
            G["bq_ve"][oc % 2] = p.c("VE")
            G["bq_se"][oc % 2] = p.c("SE")
        q_evac_ve = p.c("VE")
        q_evac_se = p.c("SE")

        for tb in range(NTB):
            pn = PS[6]
            for kc in range(KC):
                lhs = qsq[:, kc * TT + tb * TB: kc * TT + (tb + 1) * TB]
                rhs = maskh[:, kc * H:(kc + 1) * H]
                p.op("tensor",
                     lambda e, o=pn[:, 0:H], l=lhs, r=rhs, s=(kc == 0), z=(kc == KC - 1):
                     e.matmul(o, l, r, start=s, stop=z),
                     sem="TE" if kc == KC - 1 else None,
                     waits=([("SE", q_evac_se), ("VE", G["bn_ve"]),
                             ("SE", G.get("bthr_se", 0))]
                            if kc == 0 else []))
            te_n = p.c("TE")
            p.op("vector", lambda e, d=sc512[:, 0:H], s=pn[:, 0:H]:
                 e.tensor_copy(d, s), sem="VE", waits=[("TE", te_n)])
            ve_nq = p.c("VE")
            p.op("tensor", lambda e, o=pn[0:H, 0:TB], i=sc512[:, 0:H]:
                 e.transpose(o, i, ident), sem="TE", waits=[("VE", ve_nq)])
            te_tr = p.c("TE")
            p.op("vector", lambda e, d=sc512[0:H, 0:TB], s=pn[0:H, 0:TB]:
                 e.tensor_scalar(d, s, 0.5, LN_THR, ALU.mult, ALU.add),
                 sem="VE", waits=[("TE", te_tr)])
            p.op("vector", lambda e, d=sc512[0:H, 0:TB]:
                 e.tensor_scalar(d, d, 80.0, None, ALU.min), sem="VE")
            ve_thr = p.c("VE")
            p.op("scalar", lambda e, d=thr_fm[:, tb * TB:(tb + 1) * TB],
                 s=sc512[0:H, 0:TB]:
                 e.activation(d, s, AF.Exp, bias=czero[0:H], scale=one[0:H]),
                 sem="SE", waits=[("VE", ve_thr), ("TE", te_tr)])
            G["bthr_se"] = p.c("SE")
            G["bn_ve"] = p.c("VE")
        thr_se = p.c("SE")

        def b_flush(pend):
            h_, te_qkv_, oc_ = pend
            even_ = (h_ % 2 == 0)
            pk_ = PS[4] if even_ else PS[5]
            if even_:
                pay = pk_[0:64, :]; zrow = pk_[64:65, :]
                drow = attn[0:64, oc_ * TT:(oc_ + 1) * TT]
                zdst = zst[64:65, 0:TT]
            else:
                pay = pk_[64:128, :]; zrow = pk_[0:1, :]
                drow = attn[64:128, oc_ * TT:(oc_ + 1) * TT]
                zdst = zst[0:1, 0:TT]
            p.op("vector", lambda e, d=drow, s=pay: e.tensor_copy(d, s),
                 sem="VE", waits=[("TE", te_qkv_)])
            p.op("vector", lambda e, d=zdst, s=zrow: e.tensor_copy(d, s),
                 sem="VE", waits=[("ZG", G["bz_zg"].get(h_ // 2, 0))])
            ve_z = p.c("VE")
            p.op("scalar", lambda e, d=zall[h_:h_ + 1, :], s=zdst:
                 e.dma_start(out=d, in_=s), sem="ZG", by=16,
                 waits=[("VE", ve_z)])
            G["bqkv_ve"][h_ % 2] = p.c("VE")
            if h_ % 2 == 1:
                G["bz_zg"][h_ // 2 + 1] = p.c("ZG")
        te_phiH = {}
        se_cosH = {}
        gp_sinH = {}

        def b_phi(h):
            oc_, hi_ = h // 2, h % 2
            base = 0 if hi_ == 0 else 64
            banks = (2, 3) if h % 2 == 0 else (6, 7)
            tp = (0, 0) if hi_ == 0 else (64, 0)
            tps = []
            for mc in range(2):
                pps = PS[banks[mc]]
                lhs = rf_rep[base:base + 64, mc * 128:(mc + 1) * 128]
                rhs = qtile[base:base + 64, oc_ * TT:(oc_ + 1) * TT]
                p.op("tensor", lambda e, o=pps, l=lhs, r=rhs, tp_=tp:
                     e.matmul(o, l, r, start=True, stop=True, tile_position=tp_),
                     sem="TE",
                     waits=[("VE", G["bphiB"][banks[mc]]), ("VE", q_evac_ve)])
                tps.append(p.c("TE"))
            te_phiH[h] = tps

        def b_epi(h):
            qf = qf0 if h % 2 == 0 else qf1
            banks = (2, 3) if h % 2 == 0 else (6, 7)
            for mc in range(2):
                pps = PS[banks[mc]]
                p0m, um, sgnm = bp0[mc], bu[mc], bsgn[mc]
                s1m, a2m = bs1[mc], ba2[mc]
                p.op("vector", lambda e, d=ptbi, s=pps:
                     e.tensor_scalar(d, s, 1.0 / (2 * PI), None, ALU.mult),
                     sem="VE", waits=[("TE", te_phiH[h][mc])])
                p.op("vector", lambda e, d=p0m, a=ptbi, s=pps:
                     e.scalar_tensor_tensor(d, a, -2 * PI, s, ALU.mult, ALU.add),
                     sem="VE", waits=[("SE", G["bp0"][mc])])
                G["bphiB"][banks[mc]] = p.c("VE")
                ve_p0 = p.c("VE")
                p.op("vector", lambda e, d=actcB, s=p0m:
                     e.tensor_scalar(d, s, 0.0, None, ALU.is_ge), sem="VE")
                p.op("vector", lambda e, d=sgnm, s=actcB:
                     e.tensor_scalar(d, s, 2.0, -1.0, ALU.mult, ALU.add),
                     sem="VE", waits=[("GP", G["bsgn_gp"][mc])])
                ve_sgn = p.c("VE")
                p.op("scalar", lambda e, d=um, s=p0m:
                     e.activation(d, s, AF.Abs, bias=czero, scale=one),
                     sem="SE", waits=[("VE", ve_p0)])
                G["bp0"][mc] = p.c("SE")
                p.op("scalar", lambda e, d=s1m, s=um:
                     e.activation(d, s, AF.Sin, bias=consts[:, C_PI:C_PI + 1],
                                  scale=consts[:, C_MONE:C_MONE + 1]),
                     sem="SE", waits=[("GP", G["bsgn_gp"][mc])])
                se_s1 = p.c("SE")
                p.op("gpsimd", lambda e, d=qf[:, (2 + mc) * TT:(3 + mc) * TT],
                     a=sgnm, b=s1m:
                     e.tensor_tensor(d, a, b, ALU.mult), sem="GP",
                     waits=[("SE", se_s1), ("VE", ve_sgn), ("GP", p.c("GP")),
                            ("TE", G["bqf_te"][h % 2])])
                G["bsgn_gp"][mc] = p.c("GP")
                p.op("scalar", lambda e, d=a2m, s=um:
                     e.activation(d, s, AF.Abs, bias=consts[:, C_MPI:C_MPI + 1],
                                  scale=one), sem="SE")
                p.op("scalar", lambda e, d=qf[:, mc * TT:(mc + 1) * TT], s=a2m:
                     e.activation(d, s, AF.Sin, bias=consts[:, C_MHPI:C_MHPI + 1],
                                  scale=one),
                     sem="SE", waits=[("TE", G["bqf_te"][h % 2])])
            se_cosH[h] = p.c("SE")
            gp_sinH[h] = p.c("GP")

        def b_qkv(h):
            qf = qf0 if h % 2 == 0 else qf1
            even = (h % 2 == 0)
            pk = PS[4] if even else PS[5]
            w0 = [("SE", se_cosH[h]), ("GP", gp_sinH[h]),
                  ("VE", G["bqkv_ve"][h % 2])]
            if even:
                for fc in range(4):
                    lhs = kvfm[:, (h * 4 + fc) * 65:(h * 4 + fc) * 65 + 65]
                    rhs = qf[:, fc * TT:(fc + 1) * TT]
                    p.op("tensor",
                         lambda e, o=pk[0:65, :], l=lhs, r=rhs,
                         s=(fc == 0), z=(fc == 3):
                         e.matmul(o, l, r, start=s, stop=z),
                         sem="TE" if fc == 3 else None,
                         waits=(w0 if fc == 0 else []))
            else:
                for fc in range(4):
                    lhs = kvfm[:, (h * 4 + fc) * 65 + 1:(h * 4 + fc) * 65 + 65]
                    rhs = qf[:, fc * TT:(fc + 1) * TT]
                    p.op("tensor",
                         lambda e, o=pk[64:128, :], l=lhs, r=rhs,
                         s=(fc == 0), z=(fc == 3):
                         e.matmul(o, l, r, start=s, stop=z),
                         sem=None, waits=(w0 if fc == 0 else []))
                for fc in range(4):
                    lhs = kvfm[:, (h * 4 + fc) * 65:(h * 4 + fc) * 65 + 1]
                    rhs = qf[:, fc * TT:(fc + 1) * TT]
                    p.op("tensor",
                         lambda e, o=pk[0:1, :], l=lhs, r=rhs,
                         s=(fc == 0), z=(fc == 3):
                         e.matmul(o, l, r, start=s, stop=z),
                         sem="TE" if fc == 3 else None)
            te_qkv = p.c("TE")
            G["bqf_te"][h % 2] = te_qkv
            return te_qkv

        G["bphiB"][6] = p.c("VE")
        b_phi(0)
        pend = None
        for h in range(H):
            if h + 1 < H:
                b_phi(h + 1)
            b_epi(h)
            if pend is not None:
                b_flush(pend)
                pend = None
            te_qkv = b_qkv(h)
            pend = (h, te_qkv, h // 2)
        b_flush(pend)
        pend = None
        zg_all = p.c("ZG")
        attn_ve = p.c("VE")

        p.op("vector", lambda e: e.tensor_tensor(zall[0:H, :], zall[0:H, :], thr_fm, ALU.max),
             sem="VE", waits=[("ZG", zg_all), ("SE", thr_se)])
        p.op("vector", lambda e: e.reciprocal(zall[0:H, :], zall[0:H, :]), sem="VE")
        rz_ve = p.c("VE")

        for oc in range(KC):
            prz = PS[7]
            p.op("tensor", lambda e, o=prz, l=sel_all[:, oc * 128:(oc + 1) * 128],
                 r=zall:
                 e.matmul(o, l, r, start=True, stop=True),
                 sem="TE", waits=[("VE", rz_ve), ("VE", G["brz_ve"])])
            te_rz = p.c("TE")
            p.op("vector", lambda e, d=attn[:, oc * TT:(oc + 1) * TT], s=prz:
                 e.tensor_tensor(d, d, s, ALU.mult),
                 sem="VE", waits=[("TE", te_rz), ("VE", attn_ve)])
            G["brz_ve"] = p.c("VE")
        G["bphiB"][7] = p.c("VE")
        attn_scaled_ve = p.c("VE")

        for oc in range(KC):
            pw = PS[oc % 2]
            for kc in range(KC):
                lhs = wo_sb[:, kc * D + oc * 128: kc * D + (oc + 1) * 128]
                rhs = attn[:, kc * TT:(kc + 1) * TT]
                p.op("tensor",
                     lambda e, o=pw, l=lhs, r=rhs, s=(kc == 0), z=(kc == KC - 1):
                     e.matmul(o, l, r, start=s, stop=z),
                     sem="TE" if kc == KC - 1 else None,
                     waits=([("VE", attn_scaled_ve), ("VE", G["bw_ve"][oc % 2])]
                            if kc == 0 else []))
            te_w = p.c("TE")
            p.op("vector", lambda e, d=qtile[:, oc * TT:(oc + 1) * TT],
                 s=pw, xx=xt[:, oc * TT:(oc + 1) * TT]:
                 e.tensor_tensor(d, xx, s, ALU.add),
                 sem="VE", waits=[("TE", te_w)])
            G["bw_ve"][oc % 2] = p.c("VE")
        G["y_ve"] = p.c("VE")
        G["bxb"] = p.c("VE")

        def ln1_out(oc, ve_x, _t=t):
            st(xln_s[oc * 128:(oc + 1) * 128, _t * TT:(_t + 1) * TT],
               qsq[:, oc * TT:(oc + 1) * TT], "SX", waits=[("VE", ve_x)])
            p.op("vector", lambda e, d=xmaxst[:, _t * KC + oc: _t * KC + oc + 1],
                 s=qsq[:, oc * TT:(oc + 1) * TT]:
                 e.tensor_reduce(d, s, AX.X, ALU.max, apply_absolute_value=True),
                 sem="VE")
        ln_emit(qtile, qsq, qsq, ln1_out, PS[6], PS[7])
        G["sx_prev"] = p.c("SX")
        G["bqt_ve"][t % 2] = p.c("VE")
        G["bn_ve"] = p.c("VE")
        G["brz_ve"] = p.c("VE")
        G["bw_ve"] = {0: p.c("VE"), 1: p.c("VE")}
        G["bqkv_ve"] = {0: p.c("VE"), 1: p.c("VE")}
        G["bqf_te"] = {0: p.c("TE"), 1: p.c("TE")}
    b_st_all = p.c("SX")
    b_ve_all = p.c("VE")

    if stop == "B":
        if debug:
            st(dbg[:, 0:KC * TT], attn, "SO", waits=[("VE", b_ve_all)])
            st(dbg[0:H, 4096:4096 + TT], zall[0:H, :], "SO",
               waits=[("VE", b_ve_all)])
            st(dbg[0:H, 4608:4608 + TT], thr_fm, "SO", waits=[("VE", b_ve_all)])
            st(dbg[:, 5120:5120 + KC], xmaxst[:, (NT - 1) * KC:NT * KC],
               "SO", waits=[("VE", b_ve_all)])
        p.wait("gpsimd", [("SX", b_st_all)])
        for kc in range(KC):
            st(y_t[kc * 128:(kc + 1) * 128, :], xln_s[kc * 128:(kc + 1) * 128, :],
               "SO")
        _finish(nc, p)
        return nc

    p.op("vector", lambda e: e.tensor_reduce(m1col, xmaxst, AX.X, ALU.max),
         sem="VE", waits=[("VE", b_ve_all)])
    ve_m = p.c("VE")
    p.op("tensor", lambda e, o=PS[6][0:1, 0:128], i=m1col:
         e.transpose(o, i, ident), sem="TE",
         waits=[("VE", ve_m), ("SE", p.c("SE"))])
    te_m = p.c("TE")
    p.op("vector", lambda e: e.tensor_reduce(r1x1, PS[6][0:1, 0:128], AX.X, ALU.max),
         sem="VE", waits=[("TE", te_m)])
    p.op("vector", lambda e: e.tensor_scalar(ccst, onesrow[0:1, 0:8],
                                             r1x1[0:1, 0:1], None, ALU.mult),
         sem="VE")
    ve_cc = p.c("VE")
    p.op("gpsimd", lambda e: e.dma_start(out=cc_in1.ap(), in_=ccst),
         sem="SC", by=16, waits=[("VE", ve_cc)])
    p.wait("gpsimd", [("SC", p.c("SC"))])
    p.op("gpsimd", lambda e: e.collective_compute(
        "AllReduce", ALU.max, replica_groups=[[0, 1, 2, 3, 4, 5, 6, 7]],
        ins=[cc_in1.ap().opt()], outs=[cc_out1.ap().opt()]), sem="GE")
    p.wait("gpsimd", [("GE", p.c("GE"))])
    p.op("gpsimd", lambda e: e.dma_start(out=gmax1, in_=cc_out1.ap()[0:1, 0:1]),
         sem="SC", by=16)
    g1_st = p.c("SC")
    p.op("vector", lambda e: e.tensor_tensor(r1x1, gmax1, sig2[0:1, 0:1], ALU.mult),
         sem="VE", waits=[("SC", g1_st)])
    p.op("vector", lambda e: e.reciprocal(r1x1b, r1x1), sem="VE")
    p.op("vector", lambda e: e.tensor_scalar(r1x1b, r1x1b, 127.0, None, ALU.mult),
         sem="VE")
    p.op("vector", lambda e: e.tensor_scalar(r1x1, r1x1, INV127 * INV127, None,
                                             ALU.mult), sem="VE")
    ve_rx = p.c("VE")
    p.op("tensor", lambda e, o=PS[6][:, 0:1]: e.matmul(o, onesrow, r1x1b,
                                                       start=True, stop=True),
         sem="TE", waits=[("VE", ve_rx)])
    p.op("tensor", lambda e, o=PS[7][:, 0:1]: e.matmul(o, onesrow, r1x1,
                                                       start=True, stop=True),
         sem="TE")
    te_rx = p.c("TE")
    p.op("vector", lambda e: e.tensor_copy(rx_col, PS[6][:, 0:1]),
         sem="VE", waits=[("TE", te_rx)])
    p.op("vector", lambda e: e.tensor_copy(sxs_col, PS[7][:, 0:1]), sem="VE")
    rx_ve = p.c("VE")

    if stop == "AR1":
        if debug:
            st(dbg[0:1, 0:1], gmax1, "SO", waits=[("VE", rx_ve)])
            st(dbg[:, 4:5], rx_col, "SO", waits=[("VE", rx_ve)])
            st(dbg[:, 8:9], m1col, "SO", waits=[("VE", rx_ve)])
            st(dbg[0:1, 16:24], ccst, "SO", waits=[("VE", rx_ve)])
            st(dbg[:, 32:96], xmaxst, "SO", waits=[("VE", rx_ve)])
        _finish(nc, p)
        return nc

    p.barrier_eng("sync")

    def quant_weight(wnat, wtt, nch_nat, nat_w, sall, tgt, nbufs):
        wout = nch_nat * 128
        kcin = nat_w // 128
        nb = len(nbufs)
        G["fqn_ve"] = {i: 0 for i in range(nb)}
        for j in range(nch_nat):
            buf = nbufs[j % nb]
            lh = "LH%d" % (j % 4)
            ld(buf, wnat[j * 128:(j + 1) * 128, :], lh,
               waits=[("VE", G["fqn_ve"][j % nb]),
                      ("VE", G.get("fq_ve", 0)), ("VE", G.get("fqm_ve", 0)),
                      ("VE", G.get("fqq_ve", 0)),
                      ("SE", G.get("fq_se", 0))])
            ldj = p.c(lh)
            p.op("vector", lambda e, d=sall[:, j:j + 1], s=buf:
                 e.tensor_reduce(d, s, AX.X, ALU.max, apply_absolute_value=True),
                 sem="VE", waits=[(lh, ldj)])
            G["fqn_ve"][j % nb] = p.c("VE")
        G["fq_ve"] = p.c("VE")
        p.op("vector", lambda e, d=mred[:, 0:nch_nat], s=sall[:, 0:nch_nat]:
             e.reciprocal(d, s), sem="VE")
        p.op("vector", lambda e, d=mred[:, 0:nch_nat]:
             e.tensor_scalar(d, d, 127.0, None, ALU.mult), sem="VE")
        ve_r = p.c("VE")
        p.op("tensor", lambda e, o=PS[6][0:nch_nat, 0:128], i=mred[:, 0:nch_nat]:
             e.transpose(o, i, ident), sem="TE",
             waits=[("VE", ve_r), ("VE", G.get("fqb_ve", 0))])
        te_r = p.c("TE")
        p.op("vector", lambda e, d=tp22[0:nch_nat, :], s=PS[6][0:nch_nat, 0:128]:
             e.tensor_copy(d, s), sem="VE", waits=[("TE", te_r)])
        ve_t = p.c("VE")
        for j in range(nch_nat):
            p.op("scalar", lambda e, d=rb[0:1, j * 128:(j + 1) * 128],
                 s=tp22[j:j + 1, :]:
                 e.dma_start(out=d, in_=s), sem="ZG", by=16,
                 waits=[("VE", ve_t)])
        zg_r = p.c("ZG")
        nslab = (wout + 511) // 512
        for s_ in range(nslab):
            wlen = min(512, wout - s_ * 512)
            p.op("tensor", lambda e, o=PS[6][:, 0:wlen], l=onesrow,
                 r=rb[0:1, s_ * 512:s_ * 512 + wlen]:
                 e.matmul(o, l, r, start=True, stop=True),
                 sem="TE", waits=[("ZG", zg_r), ("VE", G.get("fqb_ve", 0))])
            te_b = p.c("TE")
            p.op("vector", lambda e, d=rb[:, s_ * 512:s_ * 512 + wlen],
                 s=PS[6][:, 0:wlen]:
                 e.tensor_copy(d, s), sem="VE", waits=[("TE", te_b)])
            G["fqb_ve"] = p.c("VE")
        rb_ve = p.c("VE")
        hw = wout // 2
        G["fqs"] = {0: 0, 1: 0}
        gidx = 0
        for kc in range(kcin):
            mults = []
            for gi in range(2):
                slot = gidx % 2
                gidx += 1
                sbuf = wtc0[:, slot * hw:(slot + 1) * hw]
                lh = "LH%d" % (gidx % 4)
                ld(sbuf, wtt[kc * 128:(kc + 1) * 128, gi * hw:(gi + 1) * hw],
                   lh, waits=[("VE", G["fqs"][slot]), ("VE", G.get("fq_ve", 0))])
                ldc = p.c(lh)
                p.op("vector", lambda e, d=avq[:, gi * hw:(gi + 1) * hw],
                     a=sbuf, b=rb[:, gi * hw:(gi + 1) * hw]:
                     e.tensor_tensor(d, a, b, ALU.mult),
                     sem="VE", waits=[(lh, ldc), ("VE", rb_ve),
                                      ("VE", G.get("fqq_ve", 0))])
                G["fqs"][slot] = p.c("VE")
            ve_m_ = p.c("VE")
            G["fqm_ve"] = ve_m_
            p.op("scalar", lambda e, d=sgnq[:, 0:wout], s=avq[:, 0:wout]:
                 e.activation(d, s, AF.Sign, bias=czero, scale=one),
                 sem="SE", waits=[("VE", ve_m_), ("VE", G.get("fqq_ve", 0))])
            se_a = p.c("SE")
            G["fq_se"] = se_a
            p.op("vector", lambda e, d=t2q[:, 0:wout], a=sgnq[:, 0:wout],
                 b=avq[:, 0:wout]:
                 e.scalar_tensor_tensor(d, a, 128.0, b, ALU.mult, ALU.add),
                 sem="VE", waits=[("SE", se_a)])
            p.op("vector", lambda e, d=tgt[:, kc * wout:(kc + 1) * wout],
                 a=sgnq[:, 0:wout], b=t2q[:, 0:wout]:
                 e.scalar_tensor_tensor(d, a, -128.0, b, ALU.mult, ALU.add),
                 sem="VE")
            G["fqq_ve"] = p.c("VE")
        return p.c("VE")

    quant_weight(wg_n, wg_tt, ICN, D, sall_g, wgq,
                 [wtc0[:, 0:D], wtc0[:, D:2 * D], avq[:, 0:D], avq[:, D:2 * D]])
    quant_weight(wu_n, wu_tt, ICN, D, sall_u, wuq,
                 [wtc0[:, 0:D], wtc0[:, D:2 * D], avq[:, 0:D], avq[:, D:2 * D]])
    wdq_ve = quant_weight(wd_n, wd_tt, KC, I, sall_d, wdq,
                          [wtc0[:, 0:I], avq[:, 0:I], natd2, natd3])
    for kc in range(ICN):
        st(wdq_s[kc * 128:(kc + 1) * 128, :], wdq[:, kc * D:(kc + 1) * D],
           "SW", waits=[("VE", wdq_ve)])
    wq_st = p.c("SW")
    wq_ve = p.c("VE")

    p.op("vector", lambda e: e.tensor_scalar(ax_all, sall_g, sxs_col, None, ALU.mult),
         sem="VE", waits=[("VE", rx_ve)])
    p.op("vector", lambda e: e.tensor_scalar(bu_all, sall_u, sxs_col, None, ALU.mult),
         sem="VE")
    scales_ve = p.c("VE")

    if stop == "FFNQ":
        if debug:
            st(dbg[:, 0:5632], wgq[:, 0:5632], "SO", waits=[("VE", scales_ve)])
            st(dbg[:, 6000:6000 + ICN], sall_g, "SO", waits=[("VE", scales_ve)])
            st(dbg[:, 6030:6030 + ICN], ax_all, "SO", waits=[("VE", scales_ve)])
        _finish(nc, p)
        return nc

    p.barrier_eng("sync")
    G["f1x"] = {0: 0, 1: 0}
    G["f1g_se"] = [0, 0, 0, 0]
    G["f1h_ve"] = [0, 0, 0, 0]
    G["f1s_st"] = {}
    G["f1q_ve"] = 0
    for t in range(NT):
        xt = xl0 if t % 2 == 0 else xl1
        lx = "LX%d" % (t % 2)
        for kc in range(KC):
            ld(xt[:, kc * TT:(kc + 1) * TT],
               xln_s[kc * 128:(kc + 1) * 128, t * TT:(t + 1) * TT], lx,
               waits=[("TE", G["f1x"][t % 2]), ("SW", wq_st), ("VE", wq_ve),
                      ("SX", b_st_all), ("SE", G.get("f1a_se", 0))])
        x_ld = p.c(lx)
        p.op("scalar", lambda e, d=avf, s=xt:
             e.activation(d, s, AF.Abs, bias=czero, scale=rx_col),
             sem="SE", waits=[(lx, x_ld), ("VE", scales_ve), ("VE", G["f1q_ve"])])
        p.op("scalar", lambda e, d=sgnf, s=xt:
             e.activation(d, s, AF.Sign, bias=czero, scale=one), sem="SE")
        se_q = p.c("SE")
        G["f1a_se"] = se_q
        xqb = xqb2[t % 2]
        p.op("vector", lambda e, d=t2f, s=avf:
             e.tensor_scalar(d, s, 128.0, 256.49, ALU.add, ALU.min),
             sem="VE", waits=[("SE", se_q)])
        p.op("vector", lambda e, d=xqb, a=t2f, b=sgnf:
             e.scalar_tensor_tensor(d, a, -128.0, b, ALU.add, ALU.mult),
             sem="VE", waits=[("TE", G["f1x"][t % 2])])
        p.op("vector", lambda e, d=xqb: e.tensor_scalar(d, d, 127.0, None, ALU.min),
             sem="VE")
        xq_ve = p.c("VE")
        G["f1q_ve"] = xq_ve

        for ic in range(ICN):
            pg = PS[(ic % 2) * 2]
            pu = PS[(ic % 2) * 2 + 1]
            for kc in range(KC):
                lg = wgq[:, kc * I + ic * 128: kc * I + (ic + 1) * 128]
                lu = wuq[:, kc * I + ic * 128: kc * I + (ic + 1) * 128]
                rr = xqb[:, kc * TT:(kc + 1) * TT]
                p.op("tensor", lambda e, o=pg, l=lg, r=rr, s=(kc == 0),
                     z=(kc == KC - 1):
                     e.matmul(o, l, r, start=s, stop=z),
                     sem="TE" if kc == KC - 1 else None,
                     waits=([("VE", xq_ve), ("SE", G["f1g_se"][(ic % 2) * 2])]
                            if kc == 0 else []))
                p.op("tensor", lambda e, o=pu, l=lu, r=rr, s=(kc == 0),
                     z=(kc == KC - 1):
                     e.matmul(o, l, r, start=s, stop=z),
                     sem="TE" if kc == KC - 1 else None,
                     waits=([("VE", G["f1h_ve"][(ic % 2) * 2 + 1])]
                            if kc == 0 else []))
            te_gu = p.c("TE")
            sl = silu0 if ic % 2 == 0 else silu1
            hbuf = hb[ic % 4]
            p.op("scalar", lambda e, d=sl, s=pg, sc=ax_all[:, ic:ic + 1]:
                 e.activation(d, s, AF.Silu, bias=czero, scale=sc),
                 sem="SE", waits=[("TE", te_gu)])
            se_s = p.c("SE")
            G["f1g_se"][(ic % 2) * 2] = se_s
            p.op("vector", lambda e, d=hbuf, a=sl, b=pu:
                 e.tensor_tensor(d, a, b, ALU.mult),
                 sem="VE", waits=[("SE", se_s),
                                  ("SH%d" % (ic % 4), G["f1s_st"].get(ic % 4, 0))])
            ve_h = p.c("VE")
            G["f1h_ve"][(ic % 2) * 2] = ve_h
            G["f1h_ve"][(ic % 2) * 2 + 1] = ve_h
            st(h_s[ic * 128:(ic + 1) * 128, t * TT:(t + 1) * TT], hbuf,
               "SH%d" % (ic % 4), waits=[("VE", ve_h)])
            G["f1s_st"][ic % 4] = p.c("SH%d" % (ic % 4))
            p.op("vector", lambda e, d=hmaxst[:, t * ICN + ic: t * ICN + ic + 1],
                 s=hbuf:
                 e.tensor_reduce(d, s, AX.X, ALU.max, apply_absolute_value=True),
                 sem="VE")
        G["f1x"][t % 2] = p.c("TE")
        G["f1g_se"] = [p.c("SE")] * 4
        G["f1h_ve"] = [p.c("VE")] * 4
    f1_ve = p.c("VE")
    f1_st = tuple(p.c("SH%d" % i) for i in range(4))

    hview = hmaxst.rearrange("p (t c) -> p c t", c=ICN)
    p.op("vector", lambda e: e.tensor_reduce(mred, hview, AX.X, ALU.max),
         sem="VE", waits=[("VE", f1_ve)])
    p.op("vector", lambda e: e.tensor_tensor(mred, mred, bu_all, ALU.mult), sem="VE")
    p.op("vector", lambda e: e.tensor_reduce(m1col, mred, AX.X, ALU.max), sem="VE")
    ve_m2 = p.c("VE")
    p.op("tensor", lambda e, o=PS[6][0:1, 0:128], i=m1col:
         e.transpose(o, i, ident), sem="TE",
         waits=[("VE", ve_m2), ("SE", p.c("SE"))])
    te_m2 = p.c("TE")
    p.op("vector", lambda e: e.tensor_reduce(r1x1, PS[6][0:1, 0:128], AX.X, ALU.max),
         sem="VE", waits=[("TE", te_m2)])
    p.op("vector", lambda e: e.tensor_scalar(ccst, onesrow[0:1, 0:8],
                                             r1x1[0:1, 0:1], None, ALU.mult),
         sem="VE")
    ve_cc2 = p.c("VE")
    p.op("gpsimd", lambda e: e.dma_start(out=cc_in2.ap(), in_=ccst),
         sem="SC", by=16, waits=[("VE", ve_cc2)])
    p.wait("gpsimd", [("SC", p.c("SC"))])
    p.op("gpsimd", lambda e: e.collective_compute(
        "AllReduce", ALU.max, replica_groups=[[0, 1, 2, 3, 4, 5, 6, 7]],
        ins=[cc_in2.ap().opt()], outs=[cc_out2.ap().opt()]), sem="GE")
    p.wait("gpsimd", [("GE", p.c("GE"))])
    p.op("gpsimd", lambda e: e.dma_start(out=gmax2, in_=cc_out2.ap()[0:1, 0:1]),
         sem="SC", by=16)
    g2_st = p.c("SC")
    p.op("vector", lambda e: e.tensor_tensor(r1x1, gmax2, sig2[0:1, 1:2], ALU.mult),
         sem="VE", waits=[("SC", g2_st)])
    p.op("vector", lambda e: e.reciprocal(r1x1b, r1x1), sem="VE")
    p.op("vector", lambda e: e.tensor_scalar(r1x1b, r1x1b, 127.0, None, ALU.mult),
         sem="VE")
    p.op("vector", lambda e: e.tensor_scalar(r1x1, r1x1, INV127 * INV127, None,
                                             ALU.mult), sem="VE")
    ve_rh = p.c("VE")
    p.op("tensor", lambda e, o=PS[6][:, 0:1]: e.matmul(o, onesrow, r1x1b,
                                                       start=True, stop=True),
         sem="TE", waits=[("VE", ve_rh)])
    p.op("tensor", lambda e, o=PS[7][:, 0:1]: e.matmul(o, onesrow, r1x1,
                                                       start=True, stop=True),
         sem="TE")
    te_rh = p.c("TE")
    p.op("vector", lambda e: e.tensor_copy(rh_col, PS[6][:, 0:1]),
         sem="VE", waits=[("TE", te_rh)])
    p.op("vector", lambda e: e.tensor_copy(shs_col, PS[7][:, 0:1]), sem="VE")
    p.op("vector", lambda e: e.tensor_scalar(reff_all, bu_all, rh_col, None,
                                             ALU.mult), sem="VE")
    p.op("vector", lambda e: e.tensor_scalar(cd_all, sall_d, shs_col, None,
                                             ALU.mult), sem="VE")
    rh_ve = p.c("VE")

    if stop == "FFN1":
        if debug:
            st(dbg[0:1, 0:1], gmax2, "SO", waits=[("VE", rh_ve)])
            st(dbg[:, 4:4 + ICN], reff_all, "SO", waits=[("VE", rh_ve)])
            st(dbg[:, 30:30 + KC], cd_all, "SO", waits=[("VE", rh_ve)])
            st(dbg[:, 40:40 + NT * ICN], hmaxst, "SO", waits=[("VE", rh_ve)])
        _finish(nc, p)
        return nc

    p.barrier_eng("sync")
    for kc in range(ICN):
        ld(wdq2[:, kc * D:(kc + 1) * D], wdq_s[kc * 128:(kc + 1) * 128, :],
           "LW", waits=[("SW", wq_st)])
    wd_ld = p.c("LW")

    G["f2x"] = {0: 0, 1: 0}
    G["f2h_se"] = {}
    G["f2d_ve"] = {0: 0, 1: 0}
    G["f2q_ve"] = {0: 0, 1: 0}
    G["f2hq_te"] = {0: 0, 1: 0}
    G["f2s_ve"] = 0
    for t in range(NT):
        xt = xl20 if t % 2 == 0 else xl21
        hqt = hq0 if t % 2 == 0 else hq1
        lx = "LX%d" % (t % 2)
        for kc in range(KC):
            ld(xt[:, kc * TT:(kc + 1) * TT],
               xln_s[kc * 128:(kc + 1) * 128, t * TT:(t + 1) * TT], lx,
               waits=[("VE", G["f2x"][t % 2])])
        x_ld = p.c(lx)
        for ic in range(ICN):
            hbuf = hl[ic % 4]
            lh = "LH%d" % (ic % 4)
            ld(hbuf, h_s[ic * 128:(ic + 1) * 128, t * TT:(t + 1) * TT], lh,
               waits=[("SE", G["f2h_se"].get(ic % 4, 0)),
                      ("SH0", f1_st[0]), ("SH1", f1_st[1]),
                      ("SH2", f1_st[2]), ("SH3", f1_st[3])])
            ld_h = p.c(lh)
            par = ic % 2
            av_, sg_, t2_ = avf2[par], sgnf2[par], t2f2[par]
            p.op("scalar", lambda e, d=av_, s=hbuf, sc=reff_all[:, ic:ic + 1]:
                 e.activation(d, s, AF.Abs, bias=czero, scale=sc),
                 sem="SE", waits=[(lh, ld_h), ("VE", rh_ve),
                                  ("VE", G["f2q_ve"][par])])
            p.op("scalar", lambda e, d=sg_, s=hbuf:
                 e.activation(d, s, AF.Sign, bias=czero, scale=one), sem="SE")
            se_q = p.c("SE")
            G["f2h_se"][ic % 4] = se_q
            p.op("vector", lambda e, d=t2_, s=av_:
                 e.tensor_scalar(d, s, 128.0, 256.49, ALU.add, ALU.min),
                 sem="VE", waits=[("SE", se_q)])
            p.op("vector", lambda e, d=hqt[:, ic * TT:(ic + 1) * TT], a=t2_,
                 b=sg_:
                 e.scalar_tensor_tensor(d, a, -128.0, b, ALU.add, ALU.mult),
                 sem="VE", waits=[("TE", G["f2hq_te"][t % 2])])
            p.op("vector", lambda e, d=hqt[:, ic * TT:(ic + 1) * TT]:
                 e.tensor_scalar(d, d, 127.0, None, ALU.min), sem="VE")
            G["f2q_ve"][par] = p.c("VE")
        hq_ve = p.c("VE")

        for oc in range(KC):
            pd = PS[oc % 2]
            for kc in range(ICN):
                lhs = wdq2[:, kc * D + oc * 128: kc * D + (oc + 1) * 128]
                rhs = hqt[:, kc * TT:(kc + 1) * TT]
                p.op("tensor",
                     lambda e, o=pd, l=lhs, r=rhs, s=(kc == 0), z=(kc == ICN - 1):
                     e.matmul(o, l, r, start=s, stop=z),
                     sem="TE" if kc == ICN - 1 else None,
                     waits=([("VE", hq_ve), ("LW", wd_ld), (lx, x_ld),
                             ("VE", G["f2d_ve"][oc % 2])] if kc == 0 else []))
            te_d = p.c("TE")
            p.op("vector", lambda e, d=yout[:, oc * TT:(oc + 1) * TT],
                 a=pd, sc=cd_all[:, oc:oc + 1], b=xt[:, oc * TT:(oc + 1) * TT]:
                 e.scalar_tensor_tensor(d, a, sc, b, ALU.mult, ALU.add),
                 sem="VE", waits=[("TE", te_d)])
            G["f2d_ve"][oc % 2] = p.c("VE")
        G["y_ve"] = p.c("VE")

        def ln2_out(oc, ve_o, _t=t):
            st(y_t[oc * 128:(oc + 1) * 128, _t * TT:(_t + 1) * TT],
               ysq2[:, oc * TT:(oc + 1) * TT], "SO", waits=[("VE", ve_o)])
        ln_emit(yout, ysq2, ysq2, ln2_out, PS[6], PS[7])
        G["f2x"][t % 2] = p.c("VE")
        G["f2d_ve"] = {0: p.c("VE"), 1: p.c("VE")}
        G["f2hq_te"][t % 2] = p.c("TE")

    _finish(nc, p)
    return nc


def _finish(nc, p):
    p.wait("gpsimd", [(s, p.c(s)) for s in
                      ("SO", "SX", "SH0", "SH1", "SH2", "SH3", "SW", "SC", "ZG")])
    with nc.Block() as block:
        for eng in ENGS:
            def run(e, _ops=p.ops[eng]):
                for f in _ops:
                    f(e)
            getattr(block, eng)(run)


def host_consts():
    c = np.zeros((128, 13), dtype=np.float32)
    c[:, C_ONE] = 1.0
    c[:, C_HALFPI] = PI / 2
    c[:, C_MHALF] = -0.5
    c[:, C_EPS] = 1e-5
    c[:, C_LNTHR] = LN_THR
    c[:, C_PHALF] = 0.5
    c[:, C_MONE] = -1.0
    c[:, C_MLN16] = MLN16
    c[:, C_PI] = PI
    c[:, C_MPI] = -PI
    c[:, C_MHPI] = -PI / 2
    return c


def kernel(L0_attn_x, L0_attn_wq, L0_attn_wk, L0_attn_wv, L0_attn_wo, L0_attn_rf,
           L0_ffn_clip_x, L0_ffn_clip_down, L0_ffn_w_gate, L0_ffn_w_up,
           L0_ffn_w_down):
    x = np.ascontiguousarray(np.asarray(L0_attn_x, dtype=np.float32))
    wq = np.ascontiguousarray(np.asarray(L0_attn_wq, dtype=np.float32).T)
    wk = np.ascontiguousarray(np.asarray(L0_attn_wk, dtype=np.float32).T)
    wv = np.ascontiguousarray(np.asarray(L0_attn_wv, dtype=np.float32).T)
    wo = np.ascontiguousarray(np.asarray(L0_attn_wo, dtype=np.float32).T)
    rf = np.asarray(L0_attn_rf, dtype=np.float32)
    rf_rep = np.ascontiguousarray(np.concatenate([rf.T, rf.T], axis=0))
    rf_bd = np.zeros((128, 2 * MF), dtype=np.float32)
    rf_bd[0:64, 0:MF] = rf.T
    rf_bd[64:128, MF:2 * MF] = rf.T
    wg = np.ascontiguousarray(np.asarray(L0_ffn_w_gate, dtype=np.float32))
    wu = np.ascontiguousarray(np.asarray(L0_ffn_w_up, dtype=np.float32))
    wd = np.ascontiguousarray(np.asarray(L0_ffn_w_down, dtype=np.float32))
    clip2 = np.array([[float(np.asarray(L0_ffn_clip_x).reshape(-1)[0]),
                       float(np.asarray(L0_ffn_clip_down).reshape(-1)[0])]],
                     dtype=np.float32)
    maskh = np.zeros((128, KC * H), dtype=np.float32)
    for kc in range(KC):
        for pl in range(128):
            maskh[pl, kc * H + 2 * kc + pl // 64] = 1.0
    sel_all = np.zeros((128, KC * 128), dtype=np.float32)
    for oc in range(KC):
        for pl in range(128):
            sel_all[2 * oc + pl // 64, oc * 128 + pl] = 1.0

    shared = {
        "wq_t": wq, "wk_t": wk, "wv_t": wv, "wo_t": wo, "rf_rep": rf_rep,
        "rf_bd": rf_bd,
        "wg_n": wg, "wu_n": wu, "wd_n": wd,
        "wg_tt": np.ascontiguousarray(wg.T),
        "wu_tt": np.ascontiguousarray(wu.T),
        "wd_tt": np.ascontiguousarray(wd.T),
        "clip2": clip2, "consts": host_consts(),
        "onesrow": np.ones((1, 128), dtype=np.float32),
        "ident": np.eye(128, dtype=np.float32),
        "maskh": maskh, "sel_all": sel_all,
    }
    in_maps = []
    for b in range(8):
        m = dict(shared)
        m["x_t"] = np.ascontiguousarray(x[b].T)
        in_maps.append(m)

    nc = build_core_program()
    res = run_bass_kernel_spmd(nc, in_maps, core_ids=list(range(8)))
    return np.stack([np.ascontiguousarray(r["y_t"].T) for r in res.results], axis=0)


# version 16
# speedup vs baseline: 1.0237x; 1.0154x over previous
import math
import numpy as np
import concourse.bass as bass
import concourse.mybir as mybir
from concourse.bass_utils import run_bass_kernel_spmd

F32 = mybir.dt.float32
BF16 = mybir.dt.bfloat16
I32 = mybir.dt.int32
AF = mybir.ActivationFunctionType
ALU = mybir.AluOpType
AX = mybir.AxisListType

T = 4096
D = 1024
H = 16
DH = 64
MF = 256
M2 = 512
I = 2816
KC = D // 128
ICN = I // 128
TT = 512
NT = T // TT
TB = 128
NTB = TT // TB

PI = math.pi
INV127 = 1.0 / 127.0
LN_THR = math.log(1e-6 * math.sqrt(MF))
MLN16 = -math.log(math.sqrt(MF))

(C_ONE, C_HALFPI, C_MHALF, C_EPS, C_LNTHR, C_PHALF, C_ZERO, C_MONE,
 C_MLN16, C_SPARE, C_PI, C_MPI, C_MHPI) = range(13)

ENGS = ("sync", "tensor", "vector", "scalar", "gpsimd")


class Plan:
    def __init__(self, nc):
        self.nc = nc
        self.ops = {e: [] for e in ENGS}
        self.sems = {}
        self.cnt = {}
        self.mark = {e: {} for e in ENGS}
        self.g = {}

    def sem(self, name):
        s = self.nc.alloc_semaphore(name)
        self.sems[name] = s
        self.cnt[name] = 0

    def op(self, eng, f, sem=None, by=1, waits=()):
        if eng == "vector":
            waits = list(waits) + [("VE", self.cnt["VE"])]
        elif eng == "scalar":
            waits = list(waits) + [("SE", self.cnt["SE"])]
        real = []
        wm = self.mark[eng]
        for sn, val in waits:
            if val > 0 and wm.get(sn, 0) < val:
                wm[sn] = val
                real.append((self.sems[sn], val))
        semh = None
        if sem is not None:
            self.cnt[sem] += by
            semh = self.sems[sem]
        def fn(e, _w=real, _f=f, _s=semh, _b=by):
            for sh, v in _w:
                e.wait_ge(sh, v)
            ins = _f(e)
            if _s is not None:
                ins.then_inc(_s, _b)
        self.ops[eng].append(fn)

    def wait(self, eng, waits):
        wm = self.mark[eng]
        for sn, val in waits:
            if val > 0 and wm.get(sn, 0) < val:
                wm[sn] = val
                s = self.sems[sn]
                self.ops[eng].append(lambda e, _s=s, _v=val: e.wait_ge(_s, _v))

    def c(self, name):
        return self.cnt[name]

    def barrier_eng(self, eng):
        self.wait(eng, [(s, self.cnt[s]) for s in self.cnt
                        if s != "GE"])


def build_core_program(stop=None, debug=False, amask="nvpkt"):
    nc = bass.Bass("TRN2", target_bir_lowering=False)

    x_t = nc.dram_tensor("x_t", [D, T], F32, kind="ExternalInput").ap()
    wq_t = nc.dram_tensor("wq_t", [D, D], F32, kind="ExternalInput").ap()
    wk_t = nc.dram_tensor("wk_t", [D, D], F32, kind="ExternalInput").ap()
    wv_t = nc.dram_tensor("wv_t", [D, D], F32, kind="ExternalInput").ap()
    wo_t = nc.dram_tensor("wo_t", [D, D], F32, kind="ExternalInput").ap()
    rf_rep_i = nc.dram_tensor("rf_rep", [128, MF], F32, kind="ExternalInput").ap()
    rf_bd_i = nc.dram_tensor("rf_bd", [128, M2], F32, kind="ExternalInput").ap()
    wg_n = nc.dram_tensor("wg_n", [I, D], F32, kind="ExternalInput").ap()
    wu_n = nc.dram_tensor("wu_n", [I, D], F32, kind="ExternalInput").ap()
    wd_n = nc.dram_tensor("wd_n", [D, I], F32, kind="ExternalInput").ap()
    wg_tt = nc.dram_tensor("wg_tt", [D, I], F32, kind="ExternalInput").ap()
    wu_tt = nc.dram_tensor("wu_tt", [D, I], F32, kind="ExternalInput").ap()
    wd_tt = nc.dram_tensor("wd_tt", [I, D], F32, kind="ExternalInput").ap()
    clip2 = nc.dram_tensor("clip2", [1, 2], F32, kind="ExternalInput").ap()
    consts_i = nc.dram_tensor("consts", [128, 13], F32, kind="ExternalInput").ap()
    onesrow_i = nc.dram_tensor("onesrow", [1, 128], F32, kind="ExternalInput").ap()
    ident_i = nc.dram_tensor("ident", [128, 128], F32, kind="ExternalInput").ap()
    maskh_i = nc.dram_tensor("maskh", [128, KC * H], F32, kind="ExternalInput").ap()
    sel_i = nc.dram_tensor("sel_all", [128, KC * 128], F32, kind="ExternalInput").ap()

    y_t = nc.dram_tensor("y_t", [D, T], F32, kind="ExternalOutput").ap()
    dbg = (nc.dram_tensor("dbg", [128, 16384], F32, kind="ExternalOutput").ap()
           if debug else None)

    xln_s = nc.dram_tensor("xln_s", [D, T], F32).ap()
    h_s = nc.dram_tensor("h_s", [I, T], F32).ap()
    wdq_s = nc.dram_tensor("wdq_s", [I, D], BF16).ap()
    cc_in1 = nc.dram_tensor("cc_in1", [1, 8], F32)
    cc_out1 = nc.dram_tensor("cc_out1", [1, 8], F32)
    cc_in2 = nc.dram_tensor("cc_in2", [1, 8], F32)
    cc_out2 = nc.dram_tensor("cc_out2", [1, 8], F32)

    A = lambda *a: nc.alloc_sbuf_tensor(*a).ap()
    consts = A("consts_sb", [128, 13], F32)
    onesrow = A("onesrow_sb", [1, 128], F32)
    ident = A("ident_sb", [128, 128], F32)
    rf_rep = A("rf_rep_sb", [128, MF], F32)
    rf_bd = A("rf_bd_sb", [128, M2], F32)
    maskh = A("maskh_sb", [128, KC * H], F32)
    sel_all = A("sel_sb", [128, KC * 128], F32)
    clip_sb = A("clip_sb", [1, 2], F32)
    sig2 = A("sig2_sb", [1, 2], F32)
    scale_sb = A("scaleK_sb", [128, H], F32)
    zst = A("zst_sb", [128, TT], F32)
    zall = A("zall_sb", [128, TT], F32)
    thr_fm = A("thr_sb", [H, TT], F32)
    rsrow = A("rsrow", [1, TT], F32)
    nurow = A("nurow", [1, TT], F32)
    sc512 = A("sc512", [128, TT], F32)
    xmaxst = A("xmaxst", [128, NT * KC], F32)
    hmaxst = A("hmaxst", [128, NT * ICN], F32)
    sall_g = A("sall_g", [128, ICN], F32)
    sall_u = A("sall_u", [128, ICN], F32)
    sall_d = A("sall_d", [128, KC], F32)
    ax_all = A("ax_all", [128, ICN], F32)
    bu_all = A("bu_all", [128, ICN], F32)
    reff_all = A("reff_all", [128, ICN], F32)
    cd_all = A("cd_all", [128, KC], F32)
    rx_col = A("rx_col", [128, 1], F32)
    sxs_col = A("sxs_col", [128, 1], F32)
    rh_col = A("rh_col", [128, 1], F32)
    shs_col = A("shs_col", [128, 1], F32)
    r1x1 = A("r1x1", [1, 1], F32)
    r1x1b = A("r1x1b", [1, 1], F32)
    gmax1 = A("gmax1", [1, 1], F32)
    gmax2 = A("gmax2", [1, 1], F32)
    ccst = A("ccst", [1, 8], F32)
    tp22 = A("tp22", [ICN, 128], F32)
    mred = A("mred", [128, ICN], F32)
    m1col = A("m1col", [128, 1], F32)

    PS = [nc.alloc_psum_tensor(f"ps{i}", [128, 512], F32).ap() for i in range(8)]

    BASE = 40960
    TOP = 229344

    def place(name, shape, dtype, off_kb):
        off = int(off_kb * 1024)
        nbytes = int(np.prod(shape[1:])) * mybir.dt.size(dtype)
        assert BASE + off + nbytes <= TOP, (name, off_kb, nbytes)
        return nc.alloc_sbuf_tensor_at(name, shape, dtype, offset=BASE + off).ap()

    wk_sb = place("wk_sb", [128, KC * D], F32, 0)
    wv_sb = place("wv_sb", [128, KC * D], F32, 32)
    kvacc = place("kvacc", [128, H * 512], F32, 64)
    xa0 = place("xa0", [128, KC * TT], F32, 96)
    ktile = place("ktile", [128, KC * TT], F32, 128)
    ksq = place("ksq", [128, KC * TT], F32, 144)
    kfps = [place("kfp0", [128, 2 * M2], F32, 144),
            place("kfp1", [128, 2 * M2], F32, 112),
            place("kfp2", [128, 2 * M2], F32, 116),
            place("kfp3", [128, 2 * M2], F32, 120)]
    pti = place("pti", [128, M2], I32, 148)
    actc = place("actc", [128, M2], F32, 148)
    ap0 = [place("ap0_0", [128, M2], F32, 150),
           place("ap0_1", [128, M2], F32, 152),
           place("ap0_2", [128, M2], F32, 124),
           place("ap0_3", [128, M2], F32, 126)]
    au = place("au", [128, M2], F32, 154)
    as1 = [place("as1_0", [128, M2], F32, 156),
           place("as1_1", [128, M2], F32, 177)]
    asgn = place("asgn", [128, M2], F32, 158)
    v65a = place("v65a", [128, NTB * H * 65], F32, 160)
    scale_a = place("scale_a", [128, NTB * H], F32, 176.5)
    kvfm = place("kvfm", [128, H * 4 * 65], F32, 96)

    wq_sb = place("wq_sb", [128, KC * D], F32, 0)
    wo_sb = place("wo_sb", [128, KC * D], F32, 32)
    xb0 = place("xb0", [128, KC * TT], F32, 64)
    qtiles = [place("qtileA", [128, KC * TT], F32, 113),
              place("qtileB", [128, KC * TT], F32, 80)]
    qsq = place("qsq", [128, KC * TT], F32, 129)
    attn = place("attn", [128, KC * TT], F32, 145)
    qf0 = place("qf0", [128, 4 * TT], F32, 161)
    qf1 = place("qf1", [128, 4 * TT], F32, 169)
    bp0 = [place(f"bp0_{m}", [128, M2], F32, 129 + 2 * m) for m in range(2)]
    bu = [place(f"bu_{m}", [128, M2], F32, 133 + 2 * m) for m in range(2)]
    bsgn = [place(f"bsgn_{m}", [128, M2], F32, 137 + 2 * m) for m in range(2)]
    bs1 = [place(f"bs1_{m}", [128, M2], F32, 141 + 2 * m) for m in range(2)]
    ptbi = place("ptbi", [128, 512], I32, 177)
    actcB = place("actcB", [128, 512], F32, 177)
    ba2 = [place(f"ba2_{m}", [128, M2], F32, 179 + 2 * m) for m in range(2)]

    wgq = place("wgq", [128, KC * I], BF16, 0)
    wuq = place("wuq", [128, KC * I], BF16, 44)
    wdq = place("wdq", [128, ICN * D], BF16, 88)
    rb = place("rb", [128, I], F32, 132)
    wtc0 = place("wtc0", [128, I], F32, 143)
    avq = place("avq", [128, I], F32, 154)
    sgnq = place("sgnq", [128, I], BF16, 165)
    t2q = place("t2q", [128, I], BF16, 171)
    natd2 = place("natd2", [128, I], F32, 165)
    natd3 = place("natd3", [128, I], F32, 132)

    xl0 = place("xl0", [128, KC * TT], F32, 88)
    xl1 = place("xl1", [128, KC * TT], F32, 104)
    xqb2 = [place("xqb0", [128, KC * TT], BF16, 120),
            place("xqb1", [128, KC * TT], BF16, 172)]
    hb = [place(f"hb{i}", [128, TT], F32, 128 + 2 * i) for i in range(4)]
    silu0 = place("silu0", [128, TT], F32, 136)
    silu1 = place("silu1", [128, TT], F32, 138)
    avf = place("avf", [128, KC * TT], F32, 140)
    sgnf = place("sgnf", [128, KC * TT], BF16, 156)
    t2f = place("t2f", [128, KC * TT], BF16, 164)

    wdq2 = place("wdq2", [128, ICN * D], BF16, 0)
    hq0 = place("hq0", [128, ICN * TT], BF16, 44)
    hq1 = place("hq1", [128, ICN * TT], BF16, 66)
    hl = [place(f"hl{i}", [128, TT], F32, 88 + 2 * i) for i in range(4)]
    xl20 = place("xl20", [128, KC * TT], F32, 96)
    xl21 = place("xl21", [128, KC * TT], F32, 112)
    yout = place("yout", [128, KC * TT], F32, 128)
    ysq2 = place("ysq2", [128, KC * TT], F32, 144)
    avf2 = [place(f"avf2_{i}", [128, TT], F32, 160 + 4 * i) for i in range(2)]
    sgnf2 = [place(f"sgnf2_{i}", [128, TT], BF16, 162 + 4 * i) for i in range(2)]
    t2f2 = [place(f"t2f2_{i}", [128, TT], BF16, 163 + 4 * i) for i in range(2)]

    p = Plan(nc)
    for s in ["LC", "LW", "LX0", "LX1", "LQ", "LH0", "LH1", "LH2", "LH3",
              "SX", "SH0", "SH1", "SH2", "SH3", "SW", "SC", "SO",
              "ZG", "TE", "VE", "SE", "GE", "GP"]:
        p.sem(s)
    G = p.g

    one = consts[:, C_ONE:C_ONE + 1]
    czero = consts[:, C_ZERO:C_ZERO + 1]

    def ld(dst, src, stream, waits=()):
        p.op("sync", lambda e, d=dst, s=src: e.dma_start(out=d, in_=s),
             sem=stream, by=16, waits=waits)

    def st(dst, src, stream, waits=()):
        p.op("gpsimd", lambda e, d=dst, s=src:
             e.dma_start(out=d, in_=s, allow_slow_non_contiguous=True),
             sem=stream, by=16, waits=waits)

    def ln_emit(ytile, ysqtile, outtile, outcb, psA, psB):
        p.op("scalar", lambda e, d=ysqtile, s=ytile:
             e.activation(d, s, AF.Square, bias=czero, scale=one),
             sem="SE", waits=[("VE", G["y_ve"])])
        ysq_se = p.c("SE")
        for kc in range(KC):
            p.op("tensor", lambda e, o=psA[0:1, :], l=one,
                 r=ytile[:, kc * TT:(kc + 1) * TT], s=(kc == 0), z=(kc == KC - 1):
                 e.matmul(o, l, r, start=s, stop=z),
                 sem="TE" if kc == KC - 1 else None,
                 waits=([("VE", G["y_ve"]), ("VE", G.get("ln_ps_free", 0)),
                         ("SE", G.get("ln_ps_free_se", 0))] if kc == 0 else []))
        for kc in range(KC):
            p.op("tensor", lambda e, o=psB[0:1, :], l=one,
                 r=ysqtile[:, kc * TT:(kc + 1) * TT], s=(kc == 0), z=(kc == KC - 1):
                 e.matmul(o, l, r, start=s, stop=z),
                 sem="TE" if kc == KC - 1 else None,
                 waits=([("SE", ysq_se)] if kc == 0 else []))
        te_stats = p.c("TE")
        p.op("vector", lambda e: e.tensor_scalar(zst[0:1, :], psA[0:1, :],
                                                 1.0 / D, None, ALU.mult),
             sem="VE", waits=[("TE", te_stats)])
        p.op("vector", lambda e: e.tensor_tensor(rsrow, zst[0:1, :], zst[0:1, :],
                                                 ALU.mult), sem="VE")
        p.op("vector", lambda e: e.scalar_tensor_tensor(
            rsrow, psB[0:1, :], 1.0 / D, rsrow, ALU.mult, ALU.subtract), sem="VE")
        ve_var = p.c("VE")
        p.op("scalar", lambda e: e.activation(nurow, rsrow, AF.Ln,
                                              bias=consts[0:1, C_EPS:C_EPS + 1],
                                              scale=one[0:1]),
             sem="SE", waits=[("VE", ve_var)])
        p.op("scalar", lambda e: e.activation(rsrow, nurow, AF.Exp,
                                              bias=czero[0:1],
                                              scale=consts[0:1, C_MHALF:C_MHALF + 1]),
             sem="SE")
        se_rs = p.c("SE")
        p.op("vector", lambda e: e.tensor_tensor(nurow, zst[0:1, :], rsrow, ALU.mult),
             sem="VE", waits=[("SE", se_rs)])
        ve_nu = p.c("VE")
        p.op("tensor", lambda e, o=psA: e.matmul(o, onesrow, rsrow,
                                                 start=True, stop=True),
             sem="TE", waits=[("VE", ve_nu)])
        p.op("tensor", lambda e, o=psB: e.matmul(o, onesrow, nurow,
                                                 start=True, stop=True), sem="TE")
        te_bc = p.c("TE")
        for oc in range(KC):
            p.op("vector", lambda e, d=sc512, a=ytile[:, oc * TT:(oc + 1) * TT], b=psA:
                 e.tensor_tensor(d, a, b, ALU.mult),
                 sem="VE", waits=[("TE", te_bc)])
            p.op("vector", lambda e, d=outtile[:, oc * TT:(oc + 1) * TT],
                 a=sc512, b=psB:
                 e.tensor_tensor(d, a, b, ALU.subtract), sem="VE")
            outcb(oc, p.c("VE"))
        G["ln_ps_free"] = p.c("VE")
        G["ln_ps_free_se"] = p.c("SE")

    for dst, src in [(consts, consts_i), (onesrow, onesrow_i), (ident, ident_i),
                     (rf_rep, rf_rep_i), (rf_bd, rf_bd_i),
                     (maskh, maskh_i), (sel_all, sel_i),
                     (clip_sb, clip2)]:
        ld(dst, src, "LC")
    setup_ld = p.c("LC")
    p.op("scalar", lambda e: e.activation(sig2, clip_sb, AF.Sigmoid,
                                          bias=czero[0:1], scale=one[0:1]),
         sem="SE", waits=[("LC", setup_ld)])
    p.op("vector", lambda e: e.memset(kvacc, 0.0), sem="VE")
    p.op("vector", lambda e: e.memset(zall, 0.0), sem="VE")
    kv_zero_ve = p.c("VE")

    for kc in range(KC):
        ld(wk_sb[:, kc * D:(kc + 1) * D], wk_t[kc * 128:(kc + 1) * 128, :], "LW")
        ld(wv_sb[:, kc * D:(kc + 1) * D], wv_t[kc * 128:(kc + 1) * 128, :], "LW")
    a_w_ld = p.c("LW")

    if stop == "A0":
        _finish(nc, p)
        return nc

    G["xa"] = {0: 0, 1: 0}
    G["ak_ve"] = {0: 0, 1: 0}
    G["ak_se"] = {0: 0, 1: 0}
    G["av_ve"] = {0: 0, 1: 0}
    G["akf_te"] = {0: 0, 1: 0, 2: 0, 3: 0}
    G["aphi"] = {0: 0, 1: 0, 2: 0, 3: 0}
    G["apt2"] = {0: 0, 1: 0, 2: 0, 3: 0}
    G["axa_te"] = 0
    G["asgn_gp"] = 0
    G["as1_gp"] = {0: 0, 1: 0}
    G["akvp"] = {0: 0, 1: 0}
    G["av65_prev"] = 0
    G["anorm_se"] = 0
    G["a_pre_se"] = 0

    for t in range(NT if stop != "A1" else 1):
        xt = xa0
        lx = "LX%d" % (t % 2)
        for kc in range(KC):
            ld(xt[:, kc * TT:(kc + 1) * TT],
               x_t[kc * 128:(kc + 1) * 128, t * TT:(t + 1) * TT], lx,
               waits=[("TE", G["axa_te"])])
        x_ld = p.c(lx)

        for oc in range(KC):
            pk = PS[oc % 2]
            for kc in range(KC):
                lhs = wk_sb[:, kc * D + oc * 128: kc * D + (oc + 1) * 128]
                rhs = xt[:, kc * TT:(kc + 1) * TT]
                p.op("tensor",
                     lambda e, o=pk, l=lhs, r=rhs, s=(kc == 0), z=(kc == KC - 1):
                     e.matmul(o, l, r, start=s, stop=z),
                     sem="TE" if kc == KC - 1 else None,
                     waits=([(lx, x_ld), ("LW", a_w_ld),
                             ("VE", G["ak_ve"][oc % 2]),
                             ("SE", G["ak_se"][oc % 2])] if kc == 0 else []))
            te_k = p.c("TE")
            p.op("vector", lambda e, d=ktile[:, oc * TT:(oc + 1) * TT], s=pk:
                 e.tensor_copy(d, s), sem="VE", waits=[("TE", te_k)])
            ve_kcp = p.c("VE")
            p.op("scalar", lambda e, d=ksq[:, oc * TT:(oc + 1) * TT],
                 s=ktile[:, oc * TT:(oc + 1) * TT]:
                 e.activation(d, s, AF.Square, bias=czero, scale=one),
                 sem="SE", waits=[("VE", ve_kcp)])
            G["ak_ve"][oc % 2] = p.c("VE")
            G["ak_se"][oc % 2] = p.c("SE")
        k_evac_ve = p.c("VE")
        k_evac_se = p.c("SE")

        for tb in range(NTB):
            if "n" not in amask:
                continue
            pn = PS[6]
            for kc in range(KC):
                lhs = ksq[:, kc * TT + tb * TB: kc * TT + (tb + 1) * TB]
                rhs = maskh[:, kc * H:(kc + 1) * H]
                p.op("tensor",
                     lambda e, o=pn[:, 0:H], l=lhs, r=rhs, s=(kc == 0), z=(kc == KC - 1):
                     e.matmul(o, l, r, start=s, stop=z),
                     sem="TE" if kc == KC - 1 else None,
                     waits=([("SE", k_evac_se), ("SE", G["anorm_se"]),
                             ("VE", G["akvp"][1])]
                            if kc == 0 else []))
            te_n = p.c("TE")
            p.op("scalar", lambda e, d=scale_a[:, tb * H:(tb + 1) * H], s=pn[:, 0:H]:
                 e.activation(d, s, AF.Exp,
                              bias=consts[:, C_MLN16:C_MLN16 + 1],
                              scale=consts[:, C_MHALF:C_MHALF + 1]),
                 sem="SE", waits=[("TE", te_n), ("VE", G["av65_prev"])])
            G["anorm_se"] = p.c("SE")
            se_scale = p.c("SE")

            if "v" not in amask:
                continue
            vb = v65a[:, tb * H * 65:(tb + 1) * H * 65]
            for nh in range(2):
                pv = PS[nh]
                for kc in range(KC):
                    lhs = xt[:, kc * TT + tb * TB: kc * TT + (tb + 1) * TB]
                    rhs = wv_sb[:, kc * D + nh * 512: kc * D + (nh + 1) * 512]
                    p.op("tensor",
                         lambda e, o=pv, l=lhs, r=rhs, s=(kc == 0), z=(kc == KC - 1):
                         e.matmul(o, l, r, start=s, stop=z),
                         sem="TE" if kc == KC - 1 else None,
                         waits=([("VE", G["av_ve"][nh]), ("VE", k_evac_ve),
                                 ("SE", k_evac_se)] if kc == 0 else []))
                te_v = p.c("TE")
                for hh in range(8):
                    h = nh * 8 + hh
                    p.op("vector",
                         lambda e, d=vb[:, h * 65: h * 65 + 64],
                         s=pv[:, hh * 64:(hh + 1) * 64],
                         sc=scale_a[:, tb * H + h:tb * H + h + 1]:
                         e.tensor_scalar(d, s, sc, None, ALU.mult),
                         sem="VE", waits=[("TE", te_v), ("SE", se_scale)])
                G["av_ve"][nh] = p.c("VE")
            p.op("vector",
                 lambda e, d=vb.rearrange("p (h c) -> p h c", c=65)[:, :, 64:65],
                 s=scale_a[:, tb * H:(tb + 1) * H].rearrange(
                     "p (h o) -> p h o", o=1):
                 e.tensor_copy(d, s), sem="VE")
        v65_ve = p.c("VE")
        G["a_pre_se"] = p.c("SE")
        G["axa_te"] = p.c("TE")

        if "v" in amask and "n" in amask:
            items = [(hp, tb) for hp in range(8) for tb in range(NTB)]
            te_phi = {}
            se_cos = {}
            gp_sin = {}
            te_stop = {}

            def emit_phi(i):
                hp, tb = items[i]
                par = i % 4
                pph = PS[(2, 3, 0, 1)[par]]
                lhs = ktile[:, hp * TT + tb * TB: hp * TT + (tb + 1) * TB]
                p.op("tensor", lambda e, o=pph, l=lhs:
                     e.matmul(o, l, rf_bd, start=True, stop=True),
                     sem="TE",
                     waits=[("VE", G["aphi"][par]), ("VE", k_evac_ve),
                            ("VE", v65_ve)])
                te_phi[i] = p.c("TE")

            def emit_epilogue(i):
                par = i % 4
                pph = PS[(2, 3, 0, 1)[par]]
                kfb = kfps[par]
                p0m = ap0[par]
                s1m = as1[i % 2]
                p.op("vector", lambda e, d=pti, s=pph:
                     e.tensor_scalar(d, s, 1.0 / (2 * PI), None, ALU.mult),
                     sem="VE", waits=[("TE", te_phi[i])])
                p.op("vector", lambda e, d=p0m, a=pti, s=pph:
                     e.scalar_tensor_tensor(d, a, -2 * PI, s, ALU.mult, ALU.add),
                     sem="VE", waits=[("SE", G["apt2"][par])])
                G["aphi"][par] = p.c("VE")
                ve_p0 = p.c("VE")
                p.op("vector", lambda e, d=actc, s=p0m:
                     e.tensor_scalar(d, s, 0.0, None, ALU.is_ge), sem="VE")
                p.op("vector", lambda e, d=asgn, s=actc:
                     e.tensor_scalar(d, s, 2.0, -1.0, ALU.mult, ALU.add),
                     sem="VE", waits=[("GP", G["asgn_gp"])])
                ve_sgn = p.c("VE")
                p.op("scalar", lambda e, d=au, s=p0m:
                     e.activation(d, s, AF.Abs, bias=czero, scale=one),
                     sem="SE", waits=[("VE", ve_p0)])
                p.op("scalar", lambda e, d=s1m, s=au:
                     e.activation(d, s, AF.Sin, bias=consts[:, C_PI:C_PI + 1],
                                  scale=consts[:, C_MONE:C_MONE + 1]),
                     sem="SE", waits=[("GP", G["as1_gp"][i % 2])])
                se_s1 = p.c("SE")
                gp_h = []
                se_c = []

                def gp_half(hh):
                    p.op("gpsimd", lambda e,
                         d=kfb[:, hh * M2 + MF:(hh + 1) * M2],
                         a=asgn[:, hh * MF:(hh + 1) * MF],
                         b=s1m[:, hh * MF:(hh + 1) * MF]:
                         e.tensor_tensor(d, a, b, ALU.mult), sem="GP",
                         waits=[("SE", se_s1), ("VE", ve_sgn),
                                ("GP", p.c("GP")), ("TE", G["akf_te"][par])])
                    gp_h.append(p.c("GP"))

                def cos_half(hh):
                    p.op("scalar", lambda e,
                         d=kfb[:, hh * M2:hh * M2 + MF],
                         s=p0m[:, hh * MF:(hh + 1) * MF]:
                         e.activation(d, s, AF.Sin,
                                      bias=consts[:, C_MHPI:C_MHPI + 1],
                                      scale=one),
                         sem="SE", waits=[("TE", G["akf_te"][par])])
                    se_c.append(p.c("SE"))

                gp_half(0)
                p.op("scalar", lambda e, d=p0m, s=au:
                     e.activation(d, s, AF.Abs, bias=consts[:, C_MPI:C_MPI + 1],
                                  scale=one),
                     sem="SE", waits=[("VE", ve_sgn)])
                cos_half(0)
                gp_half(1)
                cos_half(1)
                gp_sin[i] = tuple(gp_h)
                se_cos[i] = tuple(se_c)
                G["asgn_gp"] = gp_h[1]
                G["as1_gp"][i % 2] = gp_h[1]
                G["apt2"][par] = se_c[1]

            def emit_kv(i):
                hp, tb = items[i]
                par = i % 4
                kfb = kfps[par]
                bank = 4 + 2 * (hp % 2)
                for j in range(2):
                    h = 2 * hp + j
                    pk = PS[bank + j]
                    w = [("SE", se_cos[i][j]), ("GP", gp_sin[i][j])]
                    if tb == 0:
                        w += [("VE", G["akvp"][hp % 2]), ("VE", v65_ve),
                              ("SE", G["a_pre_se"])]
                    p.op("tensor", lambda e, o=pk[0:65, :],
                         l=v65a[:, (tb * H + h) * 65:(tb * H + h) * 65 + 65],
                         r=kfb[:, j * M2:(j + 1) * M2]:
                         e.matmul(o, l, r, start=(tb == 0), stop=(tb == NTB - 1)),
                         sem="TE" if j == 1 else None, waits=w)
                G["akf_te"][par] = p.c("TE")
                if tb == NTB - 1:
                    te_stop[hp] = p.c("TE")

            def emit_flush(hp):
                bank = 4 + 2 * (hp % 2)
                for j in range(2):
                    h = 2 * hp + j
                    p.op("vector", lambda e, d=kvacc[0:65, h * 512:(h + 1) * 512],
                         s=PS[bank + j][0:65, :]:
                         e.tensor_tensor(d, d, s, ALU.add),
                         sem="VE", waits=[("TE", te_stop[hp]), ("VE", kv_zero_ve)])
                G["akvp"][hp % 2] = p.c("VE")

            emit_phi(0)
            emit_phi(1)
            pend_flush = None
            for i, (hp, tb) in enumerate(items):
                if i + 2 < len(items):
                    emit_phi(i + 2)
                emit_epilogue(i)
                if pend_flush is not None:
                    emit_flush(pend_flush)
                    pend_flush = None
                emit_kv(i)
                if tb == NTB - 1:
                    pend_flush = hp
            if pend_flush is not None:
                emit_flush(pend_flush)
        G["xa"][t % 2] = p.c("TE")
        G["ak_ve"] = {0: p.c("VE"), 1: p.c("VE")}
        G["ak_se"] = {0: p.c("SE"), 1: p.c("SE")}
        G["av_ve"] = {0: p.c("VE"), 1: p.c("VE")}
        G["av65_prev"] = p.c("VE")

    p.op("vector", lambda e: e.memset(kvfm, 0.0), sem="VE",
         waits=[("TE", p.c("TE"))])
    for h in (range(H) if "t" in amask else []):
        for fc in range(4):
            pt_ps = PS[6]
            p.op("tensor", lambda e, o=pt_ps[:, 0:65],
                 i=kvacc[0:65, h * 512 + fc * 128: h * 512 + (fc + 1) * 128]:
                 e.transpose(o, i, ident[0:65, 0:65]),
                 sem="TE", waits=[("VE", p.c("VE")), ("SE", p.c("SE"))])
            te_t = p.c("TE")
            col = (h * 4 + fc) * 65
            if h % 2 == 0:
                p.op("vector", lambda e, d=kvfm[:, col:col + 64], s=pt_ps[:, 0:64]:
                     e.tensor_copy(d, s), sem="VE", waits=[("TE", te_t)])
                p.op("vector", lambda e, d=kvfm[:, col + 64:col + 65],
                     s=pt_ps[:, 64:65]:
                     e.tensor_copy(d, s), sem="VE")
            else:
                p.op("vector", lambda e, d=kvfm[:, col:col + 1],
                     s=pt_ps[:, 64:65]:
                     e.tensor_copy(d, s), sem="VE", waits=[("TE", te_t)])
                p.op("vector", lambda e, d=kvfm[:, col + 1:col + 65],
                     s=pt_ps[:, 0:64]:
                     e.tensor_copy(d, s), sem="VE")
    kvfm_ve = p.c("VE")

    if stop in ("A", "A1"):
        if debug:
            st(dbg[:, 0:H * 512], kvacc, "SO", waits=[("VE", kvfm_ve)])
            st(dbg[:, 8192:8192 + H * 4 * 65], kvfm, "SO", waits=[("VE", kvfm_ve)])
            st(dbg[:, 14336:14336 + 2048], ktile[:, 0:2048], "SO",
               waits=[("VE", kvfm_ve)])
        _finish(nc, p)
        return nc

    p.barrier_eng("sync")
    for kc in range(KC):
        ld(wq_sb[:, kc * D:(kc + 1) * D], wq_t[kc * 128:(kc + 1) * 128, :], "LW")
        ld(wo_sb[:, kc * D:(kc + 1) * D], wo_t[kc * 128:(kc + 1) * 128, :], "LW")
    b_w_ld = p.c("LW")

    G["xb"] = {0: 0, 1: 0}
    G["bxb"] = 0
    G["bqt_ve"] = {0: 0, 1: 0}
    G["bq_ve"] = {0: 0, 1: 0}
    G["bq_se"] = {0: 0, 1: 0}
    G["bqf_te"] = {0: 0, 1: 0}
    G["bqkv_ve"] = {0: 0, 1: 0}
    G["bphiB"] = {2: 0, 3: 0, 6: 0, 7: 0}
    G["bp0"] = {0: 0, 1: 0}
    G["bsgn_gp"] = {0: 0, 1: 0}
    G["bz_zg"] = {}
    G["bn_ve"] = 0
    G["brz_ve"] = 0
    G["bw_ve"] = {0: 0, 1: 0}
    G["y_ve"] = 0

    for t in range(NT):
        xt = xb0
        qtile = qtiles[t % 2]
        lx = "LX%d" % (t % 2)
        for kc in range(KC):
            ld(xt[:, kc * TT:(kc + 1) * TT],
               x_t[kc * 128:(kc + 1) * 128, t * TT:(t + 1) * TT], lx,
               waits=[("VE", kvfm_ve), ("VE", G["bxb"])])
        x_ld = p.c(lx)

        for oc in range(KC):
            pq = PS[oc % 2]
            for kc in range(KC):
                lhs = wq_sb[:, kc * D + oc * 128: kc * D + (oc + 1) * 128]
                rhs = xt[:, kc * TT:(kc + 1) * TT]
                p.op("tensor",
                     lambda e, o=pq, l=lhs, r=rhs, s=(kc == 0), z=(kc == KC - 1):
                     e.matmul(o, l, r, start=s, stop=z),
                     sem="TE" if kc == KC - 1 else None,
                     waits=([(lx, x_ld), ("LW", b_w_ld),
                             ("VE", G["bq_ve"][oc % 2]),
                             ("SE", G["bq_se"][oc % 2])] if kc == 0 else []))
            te_q = p.c("TE")
            p.op("vector", lambda e, d=qtile[:, oc * TT:(oc + 1) * TT], s=pq:
                 e.tensor_copy(d, s), sem="VE",
                 waits=[("TE", te_q), ("VE", G["bqt_ve"][t % 2])])
            ve_qcp = p.c("VE")
            p.op("scalar", lambda e, d=qsq[:, oc * TT:(oc + 1) * TT],
                 s=qtile[:, oc * TT:(oc + 1) * TT]:
                 e.activation(d, s, AF.Square, bias=czero, scale=one),
                 sem="SE", waits=[("VE", ve_qcp), ("SX", G.get("sx_prev", 0))])
            G["bq_ve"][oc % 2] = p.c("VE")
            G["bq_se"][oc % 2] = p.c("SE")
        q_evac_ve = p.c("VE")
        q_evac_se = p.c("SE")

        for tb in range(NTB):
            pn = PS[6]
            for kc in range(KC):
                lhs = qsq[:, kc * TT + tb * TB: kc * TT + (tb + 1) * TB]
                rhs = maskh[:, kc * H:(kc + 1) * H]
                p.op("tensor",
                     lambda e, o=pn[:, 0:H], l=lhs, r=rhs, s=(kc == 0), z=(kc == KC - 1):
                     e.matmul(o, l, r, start=s, stop=z),
                     sem="TE" if kc == KC - 1 else None,
                     waits=([("SE", q_evac_se), ("VE", G["bn_ve"]),
                             ("SE", G.get("bthr_se", 0))]
                            if kc == 0 else []))
            te_n = p.c("TE")
            p.op("vector", lambda e, d=sc512[:, 0:H], s=pn[:, 0:H]:
                 e.tensor_copy(d, s), sem="VE", waits=[("TE", te_n)])
            ve_nq = p.c("VE")
            p.op("tensor", lambda e, o=pn[0:H, 0:TB], i=sc512[:, 0:H]:
                 e.transpose(o, i, ident), sem="TE", waits=[("VE", ve_nq)])
            te_tr = p.c("TE")
            p.op("vector", lambda e, d=sc512[0:H, 0:TB], s=pn[0:H, 0:TB]:
                 e.tensor_scalar(d, s, 0.5, LN_THR, ALU.mult, ALU.add),
                 sem="VE", waits=[("TE", te_tr)])
            p.op("vector", lambda e, d=sc512[0:H, 0:TB]:
                 e.tensor_scalar(d, d, 80.0, None, ALU.min), sem="VE")
            ve_thr = p.c("VE")
            p.op("scalar", lambda e, d=thr_fm[:, tb * TB:(tb + 1) * TB],
                 s=sc512[0:H, 0:TB]:
                 e.activation(d, s, AF.Exp, bias=czero[0:H], scale=one[0:H]),
                 sem="SE", waits=[("VE", ve_thr), ("TE", te_tr)])
            G["bthr_se"] = p.c("SE")
            G["bn_ve"] = p.c("VE")
        thr_se = p.c("SE")

        def b_flush(pend):
            h_, te_qkv_, oc_ = pend
            even_ = (h_ % 2 == 0)
            pk_ = PS[4] if even_ else PS[5]
            if even_:
                pay = pk_[0:64, :]; zrow = pk_[64:65, :]
                drow = attn[0:64, oc_ * TT:(oc_ + 1) * TT]
                zdst = zst[64:65, 0:TT]
            else:
                pay = pk_[64:128, :]; zrow = pk_[0:1, :]
                drow = attn[64:128, oc_ * TT:(oc_ + 1) * TT]
                zdst = zst[0:1, 0:TT]
            p.op("vector", lambda e, d=drow, s=pay: e.tensor_copy(d, s),
                 sem="VE", waits=[("TE", te_qkv_)])
            p.op("vector", lambda e, d=zdst, s=zrow: e.tensor_copy(d, s),
                 sem="VE", waits=[("ZG", G["bz_zg"].get(h_ // 2, 0))])
            ve_z = p.c("VE")
            p.op("scalar", lambda e, d=zall[h_:h_ + 1, :], s=zdst:
                 e.dma_start(out=d, in_=s), sem="ZG", by=16,
                 waits=[("VE", ve_z)])
            G["bqkv_ve"][h_ % 2] = p.c("VE")
            if h_ % 2 == 1:
                G["bz_zg"][h_ // 2 + 1] = p.c("ZG")
        te_phiH = {}
        se_cosH = {}
        gp_sinH = {}

        def b_phi(h):
            oc_, hi_ = h // 2, h % 2
            base = 0 if hi_ == 0 else 64
            banks = (2, 3) if h % 2 == 0 else (6, 7)
            tp = (0, 0) if hi_ == 0 else (64, 0)
            tps = []
            for mc in range(2):
                pps = PS[banks[mc]]
                lhs = rf_rep[base:base + 64, mc * 128:(mc + 1) * 128]
                rhs = qtile[base:base + 64, oc_ * TT:(oc_ + 1) * TT]
                p.op("tensor", lambda e, o=pps, l=lhs, r=rhs, tp_=tp:
                     e.matmul(o, l, r, start=True, stop=True, tile_position=tp_),
                     sem="TE",
                     waits=[("VE", G["bphiB"][banks[mc]]), ("VE", q_evac_ve)])
                tps.append(p.c("TE"))
            te_phiH[h] = tps

        def b_epi(h):
            qf = qf0 if h % 2 == 0 else qf1
            banks = (2, 3) if h % 2 == 0 else (6, 7)
            for mc in range(2):
                pps = PS[banks[mc]]
                p0m, um, sgnm = bp0[mc], bu[mc], bsgn[mc]
                s1m, a2m = bs1[mc], ba2[mc]
                p.op("vector", lambda e, d=ptbi, s=pps:
                     e.tensor_scalar(d, s, 1.0 / (2 * PI), None, ALU.mult),
                     sem="VE", waits=[("TE", te_phiH[h][mc])])
                p.op("vector", lambda e, d=p0m, a=ptbi, s=pps:
                     e.scalar_tensor_tensor(d, a, -2 * PI, s, ALU.mult, ALU.add),
                     sem="VE", waits=[("SE", G["bp0"][mc])])
                G["bphiB"][banks[mc]] = p.c("VE")
                ve_p0 = p.c("VE")
                p.op("vector", lambda e, d=actcB, s=p0m:
                     e.tensor_scalar(d, s, 0.0, None, ALU.is_ge), sem="VE")
                p.op("vector", lambda e, d=sgnm, s=actcB:
                     e.tensor_scalar(d, s, 2.0, -1.0, ALU.mult, ALU.add),
                     sem="VE", waits=[("GP", G["bsgn_gp"][mc])])
                ve_sgn = p.c("VE")
                p.op("scalar", lambda e, d=um, s=p0m:
                     e.activation(d, s, AF.Abs, bias=czero, scale=one),
                     sem="SE", waits=[("VE", ve_p0)])
                G["bp0"][mc] = p.c("SE")
                p.op("scalar", lambda e, d=s1m, s=um:
                     e.activation(d, s, AF.Sin, bias=consts[:, C_PI:C_PI + 1],
                                  scale=consts[:, C_MONE:C_MONE + 1]),
                     sem="SE", waits=[("GP", G["bsgn_gp"][mc])])
                se_s1 = p.c("SE")
                p.op("gpsimd", lambda e, d=qf[:, (2 + mc) * TT:(3 + mc) * TT],
                     a=sgnm, b=s1m:
                     e.tensor_tensor(d, a, b, ALU.mult), sem="GP",
                     waits=[("SE", se_s1), ("VE", ve_sgn), ("GP", p.c("GP")),
                            ("TE", G["bqf_te"][h % 2])])
                G["bsgn_gp"][mc] = p.c("GP")
                p.op("scalar", lambda e, d=a2m, s=um:
                     e.activation(d, s, AF.Abs, bias=consts[:, C_MPI:C_MPI + 1],
                                  scale=one), sem="SE")
                p.op("scalar", lambda e, d=qf[:, mc * TT:(mc + 1) * TT], s=a2m:
                     e.activation(d, s, AF.Sin, bias=consts[:, C_MHPI:C_MHPI + 1],
                                  scale=one),
                     sem="SE", waits=[("TE", G["bqf_te"][h % 2])])
                se_cosH[(h, mc)] = p.c("SE")
                gp_sinH[(h, mc)] = p.c("GP")
            se_cosH[h] = p.c("SE")
            gp_sinH[h] = p.c("GP")

        def b_qkv(h):
            qf = qf0 if h % 2 == 0 else qf1
            even = (h % 2 == 0)
            pk = PS[4] if even else PS[5]

            def fcw(fc):
                if fc < 2:
                    w = [("SE", se_cosH[(h, fc)])]
                else:
                    w = [("GP", gp_sinH[(h, fc - 2)])]
                if fc == 0:
                    w.append(("VE", G["bqkv_ve"][h % 2]))
                return w
            if even:
                for fc in range(4):
                    lhs = kvfm[:, (h * 4 + fc) * 65:(h * 4 + fc) * 65 + 65]
                    rhs = qf[:, fc * TT:(fc + 1) * TT]
                    p.op("tensor",
                         lambda e, o=pk[0:65, :], l=lhs, r=rhs,
                         s=(fc == 0), z=(fc == 3):
                         e.matmul(o, l, r, start=s, stop=z),
                         sem="TE" if fc == 3 else None, waits=fcw(fc))
            else:
                for fc in range(4):
                    lhs = kvfm[:, (h * 4 + fc) * 65 + 1:(h * 4 + fc) * 65 + 65]
                    rhs = qf[:, fc * TT:(fc + 1) * TT]
                    p.op("tensor",
                         lambda e, o=pk[64:128, :], l=lhs, r=rhs,
                         s=(fc == 0), z=(fc == 3):
                         e.matmul(o, l, r, start=s, stop=z),
                         sem=None, waits=fcw(fc))
                for fc in range(4):
                    lhs = kvfm[:, (h * 4 + fc) * 65:(h * 4 + fc) * 65 + 1]
                    rhs = qf[:, fc * TT:(fc + 1) * TT]
                    p.op("tensor",
                         lambda e, o=pk[0:1, :], l=lhs, r=rhs,
                         s=(fc == 0), z=(fc == 3):
                         e.matmul(o, l, r, start=s, stop=z),
                         sem="TE" if fc == 3 else None)
            te_qkv = p.c("TE")
            G["bqf_te"][h % 2] = te_qkv
            return te_qkv

        G["bphiB"][6] = p.c("VE")
        b_phi(0)
        pend = None
        for h in range(H):
            if h + 1 < H:
                b_phi(h + 1)
            b_epi(h)
            if pend is not None:
                b_flush(pend)
                pend = None
            te_qkv = b_qkv(h)
            pend = (h, te_qkv, h // 2)
        b_flush(pend)
        pend = None
        zg_all = p.c("ZG")
        attn_ve = p.c("VE")

        p.op("vector", lambda e: e.tensor_tensor(zall[0:H, :], zall[0:H, :], thr_fm, ALU.max),
             sem="VE", waits=[("ZG", zg_all), ("SE", thr_se)])
        p.op("vector", lambda e: e.reciprocal(zall[0:H, :], zall[0:H, :]), sem="VE")
        rz_ve = p.c("VE")

        for oc in range(KC):
            prz = PS[7]
            p.op("tensor", lambda e, o=prz, l=sel_all[:, oc * 128:(oc + 1) * 128],
                 r=zall:
                 e.matmul(o, l, r, start=True, stop=True),
                 sem="TE", waits=[("VE", rz_ve), ("VE", G["brz_ve"])])
            te_rz = p.c("TE")
            p.op("vector", lambda e, d=attn[:, oc * TT:(oc + 1) * TT], s=prz:
                 e.tensor_tensor(d, d, s, ALU.mult),
                 sem="VE", waits=[("TE", te_rz), ("VE", attn_ve)])
            G["brz_ve"] = p.c("VE")
        G["bphiB"][7] = p.c("VE")
        attn_scaled_ve = p.c("VE")

        for oc in range(KC):
            pw = PS[oc % 2]
            for kc in range(KC):
                lhs = wo_sb[:, kc * D + oc * 128: kc * D + (oc + 1) * 128]
                rhs = attn[:, kc * TT:(kc + 1) * TT]
                p.op("tensor",
                     lambda e, o=pw, l=lhs, r=rhs, s=(kc == 0), z=(kc == KC - 1):
                     e.matmul(o, l, r, start=s, stop=z),
                     sem="TE" if kc == KC - 1 else None,
                     waits=([("VE", attn_scaled_ve), ("VE", G["bw_ve"][oc % 2])]
                            if kc == 0 else []))
            te_w = p.c("TE")
            p.op("vector", lambda e, d=qtile[:, oc * TT:(oc + 1) * TT],
                 s=pw, xx=xt[:, oc * TT:(oc + 1) * TT]:
                 e.tensor_tensor(d, xx, s, ALU.add),
                 sem="VE", waits=[("TE", te_w)])
            G["bw_ve"][oc % 2] = p.c("VE")
        G["y_ve"] = p.c("VE")
        G["bxb"] = p.c("VE")

        def ln1_out(oc, ve_x, _t=t):
            st(xln_s[oc * 128:(oc + 1) * 128, _t * TT:(_t + 1) * TT],
               qsq[:, oc * TT:(oc + 1) * TT], "SX", waits=[("VE", ve_x)])
            p.op("vector", lambda e, d=xmaxst[:, _t * KC + oc: _t * KC + oc + 1],
                 s=qsq[:, oc * TT:(oc + 1) * TT]:
                 e.tensor_reduce(d, s, AX.X, ALU.max, apply_absolute_value=True),
                 sem="VE")
        ln_emit(qtile, qsq, qsq, ln1_out, PS[6], PS[7])
        G["sx_prev"] = p.c("SX")
        G["bqt_ve"][t % 2] = p.c("VE")
        G["bn_ve"] = p.c("VE")
        G["brz_ve"] = p.c("VE")
        G["bw_ve"] = {0: p.c("VE"), 1: p.c("VE")}
        G["bqkv_ve"] = {0: p.c("VE"), 1: p.c("VE")}
        G["bqf_te"] = {0: p.c("TE"), 1: p.c("TE")}
    b_st_all = p.c("SX")
    b_ve_all = p.c("VE")

    if stop == "B":
        if debug:
            st(dbg[:, 0:KC * TT], attn, "SO", waits=[("VE", b_ve_all)])
            st(dbg[0:H, 4096:4096 + TT], zall[0:H, :], "SO",
               waits=[("VE", b_ve_all)])
            st(dbg[0:H, 4608:4608 + TT], thr_fm, "SO", waits=[("VE", b_ve_all)])
            st(dbg[:, 5120:5120 + KC], xmaxst[:, (NT - 1) * KC:NT * KC],
               "SO", waits=[("VE", b_ve_all)])
        p.wait("gpsimd", [("SX", b_st_all)])
        for kc in range(KC):
            st(y_t[kc * 128:(kc + 1) * 128, :], xln_s[kc * 128:(kc + 1) * 128, :],
               "SO")
        _finish(nc, p)
        return nc

    p.op("vector", lambda e: e.tensor_reduce(m1col, xmaxst, AX.X, ALU.max),
         sem="VE", waits=[("VE", b_ve_all)])
    ve_m = p.c("VE")
    p.op("tensor", lambda e, o=PS[6][0:1, 0:128], i=m1col:
         e.transpose(o, i, ident), sem="TE",
         waits=[("VE", ve_m), ("SE", p.c("SE"))])
    te_m = p.c("TE")
    p.op("vector", lambda e: e.tensor_reduce(r1x1, PS[6][0:1, 0:128], AX.X, ALU.max),
         sem="VE", waits=[("TE", te_m)])
    p.op("vector", lambda e: e.tensor_scalar(ccst, onesrow[0:1, 0:8],
                                             r1x1[0:1, 0:1], None, ALU.mult),
         sem="VE")
    ve_cc = p.c("VE")
    p.op("gpsimd", lambda e: e.dma_start(out=cc_in1.ap(), in_=ccst),
         sem="SC", by=16, waits=[("VE", ve_cc)])
    p.wait("gpsimd", [("SC", p.c("SC"))])
    p.op("gpsimd", lambda e: e.collective_compute(
        "AllReduce", ALU.max, replica_groups=[[0, 1, 2, 3, 4, 5, 6, 7]],
        ins=[cc_in1.ap().opt()], outs=[cc_out1.ap().opt()]), sem="GE")
    p.wait("gpsimd", [("GE", p.c("GE"))])
    p.op("gpsimd", lambda e: e.dma_start(out=gmax1, in_=cc_out1.ap()[0:1, 0:1]),
         sem="SC", by=16)
    g1_st = p.c("SC")
    p.op("vector", lambda e: e.tensor_tensor(r1x1, gmax1, sig2[0:1, 0:1], ALU.mult),
         sem="VE", waits=[("SC", g1_st)])
    p.op("vector", lambda e: e.reciprocal(r1x1b, r1x1), sem="VE")
    p.op("vector", lambda e: e.tensor_scalar(r1x1b, r1x1b, 127.0, None, ALU.mult),
         sem="VE")
    p.op("vector", lambda e: e.tensor_scalar(r1x1, r1x1, INV127 * INV127, None,
                                             ALU.mult), sem="VE")
    ve_rx = p.c("VE")
    p.op("tensor", lambda e, o=PS[6][:, 0:1]: e.matmul(o, onesrow, r1x1b,
                                                       start=True, stop=True),
         sem="TE", waits=[("VE", ve_rx)])
    p.op("tensor", lambda e, o=PS[7][:, 0:1]: e.matmul(o, onesrow, r1x1,
                                                       start=True, stop=True),
         sem="TE")
    te_rx = p.c("TE")
    p.op("vector", lambda e: e.tensor_copy(rx_col, PS[6][:, 0:1]),
         sem="VE", waits=[("TE", te_rx)])
    p.op("vector", lambda e: e.tensor_copy(sxs_col, PS[7][:, 0:1]), sem="VE")
    rx_ve = p.c("VE")

    if stop == "AR1":
        if debug:
            st(dbg[0:1, 0:1], gmax1, "SO", waits=[("VE", rx_ve)])
            st(dbg[:, 4:5], rx_col, "SO", waits=[("VE", rx_ve)])
            st(dbg[:, 8:9], m1col, "SO", waits=[("VE", rx_ve)])
            st(dbg[0:1, 16:24], ccst, "SO", waits=[("VE", rx_ve)])
            st(dbg[:, 32:96], xmaxst, "SO", waits=[("VE", rx_ve)])
        _finish(nc, p)
        return nc

    p.barrier_eng("sync")

    def quant_weight(wnat, wtt, nch_nat, nat_w, sall, tgt, nbufs):
        wout = nch_nat * 128
        kcin = nat_w // 128
        nb = len(nbufs)
        G["fqn_ve"] = {i: 0 for i in range(nb)}
        for j in range(nch_nat):
            buf = nbufs[j % nb]
            lh = "LH%d" % (j % 4)
            ld(buf, wnat[j * 128:(j + 1) * 128, :], lh,
               waits=[("VE", G["fqn_ve"][j % nb]),
                      ("VE", G.get("fq_ve", 0)), ("VE", G.get("fqm_ve", 0)),
                      ("VE", G.get("fqq_ve", 0)),
                      ("SE", G.get("fq_se", 0))])
            ldj = p.c(lh)
            p.op("vector", lambda e, d=sall[:, j:j + 1], s=buf:
                 e.tensor_reduce(d, s, AX.X, ALU.max, apply_absolute_value=True),
                 sem="VE", waits=[(lh, ldj)])
            G["fqn_ve"][j % nb] = p.c("VE")
        G["fq_ve"] = p.c("VE")
        p.op("vector", lambda e, d=mred[:, 0:nch_nat], s=sall[:, 0:nch_nat]:
             e.reciprocal(d, s), sem="VE")
        p.op("vector", lambda e, d=mred[:, 0:nch_nat]:
             e.tensor_scalar(d, d, 127.0, None, ALU.mult), sem="VE")
        ve_r = p.c("VE")
        p.op("tensor", lambda e, o=PS[6][0:nch_nat, 0:128], i=mred[:, 0:nch_nat]:
             e.transpose(o, i, ident), sem="TE",
             waits=[("VE", ve_r), ("VE", G.get("fqb_ve", 0))])
        te_r = p.c("TE")
        p.op("vector", lambda e, d=tp22[0:nch_nat, :], s=PS[6][0:nch_nat, 0:128]:
             e.tensor_copy(d, s), sem="VE", waits=[("TE", te_r)])
        ve_t = p.c("VE")
        for j in range(nch_nat):
            p.op("scalar", lambda e, d=rb[0:1, j * 128:(j + 1) * 128],
                 s=tp22[j:j + 1, :]:
                 e.dma_start(out=d, in_=s), sem="ZG", by=16,
                 waits=[("VE", ve_t)])
        zg_r = p.c("ZG")
        nslab = (wout + 511) // 512
        for s_ in range(nslab):
            wlen = min(512, wout - s_ * 512)
            p.op("tensor", lambda e, o=PS[6][:, 0:wlen], l=onesrow,
                 r=rb[0:1, s_ * 512:s_ * 512 + wlen]:
                 e.matmul(o, l, r, start=True, stop=True),
                 sem="TE", waits=[("ZG", zg_r), ("VE", G.get("fqb_ve", 0))])
            te_b = p.c("TE")
            p.op("vector", lambda e, d=rb[:, s_ * 512:s_ * 512 + wlen],
                 s=PS[6][:, 0:wlen]:
                 e.tensor_copy(d, s), sem="VE", waits=[("TE", te_b)])
            G["fqb_ve"] = p.c("VE")
        rb_ve = p.c("VE")
        hw = wout // 2
        G["fqs"] = {0: 0, 1: 0}
        gidx = 0
        for kc in range(kcin):
            mults = []
            for gi in range(2):
                slot = gidx % 2
                gidx += 1
                sbuf = wtc0[:, slot * hw:(slot + 1) * hw]
                lh = "LH%d" % (gidx % 4)
                ld(sbuf, wtt[kc * 128:(kc + 1) * 128, gi * hw:(gi + 1) * hw],
                   lh, waits=[("VE", G["fqs"][slot]), ("VE", G.get("fq_ve", 0))])
                ldc = p.c(lh)
                p.op("vector", lambda e, d=avq[:, gi * hw:(gi + 1) * hw],
                     a=sbuf, b=rb[:, gi * hw:(gi + 1) * hw]:
                     e.tensor_tensor(d, a, b, ALU.mult),
                     sem="VE", waits=[(lh, ldc), ("VE", rb_ve),
                                      ("VE", G.get("fqq_ve", 0))])
                G["fqs"][slot] = p.c("VE")
            ve_m_ = p.c("VE")
            G["fqm_ve"] = ve_m_
            p.op("scalar", lambda e, d=sgnq[:, 0:wout], s=avq[:, 0:wout]:
                 e.activation(d, s, AF.Sign, bias=czero, scale=one),
                 sem="SE", waits=[("VE", ve_m_), ("VE", G.get("fqq_ve", 0))])
            se_a = p.c("SE")
            G["fq_se"] = se_a
            p.op("vector", lambda e, d=t2q[:, 0:wout], a=sgnq[:, 0:wout],
                 b=avq[:, 0:wout]:
                 e.scalar_tensor_tensor(d, a, 128.0, b, ALU.mult, ALU.add),
                 sem="VE", waits=[("SE", se_a)])
            p.op("vector", lambda e, d=tgt[:, kc * wout:(kc + 1) * wout],
                 a=sgnq[:, 0:wout], b=t2q[:, 0:wout]:
                 e.scalar_tensor_tensor(d, a, -128.0, b, ALU.mult, ALU.add),
                 sem="VE")
            G["fqq_ve"] = p.c("VE")
        return p.c("VE")

    quant_weight(wg_n, wg_tt, ICN, D, sall_g, wgq,
                 [wtc0[:, 0:D], wtc0[:, D:2 * D], avq[:, 0:D], avq[:, D:2 * D]])
    quant_weight(wu_n, wu_tt, ICN, D, sall_u, wuq,
                 [wtc0[:, 0:D], wtc0[:, D:2 * D], avq[:, 0:D], avq[:, D:2 * D]])
    wdq_ve = quant_weight(wd_n, wd_tt, KC, I, sall_d, wdq,
                          [wtc0[:, 0:I], avq[:, 0:I], natd2, natd3])
    for kc in range(ICN):
        st(wdq_s[kc * 128:(kc + 1) * 128, :], wdq[:, kc * D:(kc + 1) * D],
           "SW", waits=[("VE", wdq_ve)])
    wq_st = p.c("SW")
    wq_ve = p.c("VE")

    p.op("vector", lambda e: e.tensor_scalar(ax_all, sall_g, sxs_col, None, ALU.mult),
         sem="VE", waits=[("VE", rx_ve)])
    p.op("vector", lambda e: e.tensor_scalar(bu_all, sall_u, sxs_col, None, ALU.mult),
         sem="VE")
    scales_ve = p.c("VE")

    if stop == "FFNQ":
        if debug:
            st(dbg[:, 0:5632], wgq[:, 0:5632], "SO", waits=[("VE", scales_ve)])
            st(dbg[:, 6000:6000 + ICN], sall_g, "SO", waits=[("VE", scales_ve)])
            st(dbg[:, 6030:6030 + ICN], ax_all, "SO", waits=[("VE", scales_ve)])
        _finish(nc, p)
        return nc

    p.barrier_eng("sync")
    G["f1x"] = {0: 0, 1: 0}
    G["f1g_se"] = [0, 0, 0, 0]
    G["f1h_ve"] = [0, 0, 0, 0]
    G["f1s_st"] = {}
    G["f1q_ve"] = 0
    for t in range(NT):
        xt = xl0 if t % 2 == 0 else xl1
        lx = "LX%d" % (t % 2)
        for kc in range(KC):
            ld(xt[:, kc * TT:(kc + 1) * TT],
               xln_s[kc * 128:(kc + 1) * 128, t * TT:(t + 1) * TT], lx,
               waits=[("TE", G["f1x"][t % 2]), ("SW", wq_st), ("VE", wq_ve),
                      ("SX", b_st_all), ("SE", G.get("f1a_se", 0))])
        x_ld = p.c(lx)
        p.op("scalar", lambda e, d=avf, s=xt:
             e.activation(d, s, AF.Abs, bias=czero, scale=rx_col),
             sem="SE", waits=[(lx, x_ld), ("VE", scales_ve), ("VE", G["f1q_ve"])])
        p.op("scalar", lambda e, d=sgnf, s=xt:
             e.activation(d, s, AF.Sign, bias=czero, scale=one), sem="SE")
        se_q = p.c("SE")
        G["f1a_se"] = se_q
        xqb = xqb2[t % 2]
        p.op("vector", lambda e, d=t2f, s=avf:
             e.tensor_scalar(d, s, 128.0, 256.49, ALU.add, ALU.min),
             sem="VE", waits=[("SE", se_q)])
        p.op("vector", lambda e, d=xqb, a=t2f, b=sgnf:
             e.scalar_tensor_tensor(d, a, -128.0, b, ALU.add, ALU.mult),
             sem="VE", waits=[("TE", G["f1x"][t % 2])])
        p.op("vector", lambda e, d=xqb: e.tensor_scalar(d, d, 127.0, None, ALU.min),
             sem="VE")
        xq_ve = p.c("VE")
        G["f1q_ve"] = xq_ve

        for ic in range(ICN):
            pg = PS[(ic % 2) * 2]
            pu = PS[(ic % 2) * 2 + 1]
            for kc in range(KC):
                lg = wgq[:, kc * I + ic * 128: kc * I + (ic + 1) * 128]
                lu = wuq[:, kc * I + ic * 128: kc * I + (ic + 1) * 128]
                rr = xqb[:, kc * TT:(kc + 1) * TT]
                p.op("tensor", lambda e, o=pg, l=lg, r=rr, s=(kc == 0),
                     z=(kc == KC - 1):
                     e.matmul(o, l, r, start=s, stop=z),
                     sem="TE" if kc == KC - 1 else None,
                     waits=([("VE", xq_ve), ("SE", G["f1g_se"][(ic % 2) * 2])]
                            if kc == 0 else []))
                p.op("tensor", lambda e, o=pu, l=lu, r=rr, s=(kc == 0),
                     z=(kc == KC - 1):
                     e.matmul(o, l, r, start=s, stop=z),
                     sem="TE" if kc == KC - 1 else None,
                     waits=([("VE", G["f1h_ve"][(ic % 2) * 2 + 1])]
                            if kc == 0 else []))
            te_gu = p.c("TE")
            sl = silu0 if ic % 2 == 0 else silu1
            hbuf = hb[ic % 4]
            p.op("scalar", lambda e, d=sl, s=pg, sc=ax_all[:, ic:ic + 1]:
                 e.activation(d, s, AF.Silu, bias=czero, scale=sc),
                 sem="SE", waits=[("TE", te_gu)])
            se_s = p.c("SE")
            G["f1g_se"][(ic % 2) * 2] = se_s
            p.op("vector", lambda e, d=hbuf, a=sl, b=pu:
                 e.tensor_tensor(d, a, b, ALU.mult),
                 sem="VE", waits=[("SE", se_s),
                                  ("SH%d" % (ic % 4), G["f1s_st"].get(ic % 4, 0))])
            ve_h = p.c("VE")
            G["f1h_ve"][(ic % 2) * 2] = ve_h
            G["f1h_ve"][(ic % 2) * 2 + 1] = ve_h
            st(h_s[ic * 128:(ic + 1) * 128, t * TT:(t + 1) * TT], hbuf,
               "SH%d" % (ic % 4), waits=[("VE", ve_h)])
            G["f1s_st"][ic % 4] = p.c("SH%d" % (ic % 4))
            p.op("vector", lambda e, d=hmaxst[:, t * ICN + ic: t * ICN + ic + 1],
                 s=hbuf:
                 e.tensor_reduce(d, s, AX.X, ALU.max, apply_absolute_value=True),
                 sem="VE")
        G["f1x"][t % 2] = p.c("TE")
        G["f1g_se"] = [p.c("SE")] * 4
        G["f1h_ve"] = [p.c("VE")] * 4
    f1_ve = p.c("VE")
    f1_st = tuple(p.c("SH%d" % i) for i in range(4))

    hview = hmaxst.rearrange("p (t c) -> p c t", c=ICN)
    p.op("vector", lambda e: e.tensor_reduce(mred, hview, AX.X, ALU.max),
         sem="VE", waits=[("VE", f1_ve)])
    p.op("vector", lambda e: e.tensor_tensor(mred, mred, bu_all, ALU.mult), sem="VE")
    p.op("vector", lambda e: e.tensor_reduce(m1col, mred, AX.X, ALU.max), sem="VE")
    ve_m2 = p.c("VE")
    p.op("tensor", lambda e, o=PS[6][0:1, 0:128], i=m1col:
         e.transpose(o, i, ident), sem="TE",
         waits=[("VE", ve_m2), ("SE", p.c("SE"))])
    te_m2 = p.c("TE")
    p.op("vector", lambda e: e.tensor_reduce(r1x1, PS[6][0:1, 0:128], AX.X, ALU.max),
         sem="VE", waits=[("TE", te_m2)])
    p.op("vector", lambda e: e.tensor_scalar(ccst, onesrow[0:1, 0:8],
                                             r1x1[0:1, 0:1], None, ALU.mult),
         sem="VE")
    ve_cc2 = p.c("VE")
    p.op("gpsimd", lambda e: e.dma_start(out=cc_in2.ap(), in_=ccst),
         sem="SC", by=16, waits=[("VE", ve_cc2)])
    p.wait("gpsimd", [("SC", p.c("SC"))])
    p.op("gpsimd", lambda e: e.collective_compute(
        "AllReduce", ALU.max, replica_groups=[[0, 1, 2, 3, 4, 5, 6, 7]],
        ins=[cc_in2.ap().opt()], outs=[cc_out2.ap().opt()]), sem="GE")
    p.wait("gpsimd", [("GE", p.c("GE"))])
    p.op("gpsimd", lambda e: e.dma_start(out=gmax2, in_=cc_out2.ap()[0:1, 0:1]),
         sem="SC", by=16)
    g2_st = p.c("SC")
    p.op("vector", lambda e: e.tensor_tensor(r1x1, gmax2, sig2[0:1, 1:2], ALU.mult),
         sem="VE", waits=[("SC", g2_st)])
    p.op("vector", lambda e: e.reciprocal(r1x1b, r1x1), sem="VE")
    p.op("vector", lambda e: e.tensor_scalar(r1x1b, r1x1b, 127.0, None, ALU.mult),
         sem="VE")
    p.op("vector", lambda e: e.tensor_scalar(r1x1, r1x1, INV127 * INV127, None,
                                             ALU.mult), sem="VE")
    ve_rh = p.c("VE")
    p.op("tensor", lambda e, o=PS[6][:, 0:1]: e.matmul(o, onesrow, r1x1b,
                                                       start=True, stop=True),
         sem="TE", waits=[("VE", ve_rh)])
    p.op("tensor", lambda e, o=PS[7][:, 0:1]: e.matmul(o, onesrow, r1x1,
                                                       start=True, stop=True),
         sem="TE")
    te_rh = p.c("TE")
    p.op("vector", lambda e: e.tensor_copy(rh_col, PS[6][:, 0:1]),
         sem="VE", waits=[("TE", te_rh)])
    p.op("vector", lambda e: e.tensor_copy(shs_col, PS[7][:, 0:1]), sem="VE")
    p.op("vector", lambda e: e.tensor_scalar(reff_all, bu_all, rh_col, None,
                                             ALU.mult), sem="VE")
    p.op("vector", lambda e: e.tensor_scalar(cd_all, sall_d, shs_col, None,
                                             ALU.mult), sem="VE")
    rh_ve = p.c("VE")

    if stop == "FFN1":
        if debug:
            st(dbg[0:1, 0:1], gmax2, "SO", waits=[("VE", rh_ve)])
            st(dbg[:, 4:4 + ICN], reff_all, "SO", waits=[("VE", rh_ve)])
            st(dbg[:, 30:30 + KC], cd_all, "SO", waits=[("VE", rh_ve)])
            st(dbg[:, 40:40 + NT * ICN], hmaxst, "SO", waits=[("VE", rh_ve)])
        _finish(nc, p)
        return nc

    p.barrier_eng("sync")
    for kc in range(ICN):
        ld(wdq2[:, kc * D:(kc + 1) * D], wdq_s[kc * 128:(kc + 1) * 128, :],
           "LW", waits=[("SW", wq_st)])
    wd_ld = p.c("LW")

    G["f2x"] = {0: 0, 1: 0}
    G["f2h_se"] = {}
    G["f2d_ve"] = {0: 0, 1: 0}
    G["f2q_ve"] = {0: 0, 1: 0}
    G["f2hq_te"] = {0: 0, 1: 0}
    G["f2s_ve"] = 0
    for t in range(NT):
        xt = xl20 if t % 2 == 0 else xl21
        hqt = hq0 if t % 2 == 0 else hq1
        lx = "LX%d" % (t % 2)
        for kc in range(KC):
            ld(xt[:, kc * TT:(kc + 1) * TT],
               xln_s[kc * 128:(kc + 1) * 128, t * TT:(t + 1) * TT], lx,
               waits=[("VE", G["f2x"][t % 2])])
        x_ld = p.c(lx)
        for ic in range(ICN):
            hbuf = hl[ic % 4]
            lh = "LH%d" % (ic % 4)
            ld(hbuf, h_s[ic * 128:(ic + 1) * 128, t * TT:(t + 1) * TT], lh,
               waits=[("SE", G["f2h_se"].get(ic % 4, 0)),
                      ("SH0", f1_st[0]), ("SH1", f1_st[1]),
                      ("SH2", f1_st[2]), ("SH3", f1_st[3])])
            ld_h = p.c(lh)
            par = ic % 2
            av_, sg_, t2_ = avf2[par], sgnf2[par], t2f2[par]
            p.op("scalar", lambda e, d=av_, s=hbuf, sc=reff_all[:, ic:ic + 1]:
                 e.activation(d, s, AF.Abs, bias=czero, scale=sc),
                 sem="SE", waits=[(lh, ld_h), ("VE", rh_ve),
                                  ("VE", G["f2q_ve"][par])])
            p.op("scalar", lambda e, d=sg_, s=hbuf:
                 e.activation(d, s, AF.Sign, bias=czero, scale=one), sem="SE")
            se_q = p.c("SE")
            G["f2h_se"][ic % 4] = se_q
            p.op("vector", lambda e, d=t2_, s=av_:
                 e.tensor_scalar(d, s, 128.0, 256.49, ALU.add, ALU.min),
                 sem="VE", waits=[("SE", se_q)])
            p.op("vector", lambda e, d=hqt[:, ic * TT:(ic + 1) * TT], a=t2_,
                 b=sg_:
                 e.scalar_tensor_tensor(d, a, -128.0, b, ALU.add, ALU.mult),
                 sem="VE", waits=[("TE", G["f2hq_te"][t % 2])])
            p.op("vector", lambda e, d=hqt[:, ic * TT:(ic + 1) * TT]:
                 e.tensor_scalar(d, d, 127.0, None, ALU.min), sem="VE")
            G["f2q_ve"][par] = p.c("VE")
        hq_ve = p.c("VE")

        for oc in range(KC):
            pd = PS[oc % 2]
            for kc in range(ICN):
                lhs = wdq2[:, kc * D + oc * 128: kc * D + (oc + 1) * 128]
                rhs = hqt[:, kc * TT:(kc + 1) * TT]
                p.op("tensor",
                     lambda e, o=pd, l=lhs, r=rhs, s=(kc == 0), z=(kc == ICN - 1):
                     e.matmul(o, l, r, start=s, stop=z),
                     sem="TE" if kc == ICN - 1 else None,
                     waits=([("VE", hq_ve), ("LW", wd_ld), (lx, x_ld),
                             ("VE", G["f2d_ve"][oc % 2])] if kc == 0 else []))
            te_d = p.c("TE")
            p.op("vector", lambda e, d=yout[:, oc * TT:(oc + 1) * TT],
                 a=pd, sc=cd_all[:, oc:oc + 1], b=xt[:, oc * TT:(oc + 1) * TT]:
                 e.scalar_tensor_tensor(d, a, sc, b, ALU.mult, ALU.add),
                 sem="VE", waits=[("TE", te_d)])
            G["f2d_ve"][oc % 2] = p.c("VE")
        G["y_ve"] = p.c("VE")

        def ln2_out(oc, ve_o, _t=t):
            st(y_t[oc * 128:(oc + 1) * 128, _t * TT:(_t + 1) * TT],
               ysq2[:, oc * TT:(oc + 1) * TT], "SO", waits=[("VE", ve_o)])
        ln_emit(yout, ysq2, ysq2, ln2_out, PS[6], PS[7])
        G["f2x"][t % 2] = p.c("VE")
        G["f2d_ve"] = {0: p.c("VE"), 1: p.c("VE")}
        G["f2hq_te"][t % 2] = p.c("TE")

    _finish(nc, p)
    return nc


def _finish(nc, p):
    p.wait("gpsimd", [(s, p.c(s)) for s in
                      ("SO", "SX", "SH0", "SH1", "SH2", "SH3", "SW", "SC", "ZG")])
    with nc.Block() as block:
        for eng in ENGS:
            def run(e, _ops=p.ops[eng]):
                for f in _ops:
                    f(e)
            getattr(block, eng)(run)


def host_consts():
    c = np.zeros((128, 13), dtype=np.float32)
    c[:, C_ONE] = 1.0
    c[:, C_HALFPI] = PI / 2
    c[:, C_MHALF] = -0.5
    c[:, C_EPS] = 1e-5
    c[:, C_LNTHR] = LN_THR
    c[:, C_PHALF] = 0.5
    c[:, C_MONE] = -1.0
    c[:, C_MLN16] = MLN16
    c[:, C_PI] = PI
    c[:, C_MPI] = -PI
    c[:, C_MHPI] = -PI / 2
    return c


def kernel(L0_attn_x, L0_attn_wq, L0_attn_wk, L0_attn_wv, L0_attn_wo, L0_attn_rf,
           L0_ffn_clip_x, L0_ffn_clip_down, L0_ffn_w_gate, L0_ffn_w_up,
           L0_ffn_w_down):
    x = np.ascontiguousarray(np.asarray(L0_attn_x, dtype=np.float32))
    wq = np.ascontiguousarray(np.asarray(L0_attn_wq, dtype=np.float32).T)
    wk = np.ascontiguousarray(np.asarray(L0_attn_wk, dtype=np.float32).T)
    wv = np.ascontiguousarray(np.asarray(L0_attn_wv, dtype=np.float32).T)
    wo = np.ascontiguousarray(np.asarray(L0_attn_wo, dtype=np.float32).T)
    rf = np.asarray(L0_attn_rf, dtype=np.float32)
    rf_rep = np.ascontiguousarray(np.concatenate([rf.T, rf.T], axis=0))
    rf_bd = np.zeros((128, 2 * MF), dtype=np.float32)
    rf_bd[0:64, 0:MF] = rf.T
    rf_bd[64:128, MF:2 * MF] = rf.T
    wg = np.ascontiguousarray(np.asarray(L0_ffn_w_gate, dtype=np.float32))
    wu = np.ascontiguousarray(np.asarray(L0_ffn_w_up, dtype=np.float32))
    wd = np.ascontiguousarray(np.asarray(L0_ffn_w_down, dtype=np.float32))
    clip2 = np.array([[float(np.asarray(L0_ffn_clip_x).reshape(-1)[0]),
                       float(np.asarray(L0_ffn_clip_down).reshape(-1)[0])]],
                     dtype=np.float32)
    maskh = np.zeros((128, KC * H), dtype=np.float32)
    for kc in range(KC):
        for pl in range(128):
            maskh[pl, kc * H + 2 * kc + pl // 64] = 1.0
    sel_all = np.zeros((128, KC * 128), dtype=np.float32)
    for oc in range(KC):
        for pl in range(128):
            sel_all[2 * oc + pl // 64, oc * 128 + pl] = 1.0

    shared = {
        "wq_t": wq, "wk_t": wk, "wv_t": wv, "wo_t": wo, "rf_rep": rf_rep,
        "rf_bd": rf_bd,
        "wg_n": wg, "wu_n": wu, "wd_n": wd,
        "wg_tt": np.ascontiguousarray(wg.T),
        "wu_tt": np.ascontiguousarray(wu.T),
        "wd_tt": np.ascontiguousarray(wd.T),
        "clip2": clip2, "consts": host_consts(),
        "onesrow": np.ones((1, 128), dtype=np.float32),
        "ident": np.eye(128, dtype=np.float32),
        "maskh": maskh, "sel_all": sel_all,
    }
    in_maps = []
    for b in range(8):
        m = dict(shared)
        m["x_t"] = np.ascontiguousarray(x[b].T)
        in_maps.append(m)

    nc = build_core_program()
    res = run_bass_kernel_spmd(nc, in_maps, core_ids=list(range(8)))
    return np.stack([np.ascontiguousarray(r["y_t"].T) for r in res.results], axis=0)


# version 17 (final)
# speedup vs baseline: 1.0361x; 1.0121x over previous
import math
import numpy as np
import concourse.bass as bass
import concourse.mybir as mybir
from concourse.bass_utils import run_bass_kernel_spmd

F32 = mybir.dt.float32
BF16 = mybir.dt.bfloat16
I32 = mybir.dt.int32
AF = mybir.ActivationFunctionType
ALU = mybir.AluOpType
AX = mybir.AxisListType

T = 4096
D = 1024
H = 16
DH = 64
MF = 256
M2 = 512
I = 2816
KC = D // 128
ICN = I // 128
TT = 512
NT = T // TT
TB = 128
NTB = TT // TB

PI = math.pi
INV127 = 1.0 / 127.0
LN_THR = math.log(1e-6 * math.sqrt(MF))
MLN16 = -math.log(math.sqrt(MF))

(C_ONE, C_HALFPI, C_MHALF, C_EPS, C_LNTHR, C_PHALF, C_ZERO, C_MONE,
 C_MLN16, C_SPARE, C_PI, C_MPI, C_MHPI) = range(13)

ENGS = ("sync", "tensor", "vector", "scalar", "gpsimd")


class Plan:
    def __init__(self, nc):
        self.nc = nc
        self.ops = {e: [] for e in ENGS}
        self.sems = {}
        self.cnt = {}
        self.mark = {e: {} for e in ENGS}
        self.g = {}

    def sem(self, name):
        s = self.nc.alloc_semaphore(name)
        self.sems[name] = s
        self.cnt[name] = 0

    def op(self, eng, f, sem=None, by=1, waits=()):
        if eng == "vector":
            waits = list(waits) + [("VE", self.cnt["VE"])]
        elif eng == "scalar":
            waits = list(waits) + [("SE", self.cnt["SE"])]
        real = []
        wm = self.mark[eng]
        for sn, val in waits:
            if val > 0 and wm.get(sn, 0) < val:
                wm[sn] = val
                real.append((self.sems[sn], val))
        semh = None
        if sem is not None:
            self.cnt[sem] += by
            semh = self.sems[sem]
        def fn(e, _w=real, _f=f, _s=semh, _b=by):
            for sh, v in _w:
                e.wait_ge(sh, v)
            ins = _f(e)
            if _s is not None:
                ins.then_inc(_s, _b)
        self.ops[eng].append(fn)

    def wait(self, eng, waits):
        wm = self.mark[eng]
        for sn, val in waits:
            if val > 0 and wm.get(sn, 0) < val:
                wm[sn] = val
                s = self.sems[sn]
                self.ops[eng].append(lambda e, _s=s, _v=val: e.wait_ge(_s, _v))

    def c(self, name):
        return self.cnt[name]

    def barrier_eng(self, eng):
        self.wait(eng, [(s, self.cnt[s]) for s in self.cnt
                        if s != "GE"])


def build_core_program(stop=None, debug=False, amask="nvpkt"):
    nc = bass.Bass("TRN2", target_bir_lowering=False)

    x_t = nc.dram_tensor("x_t", [D, T], F32, kind="ExternalInput").ap()
    wq_t = nc.dram_tensor("wq_t", [D, D], F32, kind="ExternalInput").ap()
    wk_t = nc.dram_tensor("wk_t", [D, D], F32, kind="ExternalInput").ap()
    wv_t = nc.dram_tensor("wv_t", [D, D], F32, kind="ExternalInput").ap()
    wo_t = nc.dram_tensor("wo_t", [D, D], F32, kind="ExternalInput").ap()
    rf_rep_i = nc.dram_tensor("rf_rep", [128, MF], F32, kind="ExternalInput").ap()
    rf_bd_i = nc.dram_tensor("rf_bd", [128, M2], F32, kind="ExternalInput").ap()
    wg_n = nc.dram_tensor("wg_n", [I, D], F32, kind="ExternalInput").ap()
    wu_n = nc.dram_tensor("wu_n", [I, D], F32, kind="ExternalInput").ap()
    wd_n = nc.dram_tensor("wd_n", [D, I], F32, kind="ExternalInput").ap()
    wg_tt = nc.dram_tensor("wg_tt", [D, I], F32, kind="ExternalInput").ap()
    wu_tt = nc.dram_tensor("wu_tt", [D, I], F32, kind="ExternalInput").ap()
    wd_tt = nc.dram_tensor("wd_tt", [I, D], F32, kind="ExternalInput").ap()
    clip2 = nc.dram_tensor("clip2", [1, 2], F32, kind="ExternalInput").ap()
    consts_i = nc.dram_tensor("consts", [128, 13], F32, kind="ExternalInput").ap()
    onesrow_i = nc.dram_tensor("onesrow", [1, 128], F32, kind="ExternalInput").ap()
    ident_i = nc.dram_tensor("ident", [128, 128], F32, kind="ExternalInput").ap()
    maskh_i = nc.dram_tensor("maskh", [128, KC * H], F32, kind="ExternalInput").ap()
    sel_i = nc.dram_tensor("sel_all", [128, KC * 128], F32, kind="ExternalInput").ap()

    y_t = nc.dram_tensor("y_t", [D, T], F32, kind="ExternalOutput").ap()
    dbg = (nc.dram_tensor("dbg", [128, 16384], F32, kind="ExternalOutput").ap()
           if debug else None)

    xln_s = nc.dram_tensor("xln_s", [D, T], F32).ap()
    h_s = nc.dram_tensor("h_s", [I, T], F32).ap()
    wdq_s = nc.dram_tensor("wdq_s", [I, D], BF16).ap()
    cc_in1 = nc.dram_tensor("cc_in1", [1, 8], F32)
    cc_out1 = nc.dram_tensor("cc_out1", [1, 8], F32)
    cc_in2 = nc.dram_tensor("cc_in2", [1, 8], F32)
    cc_out2 = nc.dram_tensor("cc_out2", [1, 8], F32)

    A = lambda *a: nc.alloc_sbuf_tensor(*a).ap()
    consts = A("consts_sb", [128, 13], F32)
    onesrow = A("onesrow_sb", [1, 128], F32)
    ident = A("ident_sb", [128, 128], F32)
    rf_rep = A("rf_rep_sb", [128, MF], F32)
    rf_bd = A("rf_bd_sb", [128, M2], F32)
    maskh = A("maskh_sb", [128, KC * H], F32)
    sel_all = A("sel_sb", [128, KC * 128], F32)
    clip_sb = A("clip_sb", [1, 2], F32)
    sig2 = A("sig2_sb", [1, 2], F32)
    scale_sb = A("scaleK_sb", [128, H], F32)
    zst = A("zst_sb", [128, TT], F32)
    zall = A("zall_sb", [128, TT], F32)
    thr_fm = A("thr_sb", [H, TT], F32)
    rsrow = A("rsrow", [1, TT], F32)
    nurow = A("nurow", [1, TT], F32)
    sc512 = A("sc512", [128, TT], F32)
    xmaxst = A("xmaxst", [128, NT * KC], F32)
    hmaxst = A("hmaxst", [128, NT * ICN], F32)
    sall_g = A("sall_g", [128, ICN], F32)
    sall_u = A("sall_u", [128, ICN], F32)
    sall_d = A("sall_d", [128, KC], F32)
    ax_all = A("ax_all", [128, ICN], F32)
    bu_all = A("bu_all", [128, ICN], F32)
    reff_all = A("reff_all", [128, ICN], F32)
    cd_all = A("cd_all", [128, KC], F32)
    rx_col = A("rx_col", [128, 1], F32)
    sxs_col = A("sxs_col", [128, 1], F32)
    rh_col = A("rh_col", [128, 1], F32)
    shs_col = A("shs_col", [128, 1], F32)
    r1x1 = A("r1x1", [1, 1], F32)
    r1x1b = A("r1x1b", [1, 1], F32)
    gmax1 = A("gmax1", [1, 1], F32)
    gmax2 = A("gmax2", [1, 1], F32)
    ccst = A("ccst", [1, 8], F32)
    tp22 = A("tp22", [ICN, 128], F32)
    mred = A("mred", [128, ICN], F32)
    m1col = A("m1col", [128, 1], F32)

    PS = [nc.alloc_psum_tensor(f"ps{i}", [128, 512], F32).ap() for i in range(8)]

    BASE = 40960
    TOP = 229344

    def place(name, shape, dtype, off_kb):
        off = int(off_kb * 1024)
        nbytes = int(np.prod(shape[1:])) * mybir.dt.size(dtype)
        assert BASE + off + nbytes <= TOP, (name, off_kb, nbytes)
        return nc.alloc_sbuf_tensor_at(name, shape, dtype, offset=BASE + off).ap()

    wk_sb = place("wk_sb", [128, KC * D], F32, 0)
    wv_sb = place("wv_sb", [128, KC * D], F32, 32)
    kvacc = place("kvacc", [128, H * 512], F32, 64)
    xa0 = place("xa0", [128, KC * TT], F32, 96)
    ktile = place("ktile", [128, KC * TT], F32, 128)
    ksq = place("ksq", [128, KC * TT], F32, 144)
    kfps = [place("kfp0", [128, 2 * M2], F32, 144),
            place("kfp1", [128, 2 * M2], F32, 112),
            place("kfp2", [128, 2 * M2], F32, 116),
            place("kfp3", [128, 2 * M2], F32, 120)]
    pti = place("pti", [128, M2], I32, 148)
    actc = place("actc", [128, M2], F32, 148)
    ap0 = [place("ap0_0", [128, M2], F32, 150),
           place("ap0_1", [128, M2], F32, 152),
           place("ap0_2", [128, M2], F32, 124),
           place("ap0_3", [128, M2], F32, 126)]
    au = place("au", [128, M2], F32, 154)
    as1 = [place("as1_0", [128, M2], F32, 156),
           place("as1_1", [128, M2], F32, 177)]
    asgn = place("asgn", [128, M2], F32, 158)
    v65a = place("v65a", [128, NTB * H * 65], F32, 160)
    scale_a = place("scale_a", [128, NTB * H], F32, 176.5)
    kvfm = place("kvfm", [128, H * 4 * 65], F32, 96)

    wq_sb = place("wq_sb", [128, KC * D], F32, 0)
    wo_sb = place("wo_sb", [128, KC * D], F32, 32)
    xb0 = place("xb0", [128, KC * TT], F32, 64)
    qtiles = [place("qtileA", [128, KC * TT], F32, 113),
              place("qtileB", [128, KC * TT], F32, 80)]
    qsq = place("qsq", [128, KC * TT], F32, 129)
    attn = place("attn", [128, KC * TT], F32, 145)
    qf0 = place("qf0", [128, 4 * TT], F32, 161)
    qf1 = place("qf1", [128, 4 * TT], F32, 169)
    bp0 = [place(f"bp0_{m}", [128, M2], F32, 129 + 2 * m) for m in range(2)]
    bu = [place(f"bu_{m}", [128, M2], F32, 133 + 2 * m) for m in range(2)]
    bsgn = [place(f"bsgn_{m}", [128, M2], F32, 137 + 2 * m) for m in range(2)]
    bs1 = [place(f"bs1_{m}", [128, M2], F32, 141 + 2 * m) for m in range(2)]
    ptbi = place("ptbi", [128, 512], I32, 177)
    actcB = place("actcB", [128, 512], F32, 177)
    ba2 = [place(f"ba2_{m}", [128, M2], F32, 179 + 2 * m) for m in range(2)]

    wgq = place("wgq", [128, KC * I], BF16, 0)
    wuq = place("wuq", [128, KC * I], BF16, 44)
    wdq = place("wdq", [128, ICN * D], BF16, 88)
    rb = place("rb", [128, I], F32, 132)
    wtc0 = place("wtc0", [128, I], F32, 143)
    avq = place("avq", [128, I], F32, 154)
    sgnq = place("sgnq", [128, I], BF16, 165)
    t2q = place("t2q", [128, I], BF16, 171)
    natd2 = place("natd2", [128, I], F32, 165)
    natd3 = place("natd3", [128, I], F32, 132)

    xl0 = place("xl0", [128, KC * TT], F32, 88)
    xl1 = place("xl1", [128, KC * TT], F32, 104)
    xqb2 = [place("xqb0", [128, KC * TT], BF16, 120),
            place("xqb1", [128, KC * TT], BF16, 172)]
    hb = [place(f"hb{i}", [128, TT], F32, 128 + 2 * i) for i in range(4)]
    silu0 = place("silu0", [128, TT], F32, 136)
    silu1 = place("silu1", [128, TT], F32, 138)
    avf = place("avf", [128, KC * TT], F32, 140)
    sgnf = place("sgnf", [128, KC * TT], BF16, 156)
    t2f = place("t2f", [128, KC * TT], BF16, 164)

    wdq2 = place("wdq2", [128, ICN * D], BF16, 0)
    hq0 = place("hq0", [128, ICN * TT], BF16, 44)
    hq1 = place("hq1", [128, ICN * TT], BF16, 66)
    hl = [place(f"hl{i}", [128, TT], F32, 88 + 2 * i) for i in range(4)]
    xl20 = place("xl20", [128, KC * TT], F32, 96)
    xl21 = place("xl21", [128, KC * TT], F32, 112)
    yout = place("yout", [128, KC * TT], F32, 128)
    ysq2 = place("ysq2", [128, KC * TT], F32, 144)
    avf2 = [place(f"avf2_{i}", [128, TT], F32, 160 + 4 * i) for i in range(2)]
    sgnf2 = [place(f"sgnf2_{i}", [128, TT], BF16, 162 + 4 * i) for i in range(2)]
    t2f2 = [place(f"t2f2_{i}", [128, TT], BF16, 163 + 4 * i) for i in range(2)]

    p = Plan(nc)
    for s in ["LC", "LW", "LX0", "LX1", "LQ", "LH0", "LH1", "LH2", "LH3",
              "SX", "SH0", "SH1", "SH2", "SH3", "SW", "SC", "SO",
              "ZG", "TE", "VE", "SE", "GE", "GP"]:
        p.sem(s)
    G = p.g

    one = consts[:, C_ONE:C_ONE + 1]
    czero = consts[:, C_ZERO:C_ZERO + 1]

    def ld(dst, src, stream, waits=()):
        p.op("sync", lambda e, d=dst, s=src: e.dma_start(out=d, in_=s),
             sem=stream, by=16, waits=waits)

    def st(dst, src, stream, waits=()):
        p.op("gpsimd", lambda e, d=dst, s=src:
             e.dma_start(out=d, in_=s, allow_slow_non_contiguous=True),
             sem=stream, by=16, waits=waits)

    def ln_emit(ytile, ysqtile, outtile, outcb, psA, psB):
        p.op("scalar", lambda e, d=ysqtile, s=ytile:
             e.activation(d, s, AF.Square, bias=czero, scale=one),
             sem="SE", waits=[("VE", G["y_ve"])])
        ysq_se = p.c("SE")
        for kc in range(KC):
            p.op("tensor", lambda e, o=psA[0:1, :], l=one,
                 r=ytile[:, kc * TT:(kc + 1) * TT], s=(kc == 0), z=(kc == KC - 1):
                 e.matmul(o, l, r, start=s, stop=z),
                 sem="TE" if kc == KC - 1 else None,
                 waits=([("VE", G["y_ve"]), ("VE", G.get("ln_ps_free", 0)),
                         ("SE", G.get("ln_ps_free_se", 0))] if kc == 0 else []))
        for kc in range(KC):
            p.op("tensor", lambda e, o=psB[0:1, :], l=one,
                 r=ysqtile[:, kc * TT:(kc + 1) * TT], s=(kc == 0), z=(kc == KC - 1):
                 e.matmul(o, l, r, start=s, stop=z),
                 sem="TE" if kc == KC - 1 else None,
                 waits=([("SE", ysq_se)] if kc == 0 else []))
        te_stats = p.c("TE")
        p.op("vector", lambda e: e.tensor_scalar(zst[0:1, :], psA[0:1, :],
                                                 1.0 / D, None, ALU.mult),
             sem="VE", waits=[("TE", te_stats)])
        p.op("vector", lambda e: e.tensor_tensor(rsrow, zst[0:1, :], zst[0:1, :],
                                                 ALU.mult), sem="VE")
        p.op("vector", lambda e: e.scalar_tensor_tensor(
            rsrow, psB[0:1, :], 1.0 / D, rsrow, ALU.mult, ALU.subtract), sem="VE")
        ve_var = p.c("VE")
        p.op("scalar", lambda e: e.activation(nurow, rsrow, AF.Ln,
                                              bias=consts[0:1, C_EPS:C_EPS + 1],
                                              scale=one[0:1]),
             sem="SE", waits=[("VE", ve_var)])
        p.op("scalar", lambda e: e.activation(rsrow, nurow, AF.Exp,
                                              bias=czero[0:1],
                                              scale=consts[0:1, C_MHALF:C_MHALF + 1]),
             sem="SE")
        se_rs = p.c("SE")
        p.op("vector", lambda e: e.tensor_tensor(nurow, zst[0:1, :], rsrow, ALU.mult),
             sem="VE", waits=[("SE", se_rs)])
        ve_nu = p.c("VE")
        p.op("tensor", lambda e, o=psA: e.matmul(o, onesrow, rsrow,
                                                 start=True, stop=True),
             sem="TE", waits=[("VE", ve_nu)])
        p.op("tensor", lambda e, o=psB: e.matmul(o, onesrow, nurow,
                                                 start=True, stop=True), sem="TE")
        te_bc = p.c("TE")
        for oc in range(KC):
            p.op("vector", lambda e, d=sc512, a=ytile[:, oc * TT:(oc + 1) * TT], b=psA:
                 e.tensor_tensor(d, a, b, ALU.mult),
                 sem="VE", waits=[("TE", te_bc)])
            p.op("vector", lambda e, d=outtile[:, oc * TT:(oc + 1) * TT],
                 a=sc512, b=psB:
                 e.tensor_tensor(d, a, b, ALU.subtract), sem="VE")
            outcb(oc, p.c("VE"))
        G["ln_ps_free"] = p.c("VE")
        G["ln_ps_free_se"] = p.c("SE")

    for dst, src in [(consts, consts_i), (onesrow, onesrow_i), (ident, ident_i),
                     (rf_rep, rf_rep_i), (rf_bd, rf_bd_i),
                     (maskh, maskh_i), (sel_all, sel_i),
                     (clip_sb, clip2)]:
        ld(dst, src, "LC")
    setup_ld = p.c("LC")
    p.op("scalar", lambda e: e.activation(sig2, clip_sb, AF.Sigmoid,
                                          bias=czero[0:1], scale=one[0:1]),
         sem="SE", waits=[("LC", setup_ld)])
    p.op("vector", lambda e: e.memset(kvacc, 0.0), sem="VE")
    p.op("vector", lambda e: e.memset(zall, 0.0), sem="VE")
    kv_zero_ve = p.c("VE")

    for kc in range(KC):
        ld(wk_sb[:, kc * D:(kc + 1) * D], wk_t[kc * 128:(kc + 1) * 128, :], "LW")
        ld(wv_sb[:, kc * D:(kc + 1) * D], wv_t[kc * 128:(kc + 1) * 128, :], "LW")
    a_w_ld = p.c("LW")

    if stop == "A0":
        _finish(nc, p)
        return nc

    G["xa"] = {0: 0, 1: 0}
    G["ak_ve"] = {0: 0, 1: 0}
    G["ak_se"] = {0: 0, 1: 0}
    G["av_ve"] = {0: 0, 1: 0}
    G["akf_te"] = {0: 0, 1: 0, 2: 0, 3: 0}
    G["aphi"] = {0: 0, 1: 0, 2: 0, 3: 0}
    G["apt2"] = {0: 0, 1: 0, 2: 0, 3: 0}
    G["axa_te"] = 0
    G["asgn_gp"] = 0
    G["as1_gp"] = {0: 0, 1: 0}
    G["akvp"] = {0: 0, 1: 0}
    G["av65_prev"] = 0
    G["anorm_se"] = 0
    G["a_pre_se"] = 0

    for t in range(NT if stop != "A1" else 1):
        xt = xa0
        lx = "LX%d" % (t % 2)
        for kc in range(KC):
            ld(xt[:, kc * TT:(kc + 1) * TT],
               x_t[kc * 128:(kc + 1) * 128, t * TT:(t + 1) * TT], lx,
               waits=[("TE", G["axa_te"])])
        x_ld = p.c(lx)

        for oc in range(KC):
            pk = PS[oc % 2]
            for kc in range(KC):
                lhs = wk_sb[:, kc * D + oc * 128: kc * D + (oc + 1) * 128]
                rhs = xt[:, kc * TT:(kc + 1) * TT]
                p.op("tensor",
                     lambda e, o=pk, l=lhs, r=rhs, s=(kc == 0), z=(kc == KC - 1):
                     e.matmul(o, l, r, start=s, stop=z),
                     sem="TE" if kc == KC - 1 else None,
                     waits=([(lx, x_ld), ("LW", a_w_ld),
                             ("VE", G["ak_ve"][oc % 2]),
                             ("SE", G["ak_se"][oc % 2])] if kc == 0 else []))
            te_k = p.c("TE")
            p.op("vector", lambda e, d=ktile[:, oc * TT:(oc + 1) * TT], s=pk:
                 e.tensor_copy(d, s), sem="VE", waits=[("TE", te_k)])
            ve_kcp = p.c("VE")
            p.op("scalar", lambda e, d=ksq[:, oc * TT:(oc + 1) * TT],
                 s=ktile[:, oc * TT:(oc + 1) * TT]:
                 e.activation(d, s, AF.Square, bias=czero, scale=one),
                 sem="SE", waits=[("VE", ve_kcp)])
            G["ak_ve"][oc % 2] = p.c("VE")
            G["ak_se"][oc % 2] = p.c("SE")
        k_evac_ve = p.c("VE")
        k_evac_se = p.c("SE")

        for tb in range(NTB):
            if "n" not in amask:
                continue
            pn = PS[6]
            for kc in range(KC):
                lhs = ksq[:, kc * TT + tb * TB: kc * TT + (tb + 1) * TB]
                rhs = maskh[:, kc * H:(kc + 1) * H]
                p.op("tensor",
                     lambda e, o=pn[:, 0:H], l=lhs, r=rhs, s=(kc == 0), z=(kc == KC - 1):
                     e.matmul(o, l, r, start=s, stop=z),
                     sem="TE" if kc == KC - 1 else None,
                     waits=([("SE", k_evac_se), ("SE", G["anorm_se"]),
                             ("VE", G["akvp"][1])]
                            if kc == 0 else []))
            te_n = p.c("TE")
            p.op("scalar", lambda e, d=scale_a[:, tb * H:(tb + 1) * H], s=pn[:, 0:H]:
                 e.activation(d, s, AF.Exp,
                              bias=consts[:, C_MLN16:C_MLN16 + 1],
                              scale=consts[:, C_MHALF:C_MHALF + 1]),
                 sem="SE", waits=[("TE", te_n), ("VE", G["av65_prev"])])
            G["anorm_se"] = p.c("SE")
            se_scale = p.c("SE")

            if "v" not in amask:
                continue
            vb = v65a[:, tb * H * 65:(tb + 1) * H * 65]
            for nh in range(2):
                pv = PS[nh]
                for kc in range(KC):
                    lhs = xt[:, kc * TT + tb * TB: kc * TT + (tb + 1) * TB]
                    rhs = wv_sb[:, kc * D + nh * 512: kc * D + (nh + 1) * 512]
                    p.op("tensor",
                         lambda e, o=pv, l=lhs, r=rhs, s=(kc == 0), z=(kc == KC - 1):
                         e.matmul(o, l, r, start=s, stop=z),
                         sem="TE" if kc == KC - 1 else None,
                         waits=([("VE", G["av_ve"][nh]), ("VE", k_evac_ve),
                                 ("SE", k_evac_se)] if kc == 0 else []))
                te_v = p.c("TE")
                for hh in range(8):
                    h = nh * 8 + hh
                    p.op("vector",
                         lambda e, d=vb[:, h * 65: h * 65 + 64],
                         s=pv[:, hh * 64:(hh + 1) * 64],
                         sc=scale_a[:, tb * H + h:tb * H + h + 1]:
                         e.tensor_scalar(d, s, sc, None, ALU.mult),
                         sem="VE", waits=[("TE", te_v), ("SE", se_scale)])
                G["av_ve"][nh] = p.c("VE")
            p.op("vector",
                 lambda e, d=vb.rearrange("p (h c) -> p h c", c=65)[:, :, 64:65],
                 s=scale_a[:, tb * H:(tb + 1) * H].rearrange(
                     "p (h o) -> p h o", o=1):
                 e.tensor_copy(d, s), sem="VE")
        v65_ve = p.c("VE")
        G["a_pre_se"] = p.c("SE")
        G["axa_te"] = p.c("TE")

        if "v" in amask and "n" in amask:
            items = [(hp, tb) for hp in range(8) for tb in range(NTB)]
            te_phi = {}
            se_cos = {}
            gp_sin = {}
            te_stop = {}

            def emit_phi(i):
                hp, tb = items[i]
                par = i % 4
                pph = PS[(2, 3, 0, 1)[par]]
                lhs = ktile[:, hp * TT + tb * TB: hp * TT + (tb + 1) * TB]
                p.op("tensor", lambda e, o=pph, l=lhs:
                     e.matmul(o, l, rf_bd, start=True, stop=True),
                     sem="TE",
                     waits=[("VE", G["aphi"][par]), ("VE", k_evac_ve),
                            ("VE", v65_ve)])
                te_phi[i] = p.c("TE")

            def emit_epilogue(i):
                par = i % 4
                pph = PS[(2, 3, 0, 1)[par]]
                kfb = kfps[par]
                p0m = ap0[par]
                s1m = as1[i % 2]
                p.op("vector", lambda e, d=pti, s=pph:
                     e.tensor_scalar(d, s, 1.0 / (2 * PI), None, ALU.mult),
                     sem="VE", waits=[("TE", te_phi[i])])
                p.op("vector", lambda e, d=p0m, a=pti, s=pph:
                     e.scalar_tensor_tensor(d, a, -2 * PI, s, ALU.mult, ALU.add),
                     sem="VE", waits=[("SE", G["apt2"][par])])
                G["aphi"][par] = p.c("VE")
                ve_p0 = p.c("VE")
                p.op("vector", lambda e, d=actc, s=p0m:
                     e.tensor_scalar(d, s, 0.0, None, ALU.is_ge), sem="VE")
                p.op("vector", lambda e, d=asgn, s=actc:
                     e.tensor_scalar(d, s, 2.0, -1.0, ALU.mult, ALU.add),
                     sem="VE", waits=[("GP", G["asgn_gp"])])
                ve_sgn = p.c("VE")
                p.op("scalar", lambda e, d=au, s=p0m:
                     e.activation(d, s, AF.Abs, bias=czero, scale=one),
                     sem="SE", waits=[("VE", ve_p0)])
                p.op("scalar", lambda e, d=s1m, s=au:
                     e.activation(d, s, AF.Sin, bias=consts[:, C_PI:C_PI + 1],
                                  scale=consts[:, C_MONE:C_MONE + 1]),
                     sem="SE", waits=[("GP", G["as1_gp"][i % 2])])
                se_s1 = p.c("SE")
                gp_h = []
                se_c = []

                def gp_half(hh):
                    p.op("gpsimd", lambda e,
                         d=kfb[:, hh * M2 + MF:(hh + 1) * M2],
                         a=asgn[:, hh * MF:(hh + 1) * MF],
                         b=s1m[:, hh * MF:(hh + 1) * MF]:
                         e.tensor_tensor(d, a, b, ALU.mult), sem="GP",
                         waits=[("SE", se_s1), ("VE", ve_sgn),
                                ("GP", p.c("GP")), ("TE", G["akf_te"][par])])
                    gp_h.append(p.c("GP"))

                def cos_half(hh):
                    p.op("scalar", lambda e,
                         d=kfb[:, hh * M2:hh * M2 + MF],
                         s=p0m[:, hh * MF:(hh + 1) * MF]:
                         e.activation(d, s, AF.Sin,
                                      bias=consts[:, C_MHPI:C_MHPI + 1],
                                      scale=one),
                         sem="SE", waits=[("TE", G["akf_te"][par])])
                    se_c.append(p.c("SE"))

                gp_half(0)
                p.op("scalar", lambda e, d=p0m, s=au:
                     e.activation(d, s, AF.Abs, bias=consts[:, C_MPI:C_MPI + 1],
                                  scale=one),
                     sem="SE", waits=[("VE", ve_sgn)])
                cos_half(0)
                gp_half(1)
                cos_half(1)
                gp_sin[i] = tuple(gp_h)
                se_cos[i] = tuple(se_c)
                G["asgn_gp"] = gp_h[1]
                G["as1_gp"][i % 2] = gp_h[1]
                G["apt2"][par] = se_c[1]

            def emit_kv(i):
                hp, tb = items[i]
                par = i % 4
                kfb = kfps[par]
                bank = 4 + 2 * (hp % 2)
                for j in range(2):
                    h = 2 * hp + j
                    pk = PS[bank + j]
                    w = [("SE", se_cos[i][j]), ("GP", gp_sin[i][j])]
                    if tb == 0:
                        w += [("VE", G["akvp"][hp % 2]), ("VE", v65_ve),
                              ("SE", G["a_pre_se"])]
                    p.op("tensor", lambda e, o=pk[0:65, :],
                         l=v65a[:, (tb * H + h) * 65:(tb * H + h) * 65 + 65],
                         r=kfb[:, j * M2:(j + 1) * M2]:
                         e.matmul(o, l, r, start=(tb == 0), stop=(tb == NTB - 1)),
                         sem="TE" if j == 1 else None, waits=w)
                G["akf_te"][par] = p.c("TE")
                if tb == NTB - 1:
                    te_stop[hp] = p.c("TE")

            def emit_flush(hp):
                bank = 4 + 2 * (hp % 2)
                for j in range(2):
                    h = 2 * hp + j
                    p.op("vector", lambda e, d=kvacc[0:65, h * 512:(h + 1) * 512],
                         s=PS[bank + j][0:65, :]:
                         e.tensor_tensor(d, d, s, ALU.add),
                         sem="VE", waits=[("TE", te_stop[hp]), ("VE", kv_zero_ve)])
                G["akvp"][hp % 2] = p.c("VE")

            emit_phi(0)
            emit_phi(1)
            pend_flush = None
            for i, (hp, tb) in enumerate(items):
                if i + 2 < len(items):
                    emit_phi(i + 2)
                emit_epilogue(i)
                if pend_flush is not None:
                    emit_flush(pend_flush)
                    pend_flush = None
                emit_kv(i)
                if tb == NTB - 1:
                    pend_flush = hp
            if pend_flush is not None:
                emit_flush(pend_flush)
        G["xa"][t % 2] = p.c("TE")
        G["ak_ve"] = {0: p.c("VE"), 1: p.c("VE")}
        G["ak_se"] = {0: p.c("SE"), 1: p.c("SE")}
        G["av_ve"] = {0: p.c("VE"), 1: p.c("VE")}
        G["av65_prev"] = p.c("VE")

    p.op("vector", lambda e: e.memset(kvfm, 0.0), sem="VE",
         waits=[("TE", p.c("TE"))])
    for h in (range(H) if "t" in amask else []):
        for fc in range(4):
            pt_ps = PS[6]
            p.op("tensor", lambda e, o=pt_ps[:, 0:65],
                 i=kvacc[0:65, h * 512 + fc * 128: h * 512 + (fc + 1) * 128]:
                 e.transpose(o, i, ident[0:65, 0:65]),
                 sem="TE", waits=[("VE", p.c("VE")), ("SE", p.c("SE"))])
            te_t = p.c("TE")
            col = (h * 4 + fc) * 65
            if h % 2 == 0:
                p.op("vector", lambda e, d=kvfm[:, col:col + 64], s=pt_ps[:, 0:64]:
                     e.tensor_copy(d, s), sem="VE", waits=[("TE", te_t)])
                p.op("vector", lambda e, d=kvfm[:, col + 64:col + 65],
                     s=pt_ps[:, 64:65]:
                     e.tensor_copy(d, s), sem="VE")
            else:
                p.op("vector", lambda e, d=kvfm[:, col:col + 1],
                     s=pt_ps[:, 64:65]:
                     e.tensor_copy(d, s), sem="VE", waits=[("TE", te_t)])
                p.op("vector", lambda e, d=kvfm[:, col + 1:col + 65],
                     s=pt_ps[:, 0:64]:
                     e.tensor_copy(d, s), sem="VE")
    kvfm_ve = p.c("VE")

    if stop in ("A", "A1"):
        if debug:
            st(dbg[:, 0:H * 512], kvacc, "SO", waits=[("VE", kvfm_ve)])
            st(dbg[:, 8192:8192 + H * 4 * 65], kvfm, "SO", waits=[("VE", kvfm_ve)])
            st(dbg[:, 14336:14336 + 2048], ktile[:, 0:2048], "SO",
               waits=[("VE", kvfm_ve)])
        _finish(nc, p)
        return nc

    p.barrier_eng("sync")
    for kc in range(KC):
        ld(wq_sb[:, kc * D:(kc + 1) * D], wq_t[kc * 128:(kc + 1) * 128, :], "LW")
        ld(wo_sb[:, kc * D:(kc + 1) * D], wo_t[kc * 128:(kc + 1) * 128, :], "LW")
    b_w_ld = p.c("LW")

    G["xb"] = {0: 0, 1: 0}
    G["bxb"] = 0
    G["bqt_ve"] = {0: 0, 1: 0}
    G["bq_ve"] = {0: 0, 1: 0}
    G["bq_se"] = {0: 0, 1: 0}
    G["bqf_te"] = {0: 0, 1: 0}
    G["bqkv_ve"] = {0: 0, 1: 0}
    G["bphiB"] = {2: 0, 3: 0, 6: 0, 7: 0}
    G["bp0"] = {0: 0, 1: 0}
    G["bsgn_gp"] = {0: 0, 1: 0}
    G["bz_zg"] = {}
    G["bn_ve"] = 0
    G["brz_ve"] = 0
    G["bw_ve"] = {0: 0, 1: 0}
    G["y_ve"] = 0

    for t in range(NT):
        xt = xb0
        qtile = qtiles[t % 2]
        lx = "LX%d" % (t % 2)
        for kc in range(KC):
            ld(xt[:, kc * TT:(kc + 1) * TT],
               x_t[kc * 128:(kc + 1) * 128, t * TT:(t + 1) * TT], lx,
               waits=[("VE", kvfm_ve), ("VE", G["bxb"])])
        x_ld = p.c(lx)

        for oc in range(KC):
            pq = PS[oc % 2]
            for kc in range(KC):
                lhs = wq_sb[:, kc * D + oc * 128: kc * D + (oc + 1) * 128]
                rhs = xt[:, kc * TT:(kc + 1) * TT]
                p.op("tensor",
                     lambda e, o=pq, l=lhs, r=rhs, s=(kc == 0), z=(kc == KC - 1):
                     e.matmul(o, l, r, start=s, stop=z),
                     sem="TE" if kc == KC - 1 else None,
                     waits=([(lx, x_ld), ("LW", b_w_ld),
                             ("VE", G["bq_ve"][oc % 2]),
                             ("SE", G["bq_se"][oc % 2])] if kc == 0 else []))
            te_q = p.c("TE")
            p.op("vector", lambda e, d=qtile[:, oc * TT:(oc + 1) * TT], s=pq:
                 e.tensor_copy(d, s), sem="VE",
                 waits=[("TE", te_q), ("VE", G["bqt_ve"][t % 2])])
            ve_qcp = p.c("VE")
            p.op("scalar", lambda e, d=qsq[:, oc * TT:(oc + 1) * TT],
                 s=qtile[:, oc * TT:(oc + 1) * TT]:
                 e.activation(d, s, AF.Square, bias=czero, scale=one),
                 sem="SE", waits=[("VE", ve_qcp), ("SX", G.get("sx_prev", 0))])
            G["bq_ve"][oc % 2] = p.c("VE")
            G["bq_se"][oc % 2] = p.c("SE")
        q_evac_ve = p.c("VE")
        q_evac_se = p.c("SE")

        for tb in range(NTB):
            pn = PS[6]
            for kc in range(KC):
                lhs = qsq[:, kc * TT + tb * TB: kc * TT + (tb + 1) * TB]
                rhs = maskh[:, kc * H:(kc + 1) * H]
                p.op("tensor",
                     lambda e, o=pn[:, 0:H], l=lhs, r=rhs, s=(kc == 0), z=(kc == KC - 1):
                     e.matmul(o, l, r, start=s, stop=z),
                     sem="TE" if kc == KC - 1 else None,
                     waits=([("SE", q_evac_se), ("VE", G["bn_ve"]),
                             ("SE", G.get("bthr_se", 0))]
                            if kc == 0 else []))
            te_n = p.c("TE")
            p.op("vector", lambda e, d=sc512[:, 0:H], s=pn[:, 0:H]:
                 e.tensor_copy(d, s), sem="VE", waits=[("TE", te_n)])
            ve_nq = p.c("VE")
            p.op("tensor", lambda e, o=pn[0:H, 0:TB], i=sc512[:, 0:H]:
                 e.transpose(o, i, ident), sem="TE", waits=[("VE", ve_nq)])
            te_tr = p.c("TE")
            p.op("vector", lambda e, d=sc512[0:H, 0:TB], s=pn[0:H, 0:TB]:
                 e.tensor_scalar(d, s, 0.5, LN_THR, ALU.mult, ALU.add),
                 sem="VE", waits=[("TE", te_tr)])
            p.op("vector", lambda e, d=sc512[0:H, 0:TB]:
                 e.tensor_scalar(d, d, 80.0, None, ALU.min), sem="VE")
            ve_thr = p.c("VE")
            p.op("scalar", lambda e, d=thr_fm[:, tb * TB:(tb + 1) * TB],
                 s=sc512[0:H, 0:TB]:
                 e.activation(d, s, AF.Exp, bias=czero[0:H], scale=one[0:H]),
                 sem="SE", waits=[("VE", ve_thr), ("TE", te_tr)])
            G["bthr_se"] = p.c("SE")
            G["bn_ve"] = p.c("VE")
        thr_se = p.c("SE")

        def b_flush(pend):
            h_, te_qkv_, oc_ = pend
            even_ = (h_ % 2 == 0)
            pk_ = PS[4] if even_ else PS[5]
            if even_:
                pay = pk_[0:64, :]; zrow = pk_[64:65, :]
                drow = attn[0:64, oc_ * TT:(oc_ + 1) * TT]
                zdst = zst[64:65, 0:TT]
            else:
                pay = pk_[64:128, :]; zrow = pk_[0:1, :]
                drow = attn[64:128, oc_ * TT:(oc_ + 1) * TT]
                zdst = zst[0:1, 0:TT]
            p.op("vector", lambda e, d=drow, s=pay: e.tensor_copy(d, s),
                 sem="VE", waits=[("TE", te_qkv_)])
            p.op("vector", lambda e, d=zdst, s=zrow: e.tensor_copy(d, s),
                 sem="VE", waits=[("ZG", G["bz_zg"].get(h_ // 2, 0))])
            ve_z = p.c("VE")
            p.op("scalar", lambda e, d=zall[h_:h_ + 1, :], s=zdst:
                 e.dma_start(out=d, in_=s), sem="ZG", by=16,
                 waits=[("VE", ve_z)])
            G["bqkv_ve"][h_ % 2] = p.c("VE")
            if h_ % 2 == 1:
                G["bz_zg"][h_ // 2 + 1] = p.c("ZG")
        te_phiH = {}
        se_cosH = {}
        gp_sinH = {}

        def b_phi(h):
            oc_, hi_ = h // 2, h % 2
            base = 0 if hi_ == 0 else 64
            banks = (2, 3) if h % 2 == 0 else (6, 7)
            tp = (0, 0) if hi_ == 0 else (64, 0)
            tps = []
            for mc in range(2):
                pps = PS[banks[mc]]
                lhs = rf_rep[base:base + 64, mc * 128:(mc + 1) * 128]
                rhs = qtile[base:base + 64, oc_ * TT:(oc_ + 1) * TT]
                p.op("tensor", lambda e, o=pps, l=lhs, r=rhs, tp_=tp:
                     e.matmul(o, l, r, start=True, stop=True, tile_position=tp_),
                     sem="TE",
                     waits=[("VE", G["bphiB"][banks[mc]]), ("VE", q_evac_ve)])
                tps.append(p.c("TE"))
            te_phiH[h] = tps

        def b_epi(h):
            qf = qf0 if h % 2 == 0 else qf1
            banks = (2, 3) if h % 2 == 0 else (6, 7)
            for mc in range(2):
                pps = PS[banks[mc]]
                p0m, um, sgnm = bp0[mc], bu[mc], bsgn[mc]
                s1m, a2m = bs1[mc], ba2[mc]
                p.op("vector", lambda e, d=ptbi, s=pps:
                     e.tensor_scalar(d, s, 1.0 / (2 * PI), None, ALU.mult),
                     sem="VE", waits=[("TE", te_phiH[h][mc])])
                p.op("vector", lambda e, d=p0m, a=ptbi, s=pps:
                     e.scalar_tensor_tensor(d, a, -2 * PI, s, ALU.mult, ALU.add),
                     sem="VE", waits=[("SE", G["bp0"][mc])])
                G["bphiB"][banks[mc]] = p.c("VE")
                ve_p0 = p.c("VE")
                p.op("vector", lambda e, d=actcB, s=p0m:
                     e.tensor_scalar(d, s, 0.0, None, ALU.is_ge), sem="VE")
                p.op("vector", lambda e, d=sgnm, s=actcB:
                     e.tensor_scalar(d, s, 2.0, -1.0, ALU.mult, ALU.add),
                     sem="VE", waits=[("GP", G["bsgn_gp"][mc])])
                ve_sgn = p.c("VE")
                p.op("scalar", lambda e, d=um, s=p0m:
                     e.activation(d, s, AF.Abs, bias=czero, scale=one),
                     sem="SE", waits=[("VE", ve_p0)])
                G["bp0"][mc] = p.c("SE")
                p.op("scalar", lambda e, d=s1m, s=um:
                     e.activation(d, s, AF.Sin, bias=consts[:, C_PI:C_PI + 1],
                                  scale=consts[:, C_MONE:C_MONE + 1]),
                     sem="SE", waits=[("GP", G["bsgn_gp"][mc])])
                se_s1 = p.c("SE")
                p.op("gpsimd", lambda e, d=qf[:, (2 + mc) * TT:(3 + mc) * TT],
                     a=sgnm, b=s1m:
                     e.tensor_tensor(d, a, b, ALU.mult), sem="GP",
                     waits=[("SE", se_s1), ("VE", ve_sgn), ("GP", p.c("GP")),
                            ("TE", G["bqf_te"][h % 2])])
                G["bsgn_gp"][mc] = p.c("GP")
                p.op("scalar", lambda e, d=a2m, s=um:
                     e.activation(d, s, AF.Abs, bias=consts[:, C_MPI:C_MPI + 1],
                                  scale=one), sem="SE")
                p.op("scalar", lambda e, d=qf[:, mc * TT:(mc + 1) * TT], s=a2m:
                     e.activation(d, s, AF.Sin, bias=consts[:, C_MHPI:C_MHPI + 1],
                                  scale=one),
                     sem="SE", waits=[("TE", G["bqf_te"][h % 2])])
                se_cosH[(h, mc)] = p.c("SE")
                gp_sinH[(h, mc)] = p.c("GP")
            se_cosH[h] = p.c("SE")
            gp_sinH[h] = p.c("GP")

        def b_qkv(h):
            qf = qf0 if h % 2 == 0 else qf1
            even = (h % 2 == 0)
            pk = PS[4] if even else PS[5]

            def fcw(fc):
                if fc < 2:
                    w = [("SE", se_cosH[(h, fc)])]
                else:
                    w = [("GP", gp_sinH[(h, fc - 2)])]
                if fc == 0:
                    w.append(("VE", G["bqkv_ve"][h % 2]))
                return w
            if even:
                for fc in range(4):
                    lhs = kvfm[:, (h * 4 + fc) * 65:(h * 4 + fc) * 65 + 65]
                    rhs = qf[:, fc * TT:(fc + 1) * TT]
                    p.op("tensor",
                         lambda e, o=pk[0:65, :], l=lhs, r=rhs,
                         s=(fc == 0), z=(fc == 3):
                         e.matmul(o, l, r, start=s, stop=z),
                         sem="TE" if fc == 3 else None, waits=fcw(fc))
            else:
                for fc in range(4):
                    lhs = kvfm[:, (h * 4 + fc) * 65 + 1:(h * 4 + fc) * 65 + 65]
                    rhs = qf[:, fc * TT:(fc + 1) * TT]
                    p.op("tensor",
                         lambda e, o=pk[64:128, :], l=lhs, r=rhs,
                         s=(fc == 0), z=(fc == 3):
                         e.matmul(o, l, r, start=s, stop=z),
                         sem=None, waits=fcw(fc))
                for fc in range(4):
                    lhs = kvfm[:, (h * 4 + fc) * 65:(h * 4 + fc) * 65 + 1]
                    rhs = qf[:, fc * TT:(fc + 1) * TT]
                    p.op("tensor",
                         lambda e, o=pk[0:1, :], l=lhs, r=rhs,
                         s=(fc == 0), z=(fc == 3):
                         e.matmul(o, l, r, start=s, stop=z),
                         sem="TE" if fc == 3 else None)
            te_qkv = p.c("TE")
            G["bqf_te"][h % 2] = te_qkv
            return te_qkv

        G["bphiB"][6] = p.c("VE")
        b_phi(0)
        pend = None
        for h in range(H):
            if h + 1 < H:
                b_phi(h + 1)
            b_epi(h)
            if pend is not None:
                b_flush(pend)
                pend = None
            te_qkv = b_qkv(h)
            pend = (h, te_qkv, h // 2)
        b_flush(pend)
        pend = None
        zg_all = p.c("ZG")
        attn_ve = p.c("VE")

        p.op("vector", lambda e: e.tensor_tensor(zall[0:H, :], zall[0:H, :], thr_fm, ALU.max),
             sem="VE", waits=[("ZG", zg_all), ("SE", thr_se)])
        p.op("vector", lambda e: e.reciprocal(zall[0:H, :], zall[0:H, :]), sem="VE")
        rz_ve = p.c("VE")

        for oc in range(KC):
            prz = PS[7]
            p.op("tensor", lambda e, o=prz, l=sel_all[:, oc * 128:(oc + 1) * 128],
                 r=zall:
                 e.matmul(o, l, r, start=True, stop=True),
                 sem="TE", waits=[("VE", rz_ve), ("VE", G["brz_ve"])])
            te_rz = p.c("TE")
            p.op("vector", lambda e, d=attn[:, oc * TT:(oc + 1) * TT], s=prz:
                 e.tensor_tensor(d, d, s, ALU.mult),
                 sem="VE", waits=[("TE", te_rz), ("VE", attn_ve)])
            G["brz_ve"] = p.c("VE")
        G["bphiB"][7] = p.c("VE")
        attn_scaled_ve = p.c("VE")

        for oc in range(KC):
            pw = PS[oc % 2]
            for kc in range(KC):
                lhs = wo_sb[:, kc * D + oc * 128: kc * D + (oc + 1) * 128]
                rhs = attn[:, kc * TT:(kc + 1) * TT]
                p.op("tensor",
                     lambda e, o=pw, l=lhs, r=rhs, s=(kc == 0), z=(kc == KC - 1):
                     e.matmul(o, l, r, start=s, stop=z),
                     sem="TE" if kc == KC - 1 else None,
                     waits=([("VE", attn_scaled_ve), ("VE", G["bw_ve"][oc % 2])]
                            if kc == 0 else []))
            te_w = p.c("TE")
            p.op("vector", lambda e, d=qtile[:, oc * TT:(oc + 1) * TT],
                 s=pw, xx=xt[:, oc * TT:(oc + 1) * TT]:
                 e.tensor_tensor(d, xx, s, ALU.add),
                 sem="VE", waits=[("TE", te_w)])
            G["bw_ve"][oc % 2] = p.c("VE")
        G["y_ve"] = p.c("VE")
        G["bxb"] = p.c("VE")

        def ln1_out(oc, ve_x, _t=t):
            st(xln_s[oc * 128:(oc + 1) * 128, _t * TT:(_t + 1) * TT],
               qsq[:, oc * TT:(oc + 1) * TT], "SX", waits=[("VE", ve_x)])
            p.op("vector", lambda e, d=xmaxst[:, _t * KC + oc: _t * KC + oc + 1],
                 s=qsq[:, oc * TT:(oc + 1) * TT]:
                 e.tensor_reduce(d, s, AX.X, ALU.max, apply_absolute_value=True),
                 sem="VE")
        ln_emit(qtile, qsq, qsq, ln1_out, PS[6], PS[7])
        G["sx_prev"] = p.c("SX")
        G["bqt_ve"][t % 2] = p.c("VE")
        G["bn_ve"] = p.c("VE")
        G["brz_ve"] = p.c("VE")
        G["bw_ve"] = {0: p.c("VE"), 1: p.c("VE")}
        G["bqkv_ve"] = {0: p.c("VE"), 1: p.c("VE")}
        G["bqf_te"] = {0: p.c("TE"), 1: p.c("TE")}
    b_st_all = p.c("SX")
    b_ve_all = p.c("VE")

    if stop == "B":
        if debug:
            st(dbg[:, 0:KC * TT], attn, "SO", waits=[("VE", b_ve_all)])
            st(dbg[0:H, 4096:4096 + TT], zall[0:H, :], "SO",
               waits=[("VE", b_ve_all)])
            st(dbg[0:H, 4608:4608 + TT], thr_fm, "SO", waits=[("VE", b_ve_all)])
            st(dbg[:, 5120:5120 + KC], xmaxst[:, (NT - 1) * KC:NT * KC],
               "SO", waits=[("VE", b_ve_all)])
        p.wait("gpsimd", [("SX", b_st_all)])
        for kc in range(KC):
            st(y_t[kc * 128:(kc + 1) * 128, :], xln_s[kc * 128:(kc + 1) * 128, :],
               "SO")
        _finish(nc, p)
        return nc

    p.op("vector", lambda e: e.tensor_reduce(m1col, xmaxst, AX.X, ALU.max),
         sem="VE", waits=[("VE", b_ve_all)])
    ve_m = p.c("VE")
    p.op("tensor", lambda e, o=PS[6][0:1, 0:128], i=m1col:
         e.transpose(o, i, ident), sem="TE",
         waits=[("VE", ve_m), ("SE", p.c("SE"))])
    te_m = p.c("TE")
    p.op("vector", lambda e: e.tensor_reduce(r1x1, PS[6][0:1, 0:128], AX.X, ALU.max),
         sem="VE", waits=[("TE", te_m)])
    p.op("vector", lambda e: e.tensor_scalar(ccst, onesrow[0:1, 0:8],
                                             r1x1[0:1, 0:1], None, ALU.mult),
         sem="VE")
    ve_cc = p.c("VE")
    p.op("gpsimd", lambda e: e.dma_start(out=cc_in1.ap(), in_=ccst),
         sem="SC", by=16, waits=[("VE", ve_cc)])
    p.wait("gpsimd", [("SC", p.c("SC"))])
    p.op("gpsimd", lambda e: e.collective_compute(
        "AllReduce", ALU.max, replica_groups=[[0, 1, 2, 3, 4, 5, 6, 7]],
        ins=[cc_in1.ap().opt()], outs=[cc_out1.ap().opt()]), sem="GE")
    p.wait("gpsimd", [("GE", p.c("GE"))])
    p.op("gpsimd", lambda e: e.dma_start(out=gmax1, in_=cc_out1.ap()[0:1, 0:1]),
         sem="SC", by=16)
    g1_st = p.c("SC")
    p.op("vector", lambda e: e.tensor_tensor(r1x1, gmax1, sig2[0:1, 0:1], ALU.mult),
         sem="VE", waits=[("SC", g1_st)])
    p.op("vector", lambda e: e.reciprocal(r1x1b, r1x1), sem="VE")
    p.op("vector", lambda e: e.tensor_scalar(r1x1b, r1x1b, 127.0, None, ALU.mult),
         sem="VE")
    p.op("vector", lambda e: e.tensor_scalar(r1x1, r1x1, INV127 * INV127, None,
                                             ALU.mult), sem="VE")
    ve_rx = p.c("VE")
    p.op("tensor", lambda e, o=PS[6][:, 0:1]: e.matmul(o, onesrow, r1x1b,
                                                       start=True, stop=True),
         sem="TE", waits=[("VE", ve_rx)])
    p.op("tensor", lambda e, o=PS[7][:, 0:1]: e.matmul(o, onesrow, r1x1,
                                                       start=True, stop=True),
         sem="TE")
    te_rx = p.c("TE")
    p.op("vector", lambda e: e.tensor_copy(rx_col, PS[6][:, 0:1]),
         sem="VE", waits=[("TE", te_rx)])
    p.op("vector", lambda e: e.tensor_copy(sxs_col, PS[7][:, 0:1]), sem="VE")
    rx_ve = p.c("VE")

    if stop == "AR1":
        if debug:
            st(dbg[0:1, 0:1], gmax1, "SO", waits=[("VE", rx_ve)])
            st(dbg[:, 4:5], rx_col, "SO", waits=[("VE", rx_ve)])
            st(dbg[:, 8:9], m1col, "SO", waits=[("VE", rx_ve)])
            st(dbg[0:1, 16:24], ccst, "SO", waits=[("VE", rx_ve)])
            st(dbg[:, 32:96], xmaxst, "SO", waits=[("VE", rx_ve)])
        _finish(nc, p)
        return nc

    p.barrier_eng("sync")

    def quant_weight(wnat, wtt, nch_nat, nat_w, sall, tgt, nbufs):
        wout = nch_nat * 128
        kcin = nat_w // 128
        nb = len(nbufs)
        G["fqn_ve"] = {i: 0 for i in range(nb)}
        for j in range(nch_nat):
            buf = nbufs[j % nb]
            lh = "LH%d" % (j % 4)
            ld(buf, wnat[j * 128:(j + 1) * 128, :], lh,
               waits=[("VE", G["fqn_ve"][j % nb]),
                      ("VE", G.get("fq_ve", 0)), ("VE", G.get("fqm_ve", 0)),
                      ("VE", G.get("fqq_ve", 0)),
                      ("SE", G.get("fq_se", 0))])
            ldj = p.c(lh)
            p.op("vector", lambda e, d=sall[:, j:j + 1], s=buf:
                 e.tensor_reduce(d, s, AX.X, ALU.max, apply_absolute_value=True),
                 sem="VE", waits=[(lh, ldj)])
            G["fqn_ve"][j % nb] = p.c("VE")
        G["fq_ve"] = p.c("VE")
        p.op("vector", lambda e, d=mred[:, 0:nch_nat], s=sall[:, 0:nch_nat]:
             e.reciprocal(d, s), sem="VE")
        p.op("vector", lambda e, d=mred[:, 0:nch_nat]:
             e.tensor_scalar(d, d, 127.0, None, ALU.mult), sem="VE")
        ve_r = p.c("VE")
        p.op("tensor", lambda e, o=PS[6][0:nch_nat, 0:128], i=mred[:, 0:nch_nat]:
             e.transpose(o, i, ident), sem="TE",
             waits=[("VE", ve_r), ("VE", G.get("fqb_ve", 0))])
        te_r = p.c("TE")
        p.op("vector", lambda e, d=tp22[0:nch_nat, :], s=PS[6][0:nch_nat, 0:128]:
             e.tensor_copy(d, s), sem="VE", waits=[("TE", te_r)])
        ve_t = p.c("VE")
        for j in range(nch_nat):
            p.op("scalar", lambda e, d=rb[0:1, j * 128:(j + 1) * 128],
                 s=tp22[j:j + 1, :]:
                 e.dma_start(out=d, in_=s), sem="ZG", by=16,
                 waits=[("VE", ve_t)])
        zg_r = p.c("ZG")
        nslab = (wout + 511) // 512
        for s_ in range(nslab):
            wlen = min(512, wout - s_ * 512)
            p.op("tensor", lambda e, o=PS[6][:, 0:wlen], l=onesrow,
                 r=rb[0:1, s_ * 512:s_ * 512 + wlen]:
                 e.matmul(o, l, r, start=True, stop=True),
                 sem="TE", waits=[("ZG", zg_r), ("VE", G.get("fqb_ve", 0))])
            te_b = p.c("TE")
            p.op("vector", lambda e, d=rb[:, s_ * 512:s_ * 512 + wlen],
                 s=PS[6][:, 0:wlen]:
                 e.tensor_copy(d, s), sem="VE", waits=[("TE", te_b)])
            G["fqb_ve"] = p.c("VE")
        rb_ve = p.c("VE")
        hw = wout // 2
        G["fqs"] = {0: 0, 1: 0}
        gidx = 0
        for kc in range(kcin):
            mults = []
            for gi in range(2):
                slot = gidx % 2
                gidx += 1
                sbuf = wtc0[:, slot * hw:(slot + 1) * hw]
                lh = "LH%d" % (gidx % 4)
                ld(sbuf, wtt[kc * 128:(kc + 1) * 128, gi * hw:(gi + 1) * hw],
                   lh, waits=[("VE", G["fqs"][slot]), ("VE", G.get("fq_ve", 0))])
                ldc = p.c(lh)
                p.op("vector", lambda e, d=avq[:, gi * hw:(gi + 1) * hw],
                     a=sbuf, b=rb[:, gi * hw:(gi + 1) * hw]:
                     e.tensor_tensor(d, a, b, ALU.mult),
                     sem="VE", waits=[(lh, ldc), ("VE", rb_ve),
                                      ("VE", G.get("fqq_ve", 0))])
                G["fqs"][slot] = p.c("VE")
            ve_m_ = p.c("VE")
            G["fqm_ve"] = ve_m_
            p.op("scalar", lambda e, d=sgnq[:, 0:wout], s=avq[:, 0:wout]:
                 e.activation(d, s, AF.Sign, bias=czero, scale=one),
                 sem="SE", waits=[("VE", ve_m_), ("VE", G.get("fqq_ve", 0))])
            se_a = p.c("SE")
            G["fq_se"] = se_a
            p.op("vector", lambda e, d=t2q[:, 0:wout], a=sgnq[:, 0:wout],
                 b=avq[:, 0:wout]:
                 e.scalar_tensor_tensor(d, a, 128.0, b, ALU.mult, ALU.add),
                 sem="VE", waits=[("SE", se_a)])
            p.op("vector", lambda e, d=tgt[:, kc * wout:(kc + 1) * wout],
                 a=sgnq[:, 0:wout], b=t2q[:, 0:wout]:
                 e.scalar_tensor_tensor(d, a, -128.0, b, ALU.mult, ALU.add),
                 sem="VE")
            G["fqq_ve"] = p.c("VE")
        return p.c("VE")

    quant_weight(wg_n, wg_tt, ICN, D, sall_g, wgq,
                 [wtc0[:, 0:D], wtc0[:, D:2 * D], avq[:, 0:D], avq[:, D:2 * D]])
    quant_weight(wu_n, wu_tt, ICN, D, sall_u, wuq,
                 [wtc0[:, 0:D], wtc0[:, D:2 * D], avq[:, 0:D], avq[:, D:2 * D]])
    wdq_ve = quant_weight(wd_n, wd_tt, KC, I, sall_d, wdq,
                          [wtc0[:, 0:I], avq[:, 0:I], natd2, natd3])
    for kc in range(ICN):
        st(wdq_s[kc * 128:(kc + 1) * 128, :], wdq[:, kc * D:(kc + 1) * D],
           "SW", waits=[("VE", wdq_ve)])
    wq_st = p.c("SW")
    wq_ve = p.c("VE")

    p.op("vector", lambda e: e.tensor_scalar(ax_all, sall_g, sxs_col, None, ALU.mult),
         sem="VE", waits=[("VE", rx_ve)])
    p.op("vector", lambda e: e.tensor_scalar(bu_all, sall_u, sxs_col, None, ALU.mult),
         sem="VE")
    scales_ve = p.c("VE")

    if stop == "FFNQ":
        if debug:
            st(dbg[:, 0:5632], wgq[:, 0:5632], "SO", waits=[("VE", scales_ve)])
            st(dbg[:, 6000:6000 + ICN], sall_g, "SO", waits=[("VE", scales_ve)])
            st(dbg[:, 6030:6030 + ICN], ax_all, "SO", waits=[("VE", scales_ve)])
        _finish(nc, p)
        return nc

    p.barrier_eng("sync")
    G["f1x"] = {0: 0, 1: 0}
    G["f1g_se"] = [0, 0, 0, 0]
    G["f1h_ve"] = [0, 0, 0, 0]
    G["f1s_st"] = {}
    G["f1q_ve"] = 0
    for t in range(NT):
        xt = xl0 if t % 2 == 0 else xl1
        lx = "LX%d" % (t % 2)
        for kc in range(KC):
            ld(xt[:, kc * TT:(kc + 1) * TT],
               xln_s[kc * 128:(kc + 1) * 128, t * TT:(t + 1) * TT], lx,
               waits=[("TE", G["f1x"][t % 2]), ("SW", wq_st), ("VE", wq_ve),
                      ("SX", b_st_all), ("SE", G.get("f1a_se", 0))])
        x_ld = p.c(lx)
        p.op("scalar", lambda e, d=avf, s=xt:
             e.activation(d, s, AF.Abs, bias=czero, scale=rx_col),
             sem="SE", waits=[(lx, x_ld), ("VE", scales_ve), ("VE", G["f1q_ve"])])
        p.op("scalar", lambda e, d=sgnf, s=xt:
             e.activation(d, s, AF.Sign, bias=czero, scale=one), sem="SE")
        se_q = p.c("SE")
        G["f1a_se"] = se_q
        xqb = xqb2[t % 2]
        p.op("vector", lambda e, d=t2f, s=avf:
             e.tensor_scalar(d, s, 128.0, 256.49, ALU.add, ALU.min),
             sem="VE", waits=[("SE", se_q)])
        p.op("vector", lambda e, d=xqb, a=t2f, b=sgnf:
             e.scalar_tensor_tensor(d, a, -128.0, b, ALU.add, ALU.mult),
             sem="VE", waits=[("TE", G["f1x"][t % 2])])
        p.op("vector", lambda e, d=xqb: e.tensor_scalar(d, d, 127.0, None, ALU.min),
             sem="VE")
        xq_ve = p.c("VE")
        G["f1q_ve"] = xq_ve

        for ic in range(ICN):
            pg = PS[(ic % 2) * 2]
            pu = PS[(ic % 2) * 2 + 1]
            for kc in range(KC):
                lg = wgq[:, kc * I + ic * 128: kc * I + (ic + 1) * 128]
                lu = wuq[:, kc * I + ic * 128: kc * I + (ic + 1) * 128]
                rr = xqb[:, kc * TT:(kc + 1) * TT]
                p.op("tensor", lambda e, o=pg, l=lg, r=rr, s=(kc == 0),
                     z=(kc == KC - 1):
                     e.matmul(o, l, r, start=s, stop=z),
                     sem="TE" if kc == KC - 1 else None,
                     waits=([("VE", xq_ve), ("SE", G["f1g_se"][(ic % 2) * 2])]
                            if kc == 0 else []))
                p.op("tensor", lambda e, o=pu, l=lu, r=rr, s=(kc == 0),
                     z=(kc == KC - 1):
                     e.matmul(o, l, r, start=s, stop=z),
                     sem="TE" if kc == KC - 1 else None,
                     waits=([("VE", G["f1h_ve"][(ic % 2) * 2 + 1])]
                            if kc == 0 else []))
            te_gu = p.c("TE")
            sl = silu0 if ic % 2 == 0 else silu1
            hbuf = hb[ic % 4]
            p.op("scalar", lambda e, d=sl, s=pg, sc=ax_all[:, ic:ic + 1]:
                 e.activation(d, s, AF.Silu, bias=czero, scale=sc),
                 sem="SE", waits=[("TE", te_gu)])
            se_s = p.c("SE")
            G["f1g_se"][(ic % 2) * 2] = se_s
            p.op("vector", lambda e, d=hbuf, a=sl, b=pu:
                 e.tensor_tensor(d, a, b, ALU.mult),
                 sem="VE", waits=[("SE", se_s),
                                  ("SH%d" % (ic % 4), G["f1s_st"].get(ic % 4, 0))])
            ve_h = p.c("VE")
            G["f1h_ve"][(ic % 2) * 2] = ve_h
            G["f1h_ve"][(ic % 2) * 2 + 1] = ve_h
            st(h_s[ic * 128:(ic + 1) * 128, t * TT:(t + 1) * TT], hbuf,
               "SH%d" % (ic % 4), waits=[("VE", ve_h)])
            G["f1s_st"][ic % 4] = p.c("SH%d" % (ic % 4))
            p.op("vector", lambda e, d=hmaxst[:, t * ICN + ic: t * ICN + ic + 1],
                 s=hbuf:
                 e.tensor_reduce(d, s, AX.X, ALU.max, apply_absolute_value=True),
                 sem="VE")
        G["f1x"][t % 2] = p.c("TE")
        G["f1g_se"] = [p.c("SE")] * 4
        G["f1h_ve"] = [p.c("VE")] * 4
    f1_ve = p.c("VE")
    f1_st = tuple(p.c("SH%d" % i) for i in range(4))

    hview = hmaxst.rearrange("p (t c) -> p c t", c=ICN)
    p.op("vector", lambda e: e.tensor_reduce(mred, hview, AX.X, ALU.max),
         sem="VE", waits=[("VE", f1_ve)])
    p.op("vector", lambda e: e.tensor_tensor(mred, mred, bu_all, ALU.mult), sem="VE")
    p.op("vector", lambda e: e.tensor_reduce(m1col, mred, AX.X, ALU.max), sem="VE")
    ve_m2 = p.c("VE")
    p.op("tensor", lambda e, o=PS[6][0:1, 0:128], i=m1col:
         e.transpose(o, i, ident), sem="TE",
         waits=[("VE", ve_m2), ("SE", p.c("SE"))])
    te_m2 = p.c("TE")
    p.op("vector", lambda e: e.tensor_reduce(r1x1, PS[6][0:1, 0:128], AX.X, ALU.max),
         sem="VE", waits=[("TE", te_m2)])
    p.op("vector", lambda e: e.tensor_scalar(ccst, onesrow[0:1, 0:8],
                                             r1x1[0:1, 0:1], None, ALU.mult),
         sem="VE")
    ve_cc2 = p.c("VE")
    p.op("gpsimd", lambda e: e.dma_start(out=cc_in2.ap(), in_=ccst),
         sem="SC", by=16, waits=[("VE", ve_cc2)])
    p.wait("gpsimd", [("SC", p.c("SC"))])
    p.op("gpsimd", lambda e: e.collective_compute(
        "AllReduce", ALU.max, replica_groups=[[0, 1, 2, 3, 4, 5, 6, 7]],
        ins=[cc_in2.ap().opt()], outs=[cc_out2.ap().opt()]), sem="GE")
    p.wait("gpsimd", [("GE", p.c("GE"))])
    p.op("gpsimd", lambda e: e.dma_start(out=gmax2, in_=cc_out2.ap()[0:1, 0:1]),
         sem="SC", by=16)
    g2_st = p.c("SC")
    p.op("vector", lambda e: e.tensor_tensor(r1x1, gmax2, sig2[0:1, 1:2], ALU.mult),
         sem="VE", waits=[("SC", g2_st)])
    p.op("vector", lambda e: e.reciprocal(r1x1b, r1x1), sem="VE")
    p.op("vector", lambda e: e.tensor_scalar(r1x1b, r1x1b, 127.0, None, ALU.mult),
         sem="VE")
    p.op("vector", lambda e: e.tensor_scalar(r1x1, r1x1, INV127 * INV127, None,
                                             ALU.mult), sem="VE")
    ve_rh = p.c("VE")
    p.op("tensor", lambda e, o=PS[6][:, 0:1]: e.matmul(o, onesrow, r1x1b,
                                                       start=True, stop=True),
         sem="TE", waits=[("VE", ve_rh)])
    p.op("tensor", lambda e, o=PS[7][:, 0:1]: e.matmul(o, onesrow, r1x1,
                                                       start=True, stop=True),
         sem="TE")
    te_rh = p.c("TE")
    p.op("vector", lambda e: e.tensor_copy(rh_col, PS[6][:, 0:1]),
         sem="VE", waits=[("TE", te_rh)])
    p.op("vector", lambda e: e.tensor_copy(shs_col, PS[7][:, 0:1]), sem="VE")
    p.op("vector", lambda e: e.tensor_scalar(reff_all, bu_all, rh_col, None,
                                             ALU.mult), sem="VE")
    p.op("vector", lambda e: e.tensor_scalar(cd_all, sall_d, shs_col, None,
                                             ALU.mult), sem="VE")
    rh_ve = p.c("VE")

    if stop == "FFN1":
        if debug:
            st(dbg[0:1, 0:1], gmax2, "SO", waits=[("VE", rh_ve)])
            st(dbg[:, 4:4 + ICN], reff_all, "SO", waits=[("VE", rh_ve)])
            st(dbg[:, 30:30 + KC], cd_all, "SO", waits=[("VE", rh_ve)])
            st(dbg[:, 40:40 + NT * ICN], hmaxst, "SO", waits=[("VE", rh_ve)])
        _finish(nc, p)
        return nc

    p.barrier_eng("sync")
    for kc in range(ICN):
        ld(wdq2[:, kc * D:(kc + 1) * D], wdq_s[kc * 128:(kc + 1) * 128, :],
           "LW", waits=[("SW", wq_st)])
    wd_ld = p.c("LW")

    G["f2x"] = {0: 0, 1: 0}
    G["f2h_se"] = {}
    G["f2d_ve"] = {0: 0, 1: 0}
    G["f2q_ve"] = {0: 0, 1: 0}
    G["f2hq"] = {}
    G["f2hq_te"] = {0: 0, 1: 0}
    G["f2s_ve"] = 0
    for t in range(NT):
        xt = xl20 if t % 2 == 0 else xl21
        hqt = hq0 if t % 2 == 0 else hq1
        lx = "LX%d" % (t % 2)
        for kc in range(KC):
            ld(xt[:, kc * TT:(kc + 1) * TT],
               xln_s[kc * 128:(kc + 1) * 128, t * TT:(t + 1) * TT], lx,
               waits=[("VE", G["f2x"][t % 2])])
        x_ld = p.c(lx)
        for ic in range(ICN):
            hbuf = hl[ic % 4]
            lh = "LH%d" % (ic % 4)
            ld(hbuf, h_s[ic * 128:(ic + 1) * 128, t * TT:(t + 1) * TT], lh,
               waits=[("SE", G["f2h_se"].get(ic % 4, 0)),
                      ("SH0", f1_st[0]), ("SH1", f1_st[1]),
                      ("SH2", f1_st[2]), ("SH3", f1_st[3])])
            ld_h = p.c(lh)
            par = ic % 2
            av_, sg_, t2_ = avf2[par], sgnf2[par], t2f2[par]
            p.op("scalar", lambda e, d=av_, s=hbuf, sc=reff_all[:, ic:ic + 1]:
                 e.activation(d, s, AF.Abs, bias=czero, scale=sc),
                 sem="SE", waits=[(lh, ld_h), ("VE", rh_ve),
                                  ("VE", G["f2q_ve"][par])])
            p.op("scalar", lambda e, d=sg_, s=hbuf:
                 e.activation(d, s, AF.Sign, bias=czero, scale=one), sem="SE")
            se_q = p.c("SE")
            G["f2h_se"][ic % 4] = se_q
            p.op("vector", lambda e, d=t2_, s=av_:
                 e.tensor_scalar(d, s, 128.0, 256.49, ALU.add, ALU.min),
                 sem="VE", waits=[("SE", se_q)])
            p.op("vector", lambda e, d=hqt[:, ic * TT:(ic + 1) * TT], a=t2_,
                 b=sg_:
                 e.scalar_tensor_tensor(d, a, -128.0, b, ALU.add, ALU.mult),
                 sem="VE", waits=[("TE", G["f2hq_te"][t % 2])])
            p.op("vector", lambda e, d=hqt[:, ic * TT:(ic + 1) * TT]:
                 e.tensor_scalar(d, d, 127.0, None, ALU.min), sem="VE")
            G["f2q_ve"][par] = p.c("VE")
            G["f2hq"][ic] = p.c("VE")
        hq_ve = p.c("VE")

        for oc in range(KC):
            pd = PS[oc % 2]
            for kc in range(ICN):
                lhs = wdq2[:, kc * D + oc * 128: kc * D + (oc + 1) * 128]
                rhs = hqt[:, kc * TT:(kc + 1) * TT]
                w = [("VE", G["f2hq"][kc])]
                if kc == 0:
                    w += [("LW", wd_ld), (lx, x_ld),
                          ("VE", G["f2d_ve"][oc % 2])]
                p.op("tensor",
                     lambda e, o=pd, l=lhs, r=rhs, s=(kc == 0), z=(kc == ICN - 1):
                     e.matmul(o, l, r, start=s, stop=z),
                     sem="TE" if kc == ICN - 1 else None, waits=w)
            te_d = p.c("TE")
            p.op("vector", lambda e, d=yout[:, oc * TT:(oc + 1) * TT],
                 a=pd, sc=cd_all[:, oc:oc + 1], b=xt[:, oc * TT:(oc + 1) * TT]:
                 e.scalar_tensor_tensor(d, a, sc, b, ALU.mult, ALU.add),
                 sem="VE", waits=[("TE", te_d)])
            G["f2d_ve"][oc % 2] = p.c("VE")
        G["y_ve"] = p.c("VE")

        def ln2_out(oc, ve_o, _t=t):
            st(y_t[oc * 128:(oc + 1) * 128, _t * TT:(_t + 1) * TT],
               ysq2[:, oc * TT:(oc + 1) * TT], "SO", waits=[("VE", ve_o)])
        ln_emit(yout, ysq2, ysq2, ln2_out, PS[6], PS[7])
        G["f2x"][t % 2] = p.c("VE")
        G["f2d_ve"] = {0: p.c("VE"), 1: p.c("VE")}
        G["f2hq_te"][t % 2] = p.c("TE")

    _finish(nc, p)
    return nc


def _finish(nc, p):
    p.wait("gpsimd", [(s, p.c(s)) for s in
                      ("SO", "SX", "SH0", "SH1", "SH2", "SH3", "SW", "SC", "ZG")])
    with nc.Block() as block:
        for eng in ENGS:
            def run(e, _ops=p.ops[eng]):
                for f in _ops:
                    f(e)
            getattr(block, eng)(run)


def host_consts():
    c = np.zeros((128, 13), dtype=np.float32)
    c[:, C_ONE] = 1.0
    c[:, C_HALFPI] = PI / 2
    c[:, C_MHALF] = -0.5
    c[:, C_EPS] = 1e-5
    c[:, C_LNTHR] = LN_THR
    c[:, C_PHALF] = 0.5
    c[:, C_MONE] = -1.0
    c[:, C_MLN16] = MLN16
    c[:, C_PI] = PI
    c[:, C_MPI] = -PI
    c[:, C_MHPI] = -PI / 2
    return c


def kernel(L0_attn_x, L0_attn_wq, L0_attn_wk, L0_attn_wv, L0_attn_wo, L0_attn_rf,
           L0_ffn_clip_x, L0_ffn_clip_down, L0_ffn_w_gate, L0_ffn_w_up,
           L0_ffn_w_down):
    x = np.ascontiguousarray(np.asarray(L0_attn_x, dtype=np.float32))
    wq = np.ascontiguousarray(np.asarray(L0_attn_wq, dtype=np.float32).T)
    wk = np.ascontiguousarray(np.asarray(L0_attn_wk, dtype=np.float32).T)
    wv = np.ascontiguousarray(np.asarray(L0_attn_wv, dtype=np.float32).T)
    wo = np.ascontiguousarray(np.asarray(L0_attn_wo, dtype=np.float32).T)
    rf = np.asarray(L0_attn_rf, dtype=np.float32)
    rf_rep = np.ascontiguousarray(np.concatenate([rf.T, rf.T], axis=0))
    rf_bd = np.zeros((128, 2 * MF), dtype=np.float32)
    rf_bd[0:64, 0:MF] = rf.T
    rf_bd[64:128, MF:2 * MF] = rf.T
    wg = np.ascontiguousarray(np.asarray(L0_ffn_w_gate, dtype=np.float32))
    wu = np.ascontiguousarray(np.asarray(L0_ffn_w_up, dtype=np.float32))
    wd = np.ascontiguousarray(np.asarray(L0_ffn_w_down, dtype=np.float32))
    clip2 = np.array([[float(np.asarray(L0_ffn_clip_x).reshape(-1)[0]),
                       float(np.asarray(L0_ffn_clip_down).reshape(-1)[0])]],
                     dtype=np.float32)
    maskh = np.zeros((128, KC * H), dtype=np.float32)
    for kc in range(KC):
        for pl in range(128):
            maskh[pl, kc * H + 2 * kc + pl // 64] = 1.0
    sel_all = np.zeros((128, KC * 128), dtype=np.float32)
    for oc in range(KC):
        for pl in range(128):
            sel_all[2 * oc + pl // 64, oc * 128 + pl] = 1.0

    shared = {
        "wq_t": wq, "wk_t": wk, "wv_t": wv, "wo_t": wo, "rf_rep": rf_rep,
        "rf_bd": rf_bd,
        "wg_n": wg, "wu_n": wu, "wd_n": wd,
        "wg_tt": np.ascontiguousarray(wg.T),
        "wu_tt": np.ascontiguousarray(wu.T),
        "wd_tt": np.ascontiguousarray(wd.T),
        "clip2": clip2, "consts": host_consts(),
        "onesrow": np.ones((1, 128), dtype=np.float32),
        "ident": np.eye(128, dtype=np.float32),
        "maskh": maskh, "sel_all": sel_all,
    }
    in_maps = []
    for b in range(8):
        m = dict(shared)
        m["x_t"] = np.ascontiguousarray(x[b].T)
        in_maps.append(m)

    nc = build_core_program()
    res = run_bass_kernel_spmd(nc, in_maps, core_ids=list(range(8)))
    return np.stack([np.ascontiguousarray(r["y_t"].T) for r in res.results], axis=0)
